# version 02
# baseline (speedup 1.0000x reference)
import numpy as np
from contextlib import ExitStack

import concourse.bass as bass
import concourse.mybir as mybir
from concourse.bass_utils import run_bass_kernel_spmd
from concourse.alu_op_type import AluOpType as OP
from concourse import library_config

F32 = mybir.dt.float32
F32R = mybir.dt.float32r
I32 = mybir.dt.int32
AF = mybir.ActivationFunctionType
AX = mybir.AxisListType

NCORES = 8
D = 4096
T = 1024
TH = 512
S = 4096
H = 32
DC = 32
EB = 32
NQ = 4
NCL = 64
INV_SQRT_DK = float(1.0 / np.sqrt(np.float32(128.0)))
ENGS = ("sync", "scalar", "vector", "tensor", "gpsimd")
ENG_SEM = {"sync": "dh", "scalar": "sc", "vector": "dv", "tensor": "pe",
           "gpsimd": "gp"}


class Buf:

    __slots__ = ("w", "rs")

    def __init__(self):
        self.w = []
        self.rs = []


def _prune(toks):
    best = {}
    for (s, v) in toks:
        if best.get(s, -1) < v:
            best[s] = v
    return [(s, v) for s, v in best.items()]


class B:
    def __init__(self, nc, es):
        self.nc = nc
        self.es = es
        self.ops = {e: [] for e in ENGS}
        self.sems = {}
        self.fence_waits = []

    def _sem(self, name):
        if name not in self.sems:
            h = self.es.enter_context(self.nc.semaphore(name))
            self.sems[name] = [h, 0]
        return self.sems[name]

    def emit(self, eng, fn, reads=(), writes=(), extra=(), sem=None,
             inc_n=1, accum=False):
        sem = sem or ENG_SEM[eng]
        rec = self._sem(sem)
        waits = list(extra) + list(self.fence_waits)
        for bf in reads:
            waits += bf.w
        for bf in writes:
            if not accum:
                waits += bf.w + bf.rs
        rec[1] += inc_n
        tok = (sem, rec[1])
        for bf in reads:
            bf.rs = _prune(bf.rs + [tok])
        for bf in writes:
            if accum:
                bf.w = _prune(bf.w + [tok])
            else:
                bf.w = [tok]
                bf.rs = []
        self.ops[eng].append((fn, _prune(waits), (sem, inc_n)))
        return tok

    def fence(self):
        self.fence_waits = _prune(
            [(n, rec[1]) for n, rec in self.sems.items() if rec[1] > 0])

    def finalize(self):
        with self.nc.Block() as block:
            for eng in ENGS:
                oplist = self.ops[eng]
                if not oplist:
                    continue

                def mk(oplist):
                    def body(e):
                        seen = {}
                        for fn, waits, inc in oplist:
                            for (sn, v) in waits:
                                if seen.get(sn, -1) >= v:
                                    continue
                                e.wait_ge(self.sems[sn][0], v)
                                seen[sn] = v
                            instr = fn(e)
                            if instr is not None:
                                instr.then_inc(self.sems[inc[0]][0], inc[1])
                    return body

                getattr(block, eng)(mk(oplist))


def build_program(until_phase=99, dbg=None, dbg_out=()):
    nc = bass.Bass("TRN2", target_bir_lowering=False)

    dbg_mirrors = []

    def scratch(name, shape, dt):
        ap = nc.dram_tensor(name, shape, dt).ap()
        if name in dbg_out:
            m = nc.dram_tensor(name + "_dbg", shape, dt, kind="ExternalOutput").ap()
            dbg_mirrors.append((ap, m))
        return ap

    x_s = nc.dram_tensor("x_s", [T, D], F32, kind="ExternalInput").ap()
    w_hbm = {}
    for nm in ("wq_b", "wk_b", "wv_b", "wo_b", "wf_b"):
        w_hbm[nm] = nc.dram_tensor(nm, [EB, NQ, 128, 1024], F32,
                                   kind="ExternalInput").ap()
    cent_in = nc.dram_tensor("cent_b", [128, DC * NCL], F32, kind="ExternalInput").ap()
    bias_in = nc.dram_tensor("bias_in", [1, D], F32, kind="ExternalInput").ap()
    asg_in = nc.dram_tensor("asg_in", [1, D], I32, kind="ExternalInput").ap()
    y_s = nc.dram_tensor("y_s", [T, D], F32, kind="ExternalOutput").ap()

    kh_sh = scratch("kh_sh", [D, T], F32R)
    kl_sh = scratch("kl_sh", [D, T], F32R)
    v_sh = scratch("v_sh", [T, D], F32)
    kh_g = scratch("kh_g", [4 * D, T], F32R)
    kl_g = scratch("kl_g", [4 * D, T], F32R)
    v_g = scratch("v_g", [S, D], F32)
    qh_d = scratch("qh_d", [D, T], F32R)
    ql_d = scratch("ql_d", [D, T], F32R)
    oh_d = scratch("oh_d", [D, T], F32R)
    ol_d = scratch("ol_d", [D, T], F32R)
    x1_d = scratch("x1_d", [T, D], F32)
    lnh_d = scratch("lnh_d", [D, T], F32R)
    lnl_d = scratch("lnl_d", [D, T], F32R)
    kn_in = scratch("kn_in", [1, H], F32)
    kn_out = scratch("kn_out", [1, H], F32)
    cnt_in = scratch("cnt_in", [1, NCL], F32)
    cnt_out = scratch("cnt_out", [1, NCL], F32)

    dB = {nm: Buf() for nm in ("kh_sh", "kl_sh", "v_sh", "kh_g", "kl_g", "v_g",
                               "qh_d", "ql_d", "oh_d", "ol_d", "x1_d",
                               "lnh_d", "lnl_d", "kn_io", "cnt_io", "y")}

    es = ExitStack()
    b = B(nc, es)

    def sb(name, shape, dt=F32, st=None):
        stk = st or es
        t = stk.enter_context(nc.sbuf_tensor(name, shape, dt))
        return t, Buf()

    def pst(name, shape, st=None, dt=F32):
        stk = st or es
        t = stk.enter_context(nc.psum_tensor(name, shape, dt))
        return t, Buf()

    with es:
        ident_r, B_idr = sb("ident_r", [128, 128], F32R)
        ident_f, B_idf = sb("ident_f", [128, 128], F32)
        ones_r, B_onr = sb("ones_r", [128, 1], F32R)
        ones_f, B_onf = sb("ones_f", [128, 1], F32)
        onesc, B_onc = sb("onesc", [1, 128], F32)
        eps_t, B_eps = sb("eps_t", [128, 1], F32)
        qn_sb, B_qn = sb("qn_sb", [1, H], F32)
        kn_sb, B_kn = sb("kn_sb", [1, H], F32)
        qsel_sb, B_qs = sb("qsel_sb", [128, 8], F32)
        cnt_sb, B_cnt = sb("cnt_sb", [1, NCL], F32)

        p0 = es.enter_context(ExitStack())
        i128, B_i1 = sb("i128", [128, 1], I32, p0)
        irow, B_i2 = sb("irow", [1, 128], I32, p0)
        i128f, B_i1f = sb("i128f", [128, 1], F32, p0)
        irowf, B_i2f = sb("irowf", [1, 128], F32, p0)
        irowb, B_i2b = sb("irowb", [128, 128], F32, p0)

        b.emit("gpsimd", lambda e: e.iota(i128[:], pattern=[[0, 1]], base=0,
                                          channel_multiplier=1), writes=[B_i1])
        b.emit("gpsimd", lambda e: e.iota(irow[:], pattern=[[1, 128]], base=0,
                                          channel_multiplier=0), writes=[B_i2])
        b.emit("vector", lambda e: e.tensor_copy(i128f[:], i128[:]),
               reads=[B_i1], writes=[B_i1f])
        b.emit("vector", lambda e: e.tensor_copy(irowf[:], irow[:]),
               reads=[B_i2], writes=[B_i2f])
        ps_bc0, B_pbc0 = pst("ps_bc0", [128, 128], p0)
        b.emit("vector", lambda e: e.memset(onesc[:], 1.0), writes=[B_onc])
        b.emit("tensor", lambda e: e.matmul(ps_bc0[:], onesc[:], irowf[:],
                                            start=True, stop=True),
               reads=[B_i2f, B_onc], writes=[B_pbc0])
        b.emit("vector", lambda e: e.tensor_copy(irowb[:], ps_bc0[:]),
               reads=[B_pbc0], writes=[B_i2b])
        b.emit("vector", lambda e: e.tensor_scalar(ident_f[:], irowb[:], i128f[:], None,
                                                   op0=OP.is_equal),
               reads=[B_i2b, B_i1f], writes=[B_idf])
        b.emit("vector", lambda e: e.tensor_copy(ident_r[:], ident_f[:]),
               reads=[B_idf], writes=[B_idr])
        b.emit("vector", lambda e: e.memset(ones_f[:], 1.0), writes=[B_onf])
        b.emit("scalar", lambda e: e.copy(ones_r[:], ones_f[:]), reads=[B_onf], writes=[B_onr])
        b.emit("vector", lambda e: e.memset(eps_t[:], 1e-5), writes=[B_eps])
        b.emit("vector", lambda e: e.memset(qn_sb[:], 0.0), writes=[B_qn])
        b.emit("vector", lambda e: e.memset(kn_sb[:], 0.0), writes=[B_kn])
        b.emit("vector", lambda e: e.memset(cnt_sb[:], 0.0), writes=[B_cnt])

        p0.close()
        b.fence()

        p1 = es.enter_context(ExitStack())
        lnxh, B_lh = sb("lnxh", [128, DC * TH], F32R, p1)
        lnxl, B_ll = sb("lnxl", [128, DC * TH], F32R, p1)
        xt_t, B_xt = sb("xt_t", [128, D], F32, p1)
        lnh_tm, B_htm = sb("lnh_tm", [128, D], F32R, p1)
        lnl_tm, B_ltm = sb("lnl_tm", [128, D], F32R, p1)
        stats, B_st = sb("stats", [128, 48], F32, p1)
        mv_t, B_mv = sb("mv_t", [128, 2], F32, p1)
        std_t, B_sd = sb("std_t", [128, 1], F32, p1)
        rstd_t, B_rs = sb("rstd_t", [128, 1], F32, p1)
        nmr_t, B_nm = sb("nmr_t", [128, 1], F32, p1)
        whq, wlq, B_wh, B_wl = [], [], [], []
        wfq0, B_wf0 = sb("wfq0", [128, 1024], F32, p1)
        wfq, B_wf = [wfq0, wfq0], [B_wf0, B_wf0]
        for i in range(2):
            t2, b2 = sb(f"whq{i}", [128, 1024], F32R, p1)
            t3, b3 = sb(f"wlq{i}", [128, 1024], F32R, p1)
            whq.append(t2); wlq.append(t3)
            B_wh.append(b2); B_wl.append(b3)
        evh, B_evh = sb("evh", [128, TH], F32R, p1)
        evl, B_evl = sb("evl", [128, TH], F32R, p1)
        evn, B_evn = sb("evn", [128, TH], F32, p1)
        evf, B_evf = evh, B_evh
        sqn, B_sqn = evn, B_evn
        nrm2, B_nr2 = sb("nrm2", [1, 1], F32, p1)
        ps_pr, B_pp = zip(*[pst(f"ps_pr{i}", [128, TH], p1) for i in range(2)])
        ps_tr, B_pt = zip(*[pst(f"ps_tr{i}", [128, TH], p1) for i in range(2)])
        ps_nm, B_pn = pst("ps_nm", [1, TH], p1)

        def ln_into(src_rows, half, dsth, dstl, B_dh, B_dl, chunk_cols, extra_w=()):
            for tt in range(4):
                r0 = half * TH + tt * 128
                b.emit("sync", lambda e, r0=r0: e.dma_start(out=xt_t[:], in_=src_rows[r0:r0 + 128, :]),
                       writes=[B_xt], extra=extra_w, sem="dx", inc_n=16)

                def bnall(e):
                    ii = None
                    for i in range(8):
                        ii = e.bn_stats(stats[:, i * 6:(i + 1) * 6], xt_t[:, i * 512:(i + 1) * 512])
                    return ii
                b.emit("vector", bnall, reads=[B_xt], writes=[B_st])
                b.emit("vector", lambda e: e.bn_aggr(mv_t[:], stats[:]), reads=[B_st], writes=[B_mv])
                b.emit("scalar", lambda e: e.activation(std_t[:], mv_t[:, 1:2], AF.Sqrt,
                                                        bias=eps_t[:], scale=1.0),
                       reads=[B_mv, B_eps], writes=[B_sd])
                b.emit("vector", lambda e: e.reciprocal(rstd_t[:], std_t[:]),
                       reads=[B_sd], writes=[B_rs])
                b.emit("vector", lambda e: e.tensor_scalar(nmr_t[:], mv_t[:, 0:1], rstd_t[:], -1.0,
                                                           op0=OP.mult, op1=OP.mult),
                       reads=[B_mv, B_rs], writes=[B_nm])
                b.emit("scalar", lambda e: e.activation(lnh_tm[:], xt_t[:], AF.Identity,
                                                        bias=nmr_t[:], scale=rstd_t[:]),
                       reads=[B_xt, B_nm, B_rs], writes=[B_htm])
                b.emit("vector", lambda e: e.scalar_tensor_tensor(
                    lnl_tm[:], xt_t[:], rstd_t[:], lnh_tm[:].bitcast(F32),
                    op0=OP.mult, op1=OP.subtract),
                    reads=[B_xt, B_rs, B_htm], writes=[B_ltm])
                b.emit("vector", lambda e: e.tensor_scalar(lnl_tm[:], lnl_tm[:], nmr_t[:], None,
                                                           op0=OP.add),
                       reads=[B_nm], writes=[B_ltm])
                for (src, B_src, dst, B_dst) in ((lnh_tm, B_htm, dsth, B_dh),
                                                 (lnl_tm, B_ltm, dstl, B_dl)):
                    for g in range(8):
                        pb, B_pb = ps_tr[g % 2], B_pt[g % 2]

                        def tr4(e, src=src, g=g, pb=pb):
                            ii = None
                            for k in range(4):
                                c = g * 4 + k
                                ii = e.transpose(pb[:, k * 128:(k + 1) * 128].bitcast(F32R),
                                                 src[:, c * 128:(c + 1) * 128], ident_r[:])
                            return ii
                        b.emit("tensor", tr4, reads=[B_src, B_idr], writes=[B_pb])

                        def ev4(e, dst=dst, g=g, tt=tt, pb=pb, cc=chunk_cols):
                            o = dst[:, 0:0]
                            ap_out = dst[:].rearrange("p (c t) -> p c t", t=cc)[
                                :, g * 4:(g + 1) * 4, tt * 128:tt * 128 + 128]
                            ap_in = pb[:].bitcast(F32R).rearrange("p (k t) -> p k t", t=128)
                            return e.copy(ap_out, ap_in)
                        b.emit("scalar", ev4, reads=[B_pb], writes=[B_dst], accum=(tt + g > 0))

        def stream_quarter(wap, eb, q, i, dma_sem="dw"):
            b.emit("sync", lambda e, eb=eb, q=q, i=i: e.dma_start(out=wfq[i][:], in_=wap[eb, q]),
                   reads=[], writes=[B_wf[i]], sem=dma_sem, inc_n=16)
            b.emit("scalar", lambda e, i=i: e.copy(whq[i][:], wfq[i][:]),
                   reads=[B_wf[i]], writes=[B_wh[i]])
            b.emit("vector", lambda e, i=i: e.tensor_tensor(wlq[i][:], wfq[i][:],
                                                            whq[i][:].bitcast(F32), OP.subtract),
                   reads=[B_wf[i], B_wh[i]], writes=[B_wl[i]])

        def proj_block(wap, eb, xh, xl, B_xh, B_xl, psum, B_psum):
            for q in range(NQ):
                i = q % 2
                stream_quarter(wap, eb, q, i)

                def mms(e, q=q, i=i):
                    ii = None
                    for cl in range(8):
                        c = q * 8 + cl
                        first = (q == 0 and cl == 0)
                        last = (q == NQ - 1 and cl == 7)
                        wh_ap = whq[i][:, cl * 128:(cl + 1) * 128]
                        wl_ap = wlq[i][:, cl * 128:(cl + 1) * 128]
                        xh_ap = xh[:, c * TH:(c + 1) * TH]
                        xl_ap = xl[:, c * TH:(c + 1) * TH]
                        e.matmul(psum[:], wh_ap, xh_ap, start=first, stop=False)
                        e.matmul(psum[:], wh_ap, xl_ap, start=False, stop=False)
                        ii = e.matmul(psum[:], wl_ap, xh_ap, start=False, stop=last)
                    return ii
                b.emit("tensor", mms, reads=[B_wh[i], B_wl[i], B_xh, B_xl],
                       writes=[B_psum], accum=(q > 0))

        def evac_split_store(psum, B_psum, dsth_ap, dstl_ap, B_dsth, B_dstl,
                             norm_buf, norm_col, first_hl):
            b.emit("scalar", lambda e: e.copy(evh[:], psum[:]), reads=[B_psum], writes=[B_evh])
            b.emit("vector", lambda e: e.tensor_tensor(evl[:], psum[:],
                                                       evh[:].bitcast(F32), OP.subtract),
                   reads=[B_psum, B_evh], writes=[B_evl])
            b.emit("gpsimd", lambda e: e.dma_start(out=dsth_ap, in_=evh[:]),
                   reads=[B_evh], writes=[B_dsth], sem="gd", inc_n=16, accum=not first_hl)
            b.emit("gpsimd", lambda e: e.dma_start(out=dstl_ap, in_=evl[:]),
                   reads=[B_evl], writes=[B_dstl], sem="gd", inc_n=16, accum=not first_hl)
            b.emit("scalar", lambda e: e.activation(sqn[:], evh[:].bitcast(F32), AF.Square,
                                                    scale=1.0),
                   reads=[B_evh], writes=[B_sqn])
            b.emit("tensor", lambda e: e.matmul(ps_nm[:], ones_r[:], sqn[:], start=True, stop=True),
                   reads=[B_sqn, B_onr], writes=[B_pn])
            b.emit("vector", lambda e: e.tensor_reduce(nrm2[:], ps_nm[:], axis=AX.X, op=OP.max),
                   reads=[B_pn], writes=[B_nr2])
            nb, col = norm_buf
            b.emit("vector", lambda e, col=norm_col: e.tensor_tensor(
                nb[:, col:col + 1], nb[:, col:col + 1], nrm2[:], OP.max),
                reads=[B_nr2], writes=[norm_buf[1]] if False else [])

        def norm_rmw(nb_t, B_nb, col):
            b.emit("vector", lambda e, col=col: e.tensor_tensor(
                nb_t[:, col:col + 1], nb_t[:, col:col + 1], nrm2[:], OP.max),
                reads=[B_nr2, B_nb], writes=[B_nb])

        if until_phase >= 1:
            for half in range(2):
                ln_into(x_s, half, lnxh, lnxl, B_lh, B_ll, TH)
                for wname in ("k", "v", "q"):
                    wap = w_hbm[f"w{wname}_b"]
                    for eb in range(EB):
                        psum, B_psum = ps_pr[eb % 2], B_pp[eb % 2]
                        proj_block(wap, eb, lnxh, lnxl, B_lh, B_ll, psum, B_psum)
                        if wname in ("k", "q"):
                            dh = kh_sh if wname == "k" else qh_d
                            dl = kl_sh if wname == "k" else ql_d
                            B_dh = dB["kh_sh"] if wname == "k" else dB["qh_d"]
                            B_dl = dB["kl_sh"] if wname == "k" else dB["ql_d"]
                            first_hl = (half == 0 and eb == 0)
                            b.emit("scalar", lambda e, p=psum: e.copy(evh[:], p[:]),
                                   reads=[B_psum], writes=[B_evh])
                            b.emit("vector", lambda e, p=psum: e.tensor_tensor(
                                evl[:], p[:], evh[:].bitcast(F32), OP.subtract),
                                reads=[B_psum, B_evh], writes=[B_evl])
                            b.emit("gpsimd", lambda e, d=dh, eb=eb, h=half: e.dma_start(
                                out=d[eb * 128:(eb + 1) * 128, h * TH:(h + 1) * TH], in_=evh[:]),
                                reads=[B_evh], writes=[B_dh], sem="gd", inc_n=16,
                                accum=not first_hl)
                            b.emit("gpsimd", lambda e, d=dl, eb=eb, h=half: e.dma_start(
                                out=d[eb * 128:(eb + 1) * 128, h * TH:(h + 1) * TH], in_=evl[:]),
                                reads=[B_evl], writes=[B_dl], sem="gd", inc_n=16,
                                accum=not first_hl)
                            b.emit("scalar", lambda e: e.activation(sqn[:],
                                                                    evh[:].bitcast(F32),
                                                                    AF.Square, scale=1.0),
                                   reads=[B_evh], writes=[B_sqn])
                            b.emit("tensor", lambda e: e.matmul(ps_nm[:], ones_f[:],
                                                                sqn[:],
                                                                start=True, stop=True),
                                   reads=[B_sqn, B_onf], writes=[B_pn])
                            b.emit("vector", lambda e: e.tensor_reduce(nrm2[:], ps_nm[:],
                                                                       axis=AX.X, op=OP.max),
                                   reads=[B_pn], writes=[B_nr2])
                            if wname == "k":
                                norm_rmw(kn_sb, B_kn, eb)
                            else:
                                norm_rmw(qn_sb, B_qn, eb)
                        else:
                            b.emit("scalar", lambda e, p=psum: e.copy(evf[:].bitcast(F32), p[:]),
                                   reads=[B_psum], writes=[B_evf])
                            pb, B_pb = ps_tr[eb % 2], B_pt[eb % 2]

                            def tr4v(e, pb=pb):
                                ii = None
                                for j in range(4):
                                    ii = e.transpose(pb[:, j * 128:(j + 1) * 128].bitcast(F32),
                                                     evf[:, j * 128:(j + 1) * 128].bitcast(F32),
                                                     ident_f[:])
                                return ii
                            b.emit("tensor", tr4v, reads=[B_evf, B_idf], writes=[B_pb])
                            b.emit("vector", lambda e, pb=pb: e.tensor_copy(evn[:], pb[:]),
                                   reads=[B_pb], writes=[B_evn])
                            b.emit("gpsimd", lambda e, eb=eb, h=half: e.dma_start(
                                out=v_sh[h * TH:(h + 1) * TH, eb * 128:(eb + 1) * 128]
                                .rearrange("(j p) c -> p j c", p=128),
                                in_=evn[:].rearrange("p (j c) -> p j c", c=128)),
                                reads=[B_evn], writes=[dB["v_sh"]], sem="gd", inc_n=16,
                                accum=not (half == 0 and eb == 0))
                if half == 1:
                    b.emit("gpsimd", lambda e: e.dma_start(out=kn_in, in_=kn_sb[:]),
                           reads=[B_kn], writes=[dB["kn_io"]], sem="gd", inc_n=16)
                    b.emit("gpsimd", lambda e: e.collective_compute(
                        "AllReduce", OP.max, replica_groups=[[0, 1, 2, 3, 4, 5, 6, 7]],
                        ins=[kn_in.opt()], outs=[kn_out.opt()]),
                        reads=[dB["kn_io"]], writes=[dB["kn_io"]], sem="cc")
                    b.emit("gpsimd", lambda e: e.collective_compute(
                        "AllGather", OP.bypass, replica_groups=[[0, 1, 2, 3], [4, 5, 6, 7]],
                        ins=[kh_sh.opt()], outs=[kh_g.opt()]),
                        reads=[dB["kh_sh"]], writes=[dB["kh_g"]], sem="cc")
                    b.emit("gpsimd", lambda e: e.collective_compute(
                        "AllGather", OP.bypass, replica_groups=[[0, 1, 2, 3], [4, 5, 6, 7]],
                        ins=[kl_sh.opt()], outs=[kl_g.opt()]),
                        reads=[dB["kl_sh"]], writes=[dB["kl_g"]], sem="cc")
                    b.emit("gpsimd", lambda e: e.collective_compute(
                        "AllGather", OP.bypass, replica_groups=[[0, 1, 2, 3], [4, 5, 6, 7]],
                        ins=[v_sh.opt()], outs=[v_g.opt()]),
                        reads=[dB["v_sh"]], writes=[dB["v_g"]], sem="cc")

        if dbg == 1 or until_phase < 2:
            p1.close()
            b.fence()
            for (sap, map_) in dbg_mirrors:
                b.emit("gpsimd", lambda e, s=sap, m=map_: e.dma_start(out=m, in_=s),
                       sem="gd", inc_n=16)
            b.fence()
            b.emit("gpsimd", lambda e: e.engine_nop())
            b.finalize()
            return nc

        p1.close()
        b.fence()
        p2 = es.enter_context(ExitStack())
        khb, klb, vbb, qhb, qlb = [], [], [], [], []
        B_khb, B_klb, B_vbb, B_qhb, B_qlb = [], [], [], [], []
        for i in range(2):
            t1, b1 = sb(f"khb{i}", [128, S], F32R, p2); khb.append(t1); B_khb.append(b1)
            t2, b2 = sb(f"klb{i}", [128, S], F32R, p2); klb.append(t2); B_klb.append(b2)
            t3, b3 = sb(f"vbb{i}", [128, 32 * 129], F32, p2); vbb.append(t3); B_vbb.append(b3)
            t4, b4 = sb(f"qhb{i}", [128, T], F32R, p2); qhb.append(t4); B_qhb.append(b4)
            t5, b5 = sb(f"qlb{i}", [128, T], F32R, p2); qlb.append(t5); B_qlb.append(b5)
        ptile, B_ptile = zip(*[sb(f"ptile{i}", [128, TH], F32, p2) for i in range(3)])
        otm, B_otm = sb("otm", [128, TH], F32, p2)
        ohh, B_ohh = sb("ohh", [128, TH], F32R, p2)
        ohl, B_ohl = sb("ohl", [128, TH], F32R, p2)
        rz_t, B_rz = sb("rz_t", [128, 4], F32, p2)
        kng, B_kng = sb("kng", [1, H], F32, p2)
        mh_t, B_mh = sb("mh_t", [1, 1], F32, p2)
        mhb, B_mhb = sb("mhb", [128, 1], F32, p2)
        ps_s, B_ps = zip(*[pst(f"ps_s{i}", [128, TH], p2) for i in range(2)])
        ps_av, B_pav = zip(*[pst(f"ps_av{i}", [128, 129], p2) for i in range(4)])
        ps_ot, B_pot = pst("ps_ot", [128, TH], p2)
        ps_bc2, B_pbc2 = pst("ps_bc2", [128, 1], p2)

        for i in range(2):
            b.emit("vector", lambda e, i=i: e.memset(
                vbb[i][:].rearrange("p (c j) -> p c j", j=129)[:, :, 128:129], 1.0),
                writes=[B_vbb[i]])
        b.emit("sync", lambda e: e.dma_start(out=kng[:], in_=kn_out),
               reads=[dB["kn_io"]], writes=[B_kng], sem="dkg", inc_n=16)

        def load_head(h, i):
            for r in range(4):
                b.emit("sync", lambda e, h=h, i=i, r=r: e.dma_start(
                    out=khb[i][:, r * T:(r + 1) * T],
                    in_=kh_g[r * D + h * 128: r * D + (h + 1) * 128, :]),
                    reads=[dB["kh_g"]], writes=[B_khb[i]], sem=f"ka{i}", inc_n=16,
                    accum=(r > 0))
                b.emit("sync", lambda e, h=h, i=i, r=r: e.dma_start(
                    out=klb[i][:, r * T:(r + 1) * T],
                    in_=kl_g[r * D + h * 128: r * D + (h + 1) * 128, :]),
                    reads=[dB["kl_g"]], writes=[B_klb[i]], sem=f"kb{i}", inc_n=16,
                    accum=(r > 0))
            b.emit("sync", lambda e, h=h, i=i: e.dma_start(
                out=qhb[i][:], in_=qh_d[h * 128:(h + 1) * 128, :]),
                reads=[dB["qh_d"]], writes=[B_qhb[i]], sem=f"qa{i}", inc_n=16)
            b.emit("sync", lambda e, h=h, i=i: e.dma_start(
                out=qlb[i][:], in_=ql_d[h * 128:(h + 1) * 128, :]),
                reads=[dB["ql_d"]], writes=[B_qlb[i]], sem=f"qb{i}", inc_n=16)

            def ldv(e, h=h, i=i):
                return e.dma_start(
                    out=vbb[i][:].rearrange("p (c j) -> p c j", j=129)[:, :, 0:128],
                    in_=v_g[:, h * 128:(h + 1) * 128].rearrange("(c p) j -> p c j", p=128))
            b.emit("sync", ldv, reads=[dB["v_g"]], writes=[B_vbb[i]], sem=f"va{i}",
                   inc_n=16)

        load_head(0, 0)
        for h in range(H):
            i = h % 2
            if h + 1 < H:
                load_head(h + 1, (h + 1) % 2)
            b.emit("vector", lambda e, h=h: e.tensor_tensor(
                mh_t[:], qn_sb[:, h:h + 1], kng[:, h:h + 1], OP.mult),
                reads=[B_qn, B_kng], writes=[B_mh])
            b.emit("scalar", lambda e: e.activation(mh_t[:], mh_t[:], AF.Sqrt,
                                                    scale=1.0002 / 128.0),
                   reads=[B_mh], writes=[B_mh])
            b.emit("vector", lambda e: e.tensor_scalar(mh_t[:], mh_t[:], -1.0, None, op0=OP.mult),
                   reads=[B_mh], writes=[B_mh])
            b.emit("tensor", lambda e: e.matmul(ps_bc2[:], onesc[:], mh_t[:],
                                                start=True, stop=True),
                   reads=[B_mh, B_onc], writes=[B_pbc2])
            b.emit("vector", lambda e: e.tensor_copy(mhb[:], ps_bc2[:]),
                   reads=[B_pbc2], writes=[B_mhb])
            for qi in range(2):
                for c in range(32):
                    sps, B_sps = ps_s[c % 2], B_ps[c % 2]
                    pt, B_pt2 = ptile[c % 3], B_ptile[c % 3]

                    def smm(e, i=i, c=c, qi=qi, sps=sps):
                        e.matmul(sps[:], khb[i][:, c * 128:(c + 1) * 128],
                                 qhb[i][:, qi * TH:(qi + 1) * TH], start=True, stop=False)
                        e.matmul(sps[:], khb[i][:, c * 128:(c + 1) * 128],
                                 qlb[i][:, qi * TH:(qi + 1) * TH], start=False, stop=False)
                        return e.matmul(sps[:], klb[i][:, c * 128:(c + 1) * 128],
                                        qhb[i][:, qi * TH:(qi + 1) * TH], start=False, stop=True)
                    b.emit("tensor", smm, reads=[B_khb[i], B_klb[i], B_qhb[i], B_qlb[i]],
                           writes=[B_sps])
                    b.emit("scalar", lambda e, sps=sps, pt=pt: e.activation(
                        pt[:], sps[:], AF.Exp, bias=mhb[:], scale=INV_SQRT_DK),
                        reads=[B_sps, B_mhb], writes=[B_pt2])

                    def avm(e, i=i, c=c, pt=pt):
                        ii = None
                        for j in range(4):
                            ii = e.matmul(ps_av[j][:], pt[:, j * 128:(j + 1) * 128],
                                          vbb[i][:, c * 129:(c + 1) * 129],
                                          start=(c == 0), stop=(c == 31))
                        return ii
                    b.emit("tensor", avm, reads=[B_pt2, B_vbb[i]],
                           writes=list(B_pav), accum=(c > 0))
                for j in range(4):
                    b.emit("vector", lambda e, j=j: e.reciprocal(rz_t[:, j:j + 1],
                                                                 ps_av[j][:, 128:129]),
                           reads=[B_pav[j]], writes=[B_rz], accum=(j > 0))
                    b.emit("vector", lambda e, j=j: e.tensor_scalar(
                        otm[:, j * 128:(j + 1) * 128], ps_av[j][:, 0:128],
                        rz_t[:, j:j + 1], None, op0=OP.mult),
                        reads=[B_pav[j], B_rz], writes=[B_otm], accum=(j > 0))

                def tr4o(e):
                    ii = None
                    for j in range(4):
                        ii = e.transpose(ps_ot[:, j * 128:(j + 1) * 128].bitcast(F32),
                                         otm[:, j * 128:(j + 1) * 128], ident_f[:])
                    return ii
                b.emit("tensor", tr4o, reads=[B_otm, B_idf], writes=[B_pot])
                b.emit("scalar", lambda e: e.copy(ohh[:], ps_ot[:]), reads=[B_pot], writes=[B_ohh])
                b.emit("vector", lambda e: e.tensor_tensor(ohl[:], ps_ot[:],
                                                           ohh[:].bitcast(F32), OP.subtract),
                       reads=[B_pot, B_ohh], writes=[B_ohl])
                b.emit("gpsimd", lambda e, h=h, qi=qi: e.dma_start(
                    out=oh_d[h * 128:(h + 1) * 128, qi * TH:(qi + 1) * TH], in_=ohh[:]),
                    reads=[B_ohh], writes=[dB["oh_d"]], sem="gd", inc_n=16,
                    accum=not (h == 0 and qi == 0))
                b.emit("gpsimd", lambda e, h=h, qi=qi: e.dma_start(
                    out=ol_d[h * 128:(h + 1) * 128, qi * TH:(qi + 1) * TH], in_=ohl[:]),
                    reads=[B_ohl], writes=[dB["ol_d"]], sem="gd", inc_n=16,
                    accum=not (h == 0 and qi == 0))

        if dbg == 2 or until_phase < 3:
            p2.close()
            b.fence()
            for (sap, map_) in dbg_mirrors:
                b.emit("gpsimd", lambda e, s=sap, m=map_: e.dma_start(out=m, in_=s),
                       sem="gd", inc_n=16)
            b.fence()
            b.emit("gpsimd", lambda e: e.engine_nop())
            b.finalize()
            return nc

        p2.close()
        b.fence()
        p3 = es.enter_context(ExitStack())
        o_hh, B_oh = sb("o_hh", [128, DC * TH], F32R, p3)
        o_ll, B_ol = sb("o_ll", [128, DC * TH], F32R, p3)
        wfq3, whq3, wlq3, B_wf3, B_wh3, B_wl3 = [], [], [], [], [], []
        for i in range(2):
            t1, b1 = sb(f"wfq3{i}", [128, 1024], F32, p3)
            t2, b2 = sb(f"whq3{i}", [128, 1024], F32R, p3)
            t3, b3 = sb(f"wlq3{i}", [128, 1024], F32R, p3)
            wfq3.append(t1); whq3.append(t2); wlq3.append(t3)
            B_wf3.append(b1); B_wh3.append(b2); B_wl3.append(b3)
        of_t, B_oft = sb("of_t", [128, TH], F32, p3)
        x_t4, B_xt4 = sb("x_t4", [128, TH], F32, p3)
        x1_t, B_x1t = sb("x1_t", [128, TH], F32, p3)
        ps_pr3, B_pp3 = zip(*[pst(f"ps_pr3{i}", [128, TH], p3) for i in range(2)])
        ps_tr3, B_pt3 = zip(*[pst(f"ps_tr3{i}", [128, TH], p3) for i in range(2)])

        def proj_block3(wap, eb, xh, xl, B_xh, B_xl, psum, B_psum,
                        wf, wh, wl, B_wfL, B_whL, B_wlL):
            for q in range(NQ):
                i = q % 2
                b.emit("sync", lambda e, eb=eb, q=q, i=i: e.dma_start(out=wf[i][:], in_=wap[eb, q]),
                       writes=[B_wfL[i]], sem=f"dw3{i}", inc_n=16)
                b.emit("scalar", lambda e, i=i: e.copy(wh[i][:], wf[i][:]),
                       reads=[B_wfL[i]], writes=[B_whL[i]])
                b.emit("vector", lambda e, i=i: e.tensor_tensor(wl[i][:], wf[i][:],
                                                                wh[i][:].bitcast(F32), OP.subtract),
                       reads=[B_wfL[i], B_whL[i]], writes=[B_wlL[i]])

                def mms(e, q=q, i=i):
                    ii = None
                    for cl in range(8):
                        c = q * 8 + cl
                        first = (q == 0 and cl == 0)
                        last = (q == NQ - 1 and cl == 7)
                        wh_ap = wh[i][:, cl * 128:(cl + 1) * 128]
                        wl_ap = wl[i][:, cl * 128:(cl + 1) * 128]
                        xh_ap = xh[:, c * TH:(c + 1) * TH]
                        xl_ap = xl[:, c * TH:(c + 1) * TH]
                        e.matmul(psum[:], wh_ap, xh_ap, start=first, stop=False)
                        e.matmul(psum[:], wh_ap, xl_ap, start=False, stop=False)
                        ii = e.matmul(psum[:], wl_ap, xh_ap, start=False, stop=last)
                    return ii
                b.emit("tensor", mms, reads=[B_whL[i], B_wlL[i], B_xh, B_xl],
                       writes=[B_psum], accum=(q > 0))

        for half in range(2):
            b.emit("sync", lambda e, h=half: e.dma_start(
                out=o_hh[:].rearrange("p (c t) -> p c t", t=TH),
                in_=oh_d[:, h * TH:(h + 1) * TH].rearrange("(c p) t -> p c t", p=128)),
                reads=[dB["oh_d"]], writes=[B_oh], sem="do", inc_n=16)
            b.emit("sync", lambda e, h=half: e.dma_start(
                out=o_ll[:].rearrange("p (c t) -> p c t", t=TH),
                in_=ol_d[:, h * TH:(h + 1) * TH].rearrange("(c p) t -> p c t", p=128)),
                reads=[dB["ol_d"]], writes=[B_ol], sem="do", inc_n=16)
            for eb in range(EB):
                psum, B_psum = ps_pr3[eb % 2], B_pp3[eb % 2]
                proj_block3(w_hbm["wo_b"], eb, o_hh, o_ll, B_oh, B_ol, psum, B_psum,
                            wfq3, whq3, wlq3, B_wf3, B_wh3, B_wl3)
                b.emit("scalar", lambda e, p=psum: e.copy(of_t[:], p[:]),
                       reads=[B_psum], writes=[B_oft])
                b.emit("sync", lambda e, eb=eb, h=half: e.dma_start(
                    out=x_t4[:].rearrange("p (s j) -> p s j", j=128),
                    in_=x_s[h * TH:(h + 1) * TH, eb * 128:(eb + 1) * 128]
                    .rearrange("(s p) j -> p s j", p=128)),
                    writes=[B_xt4], sem="dx4", inc_n=16)
                pb, B_pb = ps_tr3[eb % 2], B_pt3[eb % 2]

                def tr4p(e, pb=pb):
                    ii = None
                    for j in range(4):
                        ii = e.transpose(pb[:, j * 128:(j + 1) * 128].bitcast(F32),
                                         of_t[:, j * 128:(j + 1) * 128], ident_f[:])
                    return ii
                b.emit("tensor", tr4p, reads=[B_oft, B_idf], writes=[B_pb])
                b.emit("vector", lambda e, pb=pb: e.tensor_tensor(x1_t[:], pb[:], x_t4[:], OP.add),
                       reads=[B_pb, B_xt4], writes=[B_x1t])
                b.emit("gpsimd", lambda e, eb=eb, h=half: e.dma_start(
                    out=x1_d[h * TH:(h + 1) * TH, eb * 128:(eb + 1) * 128]
                    .rearrange("(s p) j -> p s j", p=128),
                    in_=x1_t[:].rearrange("p (s j) -> p s j", j=128)),
                    reads=[B_x1t], writes=[dB["x1_d"]], sem="gd", inc_n=16,
                    accum=not (half == 0 and eb == 0))

        p3.close()
        b.fence()
        p3 = es.enter_context(ExitStack())
        ps_tr3, B_pt3 = zip(*[pst(f"ps_trb{i}", [128, TH], p3) for i in range(2)])
        cent_f, B_ctf = sb("cent_f", [128, DC * NCL], F32, p3)
        cent_h, B_cth = sb("cent_h", [128, DC * NCL], F32R, p3)
        cent_l, B_ctl = sb("cent_l", [128, DC * NCL], F32R, p3)
        b.emit("sync", lambda e: e.dma_start(out=cent_f[:], in_=cent_in),
               writes=[B_ctf], sem="dct", inc_n=16)
        b.emit("scalar", lambda e: e.copy(cent_h[:], cent_f[:]), reads=[B_ctf], writes=[B_cth])
        b.emit("vector", lambda e: e.tensor_tensor(cent_l[:], cent_f[:],
                                                   cent_h[:].bitcast(F32), OP.subtract),
               reads=[B_ctf, B_cth], writes=[B_ctl])
        ln2h_t, B_l2h = sb("ln2h_t", [128, DC * 128], F32R, p3)
        ln2l_t, B_l2l = sb("ln2l_t", [128, DC * 128], F32R, p3)
        xt2, B_xt2 = sb("xt2", [128, D], F32, p3)
        l2h_tm, B_h2 = sb("l2h_tm", [128, D], F32R, p3)
        l2l_tm, B_l2 = sb("l2l_tm", [128, D], F32R, p3)
        st2, B_st2 = sb("st2", [128, 48], F32, p3)
        mv2, B_mv2 = sb("mv2", [128, 2], F32, p3)
        sd2, B_sd2 = sb("sd2", [128, 1], F32, p3)
        rs2, B_rs2 = sb("rs2", [128, 1], F32, p3)
        nm2, B_nm2 = sb("nm2", [128, 1], F32, p3)
        dex, B_dex = sb("dex", [128, NCL], F32, p3)
        dmx, B_dmx = sb("dmx", [128, 1], F32, p3)
        dnb, B_dnb = sb("dnb", [128, 1], F32, p3)
        dzs, B_dzs = sb("dzs", [128, 1], F32, p3)
        dth, B_dth = sb("dth", [128, 1], F32, p3)
        hits, B_hit = sb("hits", [128, NCL], F32R, p3)
        ps_dt, B_pdt = pst("ps_dt", [128, NCL], p3)
        ps_ct, B_pct = pst("ps_ct", [1, NCL], p3)

        for ti in range(8):
            b.emit("sync", lambda e, ti=ti: e.dma_start(out=xt2[:],
                                                        in_=x1_d[ti * 128:(ti + 1) * 128, :]),
                   reads=[dB["x1_d"]], writes=[B_xt2], sem="dx", inc_n=16)

            def bn2(e):
                ii = None
                for i in range(8):
                    ii = e.bn_stats(st2[:, i * 6:(i + 1) * 6], xt2[:, i * 512:(i + 1) * 512])
                return ii
            b.emit("vector", bn2, reads=[B_xt2], writes=[B_st2])
            b.emit("vector", lambda e: e.bn_aggr(mv2[:], st2[:]), reads=[B_st2], writes=[B_mv2])
            b.emit("scalar", lambda e: e.activation(sd2[:], mv2[:, 1:2], AF.Sqrt,
                                                    bias=eps_t[:], scale=1.0),
                   reads=[B_mv2, B_eps], writes=[B_sd2])
            b.emit("vector", lambda e: e.reciprocal(rs2[:], sd2[:]), reads=[B_sd2], writes=[B_rs2])
            b.emit("vector", lambda e: e.tensor_scalar(nm2[:], mv2[:, 0:1], rs2[:], -1.0,
                                                       op0=OP.mult, op1=OP.mult),
                   reads=[B_mv2, B_rs2], writes=[B_nm2])
            b.emit("scalar", lambda e: e.activation(l2h_tm[:], xt2[:], AF.Identity,
                                                    bias=nm2[:], scale=rs2[:]),
                   reads=[B_xt2, B_nm2, B_rs2], writes=[B_h2])
            b.emit("vector", lambda e: e.scalar_tensor_tensor(
                l2l_tm[:], xt2[:], rs2[:], l2h_tm[:].bitcast(F32), op0=OP.mult, op1=OP.subtract),
                reads=[B_xt2, B_rs2, B_h2], writes=[B_l2])
            b.emit("vector", lambda e: e.tensor_scalar(l2l_tm[:], l2l_tm[:], nm2[:], None,
                                                       op0=OP.add),
                   reads=[B_nm2], writes=[B_l2])
            for (src, B_src, dst, B_dst) in ((l2h_tm, B_h2, ln2h_t, B_l2h),
                                             (l2l_tm, B_l2, ln2l_t, B_l2l)):
                for g in range(8):
                    pb, B_pb2 = ps_tr3[g % 2], B_pt3[g % 2]

                    def tr4l(e, src=src, g=g, pb=pb):
                        ii = None
                        for k in range(4):
                            c = g * 4 + k
                            ii = e.transpose(pb[:, k * 128:(k + 1) * 128].bitcast(F32R),
                                             src[:, c * 128:(c + 1) * 128], ident_r[:])
                        return ii
                    b.emit("tensor", tr4l, reads=[B_src, B_idr], writes=[B_pb2])
                    b.emit("scalar", lambda e, dst=dst, g=g, pb=pb: e.copy(
                        dst[:, g * 512:(g + 1) * 512], pb[:].bitcast(F32R)),
                        reads=[B_pb2], writes=[B_dst], accum=(g > 0))
            b.emit("gpsimd", lambda e, ti=ti: e.dma_start(
                out=lnh_d[:, ti * 128:(ti + 1) * 128].rearrange("(c p) t -> p c t", p=128),
                in_=ln2h_t[:].rearrange("p (c t) -> p c t", t=128)),
                reads=[B_l2h], writes=[dB["lnh_d"]], sem="gd", inc_n=16, accum=(ti > 0))
            b.emit("gpsimd", lambda e, ti=ti: e.dma_start(
                out=lnl_d[:, ti * 128:(ti + 1) * 128].rearrange("(c p) t -> p c t", p=128),
                in_=ln2l_t[:].rearrange("p (c t) -> p c t", t=128)),
                reads=[B_l2l], writes=[dB["lnl_d"]], sem="gd", inc_n=16, accum=(ti > 0))

            def dotsmm(e):
                ii = None
                for c in range(DC):
                    first = (c == 0)
                    last = (c == DC - 1)
                    e.matmul(ps_dt[:], ln2h_t[:, c * 128:(c + 1) * 128],
                             cent_h[:, c * NCL:(c + 1) * NCL], start=first, stop=False)
                    e.matmul(ps_dt[:], ln2h_t[:, c * 128:(c + 1) * 128],
                             cent_l[:, c * NCL:(c + 1) * NCL], start=False, stop=False)
                    ii = e.matmul(ps_dt[:], ln2l_t[:, c * 128:(c + 1) * 128],
                                  cent_h[:, c * NCL:(c + 1) * NCL], start=False, stop=last)
                return ii
            b.emit("tensor", dotsmm, reads=[B_l2h, B_l2l, B_cth, B_ctl], writes=[B_pdt])
            b.emit("vector", lambda e: e.tensor_reduce(dmx[:], ps_dt[:], axis=AX.X, op=OP.max),
                   reads=[B_pdt], writes=[B_dmx])
            b.emit("vector", lambda e: e.tensor_scalar(dnb[:], dmx[:], -10.0, None, op0=OP.mult),
                   reads=[B_dmx], writes=[B_dnb])
            b.emit("scalar", lambda e: e.activation(dex[:], ps_dt[:], AF.Exp,
                                                    bias=dnb[:], scale=10.0, accum_out=dzs[:]),
                   reads=[B_pdt, B_dnb], writes=[B_dex, B_dzs])
            b.emit("vector", lambda e: e.tensor_scalar(dth[:], dzs[:], 0.01, None, op0=OP.mult),
                   reads=[B_dzs], writes=[B_dth])
            b.emit("vector", lambda e: e.tensor_scalar(hits[:], dex[:], dth[:], None, op0=OP.is_gt),
                   reads=[B_dex, B_dth], writes=[B_hit])
            b.emit("vector", lambda e, ti=ti: e.tensor_reduce(
                qsel_sb[:, ti:ti + 1], hits[:].bitcast(F32), axis=AX.X, op=OP.max),
                reads=[B_hit], writes=[B_qs], accum=(ti > 0))
            b.emit("tensor", lambda e: e.matmul(ps_ct[:], ones_r[:], hits[:],
                                                start=True, stop=True),
                   reads=[B_hit, B_onr], writes=[B_pct])
            b.emit("vector", lambda e: e.tensor_tensor(cnt_sb[:], cnt_sb[:], ps_ct[:], OP.add),
                   reads=[B_pct, B_cnt], writes=[B_cnt])

        b.emit("gpsimd", lambda e: e.dma_start(out=cnt_in, in_=cnt_sb[:]),
               reads=[B_cnt], writes=[dB["cnt_io"]], sem="gd", inc_n=16)
        b.emit("gpsimd", lambda e: e.collective_compute(
            "AllReduce", OP.add, replica_groups=[[0, 1, 2, 3, 4, 5, 6, 7]],
            ins=[cnt_in.opt()], outs=[cnt_out.opt()]),
            reads=[dB["cnt_io"]], writes=[dB["cnt_io"]], sem="cc")

        if dbg == 3 or until_phase < 4:
            p3.close()
            b.fence()
            for (sap, map_) in dbg_mirrors:
                b.emit("gpsimd", lambda e, s=sap, m=map_: e.dma_start(out=m, in_=s),
                       sem="gd", inc_n=16)
            b.fence()
            b.emit("gpsimd", lambda e: e.engine_nop())
            b.finalize()
            return nc

        p3.close()
        b.fence()
        p4 = es.enter_context(ExitStack())
        bias_b, B_bb = sb("bias_b", [128, D], F32, p4)
        rsel_b, B_rb = sb("rsel_b", [128, D], F32, p4)
        p4a = es.enter_context(ExitStack())
        bias_r, B_br = sb("bias_r", [1, D], F32, p4a)
        asg_s, B_as = sb("asg_s", [1, D], I32, p4a)
        asg_f, B_af = sb("asg_f", [1, D], F32, p4a)
        asg_b, B_ab = sb("asg_b", [64, D], F32, p4a)
        i64, B_i64 = sb("i64", [64, 1], I32, p4a)
        i64f, B_i64f = sb("i64f", [64, 1], F32, p4a)
        onehot, B_oh1 = sb("onehot", [64, D], F32R, p4a)
        cnt_t, B_ct2 = sb("cnt_t", [64, 1], F32, p4a)
        csel, B_cs = sb("csel", [64, 1], F32R, p4a)
        rsel_r, B_rr = sb("rsel_r", [1, D], F32, p4a)
        ps_rs, B_prs = pst("ps_rs", [1, TH], p4a)
        ps_bc4, B_pbc4 = pst("ps_bc4", [128, TH], p4a)

        def bcast_mm(dst, B_dst, row_t, B_row, nparts, ncols):
            for q0 in range(0, ncols, TH):
                b.emit("tensor", lambda e, q0=q0: e.matmul(
                    ps_bc4[0:nparts, 0:TH], onesc[:, 0:nparts], row_t[:, q0:q0 + TH],
                    start=True, stop=True),
                    reads=[B_row, B_onc], writes=[B_pbc4])
                b.emit("vector", lambda e, q0=q0: e.tensor_copy(
                    dst[0:nparts, q0:q0 + TH], ps_bc4[0:nparts, 0:TH]),
                    reads=[B_pbc4], writes=[B_dst], accum=(q0 > 0))

        b.emit("sync", lambda e: e.dma_start(out=bias_r[:], in_=bias_in),
               writes=[B_br], sem="db1", inc_n=16)
        b.emit("sync", lambda e: e.dma_start(out=asg_s[:], in_=asg_in),
               writes=[B_as], sem="db2", inc_n=16)
        bcast_mm(bias_b, B_bb, bias_r, B_br, 128, D)
        b.emit("vector", lambda e: e.tensor_copy(asg_f[:], asg_s[:]), reads=[B_as], writes=[B_af])
        bcast_mm(asg_b, B_ab, asg_f, B_af, 64, D)
        b.emit("gpsimd", lambda e: e.iota(i64[:], pattern=[[0, 1]], base=0, channel_multiplier=1),
               writes=[B_i64])
        b.emit("vector", lambda e: e.tensor_copy(i64f[:], i64[:]), reads=[B_i64], writes=[B_i64f])
        b.emit("vector", lambda e: e.tensor_scalar(onehot[:], asg_b[:], i64f[:], None,
                                                   op0=OP.is_equal),
               reads=[B_ab, B_i64f], writes=[B_oh1])
        b.emit("sync", lambda e: e.dma_start(out=cnt_t[:], in_=cnt_out.rearrange("a b -> b a")),
               reads=[dB["cnt_io"]], writes=[B_ct2], sem="db3", inc_n=16)
        b.emit("vector", lambda e: e.tensor_scalar(csel[:], cnt_t[:], 0.5, None, op0=OP.is_gt),
               reads=[B_ct2], writes=[B_cs])
        for q in range(8):
            b.emit("tensor", lambda e, q=q: e.matmul(
                ps_rs[:], csel[:], onehot[:, q * TH:(q + 1) * TH], start=True, stop=True),
                reads=[B_cs, B_oh1], writes=[B_prs])
            b.emit("vector", lambda e, q=q: e.tensor_copy(rsel_r[:, q * TH:(q + 1) * TH], ps_rs[:]),
                   reads=[B_prs], writes=[B_rr], accum=(q > 0))
        bcast_mm(rsel_b, B_rb, rsel_r, B_rr, 128, D)

        p4a.close()
        b.fence()
        l1h, B_1h = sb("l1h", [128, DC * TH], F32R, p4)
        l1l, B_1l = sb("l1l", [128, DC * TH], F32R, p4)
        wfq4, whq4, wlq4, B_wf4, B_wh4, B_wl4 = [], [], [], [], [], []
        wfq40, B_wf40 = sb("wfq40", [128, 1024], F32, p4)
        wfq4, B_wf4 = [wfq40, wfq40], [B_wf40, B_wf40]
        for i in range(2):
            t2, b2 = sb(f"whq4{i}", [128, 1024], F32R, p4)
            t3, b3 = sb(f"wlq4{i}", [128, 1024], F32R, p4)
            whq4.append(t2); wlq4.append(t3)
            B_wh4.append(b2); B_wl4.append(b3)
        ff_t, B_fft = sb("ff_t", [128, TH], F32, p4)
        x1r_t, B_x1r = sb("x1r_t", [128, TH], F32, p4)
        out_t, B_out4 = sb("out_t", [128, TH], F32, p4)
        B_outs = [Buf() for _ in range(4)]
        ps_pr4, B_pp4 = zip(*[pst(f"ps_pr4{i}", [128, TH], p4) for i in range(2)])
        ps_tr4, B_pt4 = zip(*[pst(f"ps_tr4{i}", [128, TH], p4) for i in range(2)])

        for half in range(2):
            b.emit("sync", lambda e, h=half: e.dma_start(
                out=l1h[:].rearrange("p (c t) -> p c t", t=TH),
                in_=lnh_d[:, h * TH:(h + 1) * TH].rearrange("(c p) t -> p c t", p=128)),
                reads=[dB["lnh_d"]], writes=[B_1h], sem="dl", inc_n=16)
            b.emit("sync", lambda e, h=half: e.dma_start(
                out=l1l[:].rearrange("p (c t) -> p c t", t=TH),
                in_=lnl_d[:, h * TH:(h + 1) * TH].rearrange("(c p) t -> p c t", p=128)),
                reads=[dB["lnl_d"]], writes=[B_1l], sem="dl", inc_n=16)
            for eb in range(EB):
                psum, B_psum = ps_pr4[eb % 2], B_pp4[eb % 2]
                for q in range(NQ):
                    i = q % 2
                    b.emit("sync", lambda e, eb=eb, q=q, i=i: e.dma_start(
                        out=wfq4[i][:], in_=w_hbm["wf_b"][eb, q]),
                        writes=[B_wf4[i]], sem="dw", inc_n=16)
                    b.emit("scalar", lambda e, i=i: e.copy(whq4[i][:], wfq4[i][:]),
                           reads=[B_wf4[i]], writes=[B_wh4[i]])
                    b.emit("vector", lambda e, i=i: e.tensor_tensor(
                        wlq4[i][:], wfq4[i][:], whq4[i][:].bitcast(F32), OP.subtract),
                        reads=[B_wf4[i], B_wh4[i]], writes=[B_wl4[i]])

                    def mms4(e, q=q, i=i, psum=psum):
                        ii = None
                        for cl in range(8):
                            c = q * 8 + cl
                            first = (q == 0 and cl == 0)
                            last = (q == NQ - 1 and cl == 7)
                            wh_ap = whq4[i][:, cl * 128:(cl + 1) * 128]
                            wl_ap = wlq4[i][:, cl * 128:(cl + 1) * 128]
                            xh_ap = l1h[:, c * TH:(c + 1) * TH]
                            xl_ap = l1l[:, c * TH:(c + 1) * TH]
                            e.matmul(psum[:], wh_ap, xh_ap, start=first, stop=False)
                            e.matmul(psum[:], wh_ap, xl_ap, start=False, stop=False)
                            ii = e.matmul(psum[:], wl_ap, xh_ap, start=False, stop=last)
                        return ii
                    b.emit("tensor", mms4, reads=[B_wh4[i], B_wl4[i], B_1h, B_1l],
                           writes=[B_psum], accum=(q > 0))
                b.emit("scalar", lambda e, p=psum: e.copy(ff_t[:], p[:]),
                       reads=[B_psum], writes=[B_fft])
                b.emit("sync", lambda e, eb=eb, h=half: e.dma_start(
                    out=x1r_t[:].rearrange("p (s j) -> p s j", j=128),
                    in_=x1_d[h * TH:(h + 1) * TH, eb * 128:(eb + 1) * 128]
                    .rearrange("(s p) j -> p s j", p=128)),
                    reads=[dB["x1_d"]], writes=[B_x1r], sem="dx4", inc_n=16)
                pb, B_pb4 = ps_tr4[eb % 2], B_pt4[eb % 2]

                def tr4f(e, pb=pb):
                    ii = None
                    for j in range(4):
                        ii = e.transpose(pb[:, j * 128:(j + 1) * 128].bitcast(F32),
                                         ff_t[:, j * 128:(j + 1) * 128], ident_f[:])
                    return ii
                b.emit("tensor", tr4f, reads=[B_fft, B_idf], writes=[B_pb4])
                for j in range(4):
                    b.emit("vector", lambda e, j=j, pb=pb, eb=eb: e.tensor_tensor(
                        out_t[:, j * 128:(j + 1) * 128], pb[:, j * 128:(j + 1) * 128],
                        bias_b[:, eb * 128:(eb + 1) * 128], OP.add),
                        reads=[B_pb4, B_bb], writes=[B_outs[j]])
                    b.emit("vector", lambda e, j=j, eb=eb: e.tensor_tensor(
                        out_t[:, j * 128:(j + 1) * 128], out_t[:, j * 128:(j + 1) * 128],
                        rsel_b[:, eb * 128:(eb + 1) * 128], OP.mult),
                        reads=[B_rb, B_outs[j]], writes=[B_outs[j]])
                    b.emit("vector", lambda e, j=j, h=half: e.scalar_tensor_tensor(
                        out_t[:, j * 128:(j + 1) * 128], out_t[:, j * 128:(j + 1) * 128],
                        qsel_sb[:, h * 4 + j:h * 4 + j + 1], x1r_t[:, j * 128:(j + 1) * 128],
                        op0=OP.mult, op1=OP.add),
                        reads=[B_qs, B_x1r, B_outs[j]], writes=[B_outs[j]])
                b.emit("gpsimd", lambda e, eb=eb, h=half: e.dma_start(
                    out=y_s[h * TH:(h + 1) * TH, eb * 128:(eb + 1) * 128]
                    .rearrange("(s p) j -> p s j", p=128),
                    in_=out_t[:].rearrange("p (s j) -> p s j", j=128)),
                    reads=B_outs, writes=[dB["y"]], sem="gd", inc_n=16,
                    accum=not (half == 0 and eb == 0))

        p4.close()
        b.fence()
        for (sap, map_) in dbg_mirrors:
            b.emit("gpsimd", lambda e, s=sap, m=map_: e.dma_start(out=m, in_=s),
                   sem="gd", inc_n=16)
        if dbg_mirrors:
            b.fence()
        b.emit("gpsimd", lambda e: e.engine_nop())
        b.finalize()
    return nc


_NC_CACHE = {}


def _get_program():
    if "nc" not in _NC_CACHE:
        _NC_CACHE["nc"] = build_program()
    return _NC_CACHE["nc"]


def _block_weights(w):
    wt = np.ascontiguousarray(w.T)
    v = wt.reshape(NQ, 8, 128, EB, 128)
    v = v.transpose(3, 0, 2, 1, 4)
    return np.ascontiguousarray(v.reshape(EB, NQ, 128, 1024))


def kernel(L0_attn_x, L0_attn_w_q, L0_attn_w_k, L0_attn_w_v, L0_attn_w_o,
           L0_ffn_weight, L0_ffn_bias, L0_ffn_centroids, L0_ffn_assignments):
    x = np.ascontiguousarray(np.asarray(L0_attn_x, dtype=np.float32))
    wq = _block_weights(np.asarray(L0_attn_w_q, dtype=np.float32))
    wk = _block_weights(np.asarray(L0_attn_w_k, dtype=np.float32))
    wv = _block_weights(np.asarray(L0_attn_w_v, dtype=np.float32))
    wo = _block_weights(np.asarray(L0_attn_w_o, dtype=np.float32))
    wf = _block_weights(np.asarray(L0_ffn_weight, dtype=np.float32))
    ct = np.ascontiguousarray(np.asarray(L0_ffn_centroids, dtype=np.float32).T)
    cb = np.ascontiguousarray(ct.reshape(DC, 128, NCL).transpose(1, 0, 2)
                              .reshape(128, DC * NCL))
    bias = np.ascontiguousarray(np.asarray(L0_ffn_bias, dtype=np.float32).reshape(1, D))
    asg = np.ascontiguousarray(np.asarray(L0_ffn_assignments, dtype=np.int32).reshape(1, D))

    xf = x.reshape(2 * S, D)
    nc = _get_program()
    in_maps = []
    for c in range(NCORES):
        in_maps.append({
            "x_s": np.ascontiguousarray(xf[c * T:(c + 1) * T, :]),
            "wq_b": wq, "wk_b": wk, "wv_b": wv, "wo_b": wo, "wf_b": wf,
            "cent_b": cb, "bias_in": bias, "asg_in": asg,
        })
    res = run_bass_kernel_spmd(nc, in_maps, core_ids=list(range(NCORES)))
    out = np.concatenate([res.results[c]["y_s"] for c in range(NCORES)], axis=0)
    return out.reshape(2, S, D).astype(np.float32)


# version 03
# speedup vs baseline: 1.1591x; 1.1591x over previous
import numpy as np
from contextlib import ExitStack

import concourse.bass as bass
import concourse.mybir as mybir
from concourse.bass_utils import run_bass_kernel_spmd
from concourse.alu_op_type import AluOpType as OP
from concourse import library_config

F32 = mybir.dt.float32
F32R = mybir.dt.float32r
I32 = mybir.dt.int32
AF = mybir.ActivationFunctionType
AX = mybir.AxisListType

NCORES = 8
D = 4096
T = 1024
TH = 512
S = 4096
H = 32
DC = 32
EB = 32
NQ = 4
NCL = 64
INV_SQRT_DK = float(1.0 / np.sqrt(np.float32(128.0)))
ENGS = ("sync", "scalar", "vector", "tensor", "gpsimd")
ENG_SEM = {"sync": "dh", "scalar": "sc", "vector": "dv", "tensor": "pe",
           "gpsimd": "gp"}


class Buf:

    __slots__ = ("w", "rs")

    def __init__(self):
        self.w = []
        self.rs = []


def _prune(toks):
    best = {}
    for (s, v) in toks:
        if best.get(s, -1) < v:
            best[s] = v
    return [(s, v) for s, v in best.items()]


class B:
    def __init__(self, nc, es):
        self.nc = nc
        self.es = es
        self.ops = {e: [] for e in ENGS}
        self.sems = {}
        self.fence_waits = []

    def _sem(self, name):
        if name not in self.sems:
            h = self.es.enter_context(self.nc.semaphore(name))
            self.sems[name] = [h, 0]
        return self.sems[name]

    def emit(self, eng, fn, reads=(), writes=(), extra=(), sem=None,
             inc_n=1, accum=False):
        sem = sem or ENG_SEM[eng]
        rec = self._sem(sem)
        waits = list(extra) + list(self.fence_waits)
        for bf in reads:
            waits += bf.w
        for bf in writes:
            if not accum:
                waits += bf.w + bf.rs
        rec[1] += inc_n
        tok = (sem, rec[1])
        for bf in reads:
            bf.rs = _prune(bf.rs + [tok])
        for bf in writes:
            if accum:
                bf.w = _prune(bf.w + [tok])
            else:
                bf.w = [tok]
                bf.rs = []
        self.ops[eng].append((fn, _prune(waits), (sem, inc_n)))
        return tok

    def fence(self):
        self.fence_waits = _prune(
            [(n, rec[1]) for n, rec in self.sems.items() if rec[1] > 0])

    def finalize(self):
        with self.nc.Block() as block:
            for eng in ENGS:
                oplist = self.ops[eng]
                if not oplist:
                    continue

                def mk(oplist):
                    def body(e):
                        seen = {}
                        for fn, waits, inc in oplist:
                            for (sn, v) in waits:
                                if seen.get(sn, -1) >= v:
                                    continue
                                e.wait_ge(self.sems[sn][0], v)
                                seen[sn] = v
                            instr = fn(e)
                            if instr is not None:
                                instr.then_inc(self.sems[inc[0]][0], inc[1])
                    return body

                getattr(block, eng)(mk(oplist))


def build_program(until_phase=99, dbg=None, dbg_out=()):
    nc = bass.Bass("TRN2", target_bir_lowering=False)

    dbg_mirrors = []

    def scratch(name, shape, dt):
        ap = nc.dram_tensor(name, shape, dt).ap()
        if name in dbg_out:
            m = nc.dram_tensor(name + "_dbg", shape, dt, kind="ExternalOutput").ap()
            dbg_mirrors.append((ap, m))
        return ap

    x_s = nc.dram_tensor("x_s", [T, D], F32, kind="ExternalInput").ap()
    w_hbm = {}
    for nm in ("wq_b", "wk_b", "wv_b", "wo_b", "wf_b"):
        w_hbm[nm] = nc.dram_tensor(nm, [EB, NQ, 128, 1024], F32,
                                   kind="ExternalInput").ap()
    cent_in = nc.dram_tensor("cent_b", [128, DC * NCL], F32, kind="ExternalInput").ap()
    bias_in = nc.dram_tensor("bias_in", [1, D], F32, kind="ExternalInput").ap()
    asg_in = nc.dram_tensor("asg_in", [1, D], I32, kind="ExternalInput").ap()
    y_s = nc.dram_tensor("y_s", [T, D], F32, kind="ExternalOutput").ap()

    kh_sh = scratch("kh_sh", [D, T], F32R)
    kl_sh = scratch("kl_sh", [D, T], F32R)
    v_sh = scratch("v_sh", [T, D], F32)
    kh_g = scratch("kh_g", [4 * D, T], F32R)
    kl_g = scratch("kl_g", [4 * D, T], F32R)
    v_g = scratch("v_g", [S, D], F32)
    qh_d = scratch("qh_d", [D, T], F32R)
    ql_d = scratch("ql_d", [D, T], F32R)
    oh_d = scratch("oh_d", [D, T], F32R)
    ol_d = scratch("ol_d", [D, T], F32R)
    x1_d = scratch("x1_d", [T, D], F32)
    lnh_d = scratch("lnh_d", [D, T], F32R)
    lnl_d = scratch("lnl_d", [D, T], F32R)
    kn_in = scratch("kn_in", [1, H], F32)
    kn_out = scratch("kn_out", [1, H], F32)
    cnt_in = scratch("cnt_in", [1, NCL], F32)
    cnt_out = scratch("cnt_out", [1, NCL], F32)

    dB = {nm: Buf() for nm in ("kh_sh", "kl_sh", "v_sh", "kh_g", "kl_g", "v_g",
                               "qh_d", "ql_d", "oh_d", "ol_d", "x1_d",
                               "lnh_d", "lnl_d", "kn_io", "cnt_io", "y")}

    es = ExitStack()
    b = B(nc, es)

    def sb(name, shape, dt=F32, st=None):
        stk = st or es
        t = stk.enter_context(nc.sbuf_tensor(name, shape, dt))
        return t, Buf()

    def pst(name, shape, st=None, dt=F32):
        stk = st or es
        t = stk.enter_context(nc.psum_tensor(name, shape, dt))
        return t, Buf()

    with es:
        ident_r, B_idr = sb("ident_r", [128, 128], F32R)
        ident_f, B_idf = sb("ident_f", [128, 128], F32)
        ones_r, B_onr = sb("ones_r", [128, 1], F32R)
        ones_f, B_onf = sb("ones_f", [128, 1], F32)
        onesc, B_onc = sb("onesc", [1, 128], F32)
        eps_t, B_eps = sb("eps_t", [128, 1], F32)
        qn_sb, B_qn = sb("qn_sb", [1, H], F32)
        kn_sb, B_kn = sb("kn_sb", [1, H], F32)
        qsel_sb, B_qs = sb("qsel_sb", [128, 8], F32)
        cnt_sb, B_cnt = sb("cnt_sb", [1, NCL], F32)

        p0 = es.enter_context(ExitStack())
        i128, B_i1 = sb("i128", [128, 1], I32, p0)
        irow, B_i2 = sb("irow", [1, 128], I32, p0)
        i128f, B_i1f = sb("i128f", [128, 1], F32, p0)
        irowf, B_i2f = sb("irowf", [1, 128], F32, p0)
        irowb, B_i2b = sb("irowb", [128, 128], F32, p0)

        b.emit("gpsimd", lambda e: e.iota(i128[:], pattern=[[0, 1]], base=0,
                                          channel_multiplier=1), writes=[B_i1])
        b.emit("gpsimd", lambda e: e.iota(irow[:], pattern=[[1, 128]], base=0,
                                          channel_multiplier=0), writes=[B_i2])
        b.emit("vector", lambda e: e.tensor_copy(i128f[:], i128[:]),
               reads=[B_i1], writes=[B_i1f])
        b.emit("vector", lambda e: e.tensor_copy(irowf[:], irow[:]),
               reads=[B_i2], writes=[B_i2f])
        ps_bc0, B_pbc0 = pst("ps_bc0", [128, 128], p0)
        b.emit("vector", lambda e: e.memset(onesc[:], 1.0), writes=[B_onc])
        b.emit("tensor", lambda e: e.matmul(ps_bc0[:], onesc[:], irowf[:],
                                            start=True, stop=True),
               reads=[B_i2f, B_onc], writes=[B_pbc0])
        b.emit("vector", lambda e: e.tensor_copy(irowb[:], ps_bc0[:]),
               reads=[B_pbc0], writes=[B_i2b])
        b.emit("vector", lambda e: e.tensor_scalar(ident_f[:], irowb[:], i128f[:], None,
                                                   op0=OP.is_equal),
               reads=[B_i2b, B_i1f], writes=[B_idf])
        b.emit("vector", lambda e: e.tensor_copy(ident_r[:], ident_f[:]),
               reads=[B_idf], writes=[B_idr])
        b.emit("vector", lambda e: e.memset(ones_f[:], 1.0), writes=[B_onf])
        b.emit("scalar", lambda e: e.copy(ones_r[:], ones_f[:]), reads=[B_onf], writes=[B_onr])
        b.emit("vector", lambda e: e.memset(eps_t[:], 1e-5), writes=[B_eps])
        b.emit("vector", lambda e: e.memset(qn_sb[:], 0.0), writes=[B_qn])
        b.emit("vector", lambda e: e.memset(kn_sb[:], 0.0), writes=[B_kn])
        b.emit("vector", lambda e: e.memset(cnt_sb[:], 0.0), writes=[B_cnt])

        p0.close()
        b.fence()

        p1 = es.enter_context(ExitStack())
        lnxh, B_lh = sb("lnxh", [128, DC * TH], F32R, p1)
        lnxl, B_ll = sb("lnxl", [128, DC * TH], F32R, p1)
        xt_t, B_xt = sb("xt_t", [128, D], F32, p1)
        lnh_tm, B_htm = sb("lnh_tm", [128, D], F32R, p1)
        lnl_tm, B_ltm = sb("lnl_tm", [128, D], F32R, p1)
        stats, B_st = sb("stats", [128, 48], F32, p1)
        mv_t, B_mv = sb("mv_t", [128, 2], F32, p1)
        std_t, B_sd = sb("std_t", [128, 1], F32, p1)
        rstd_t, B_rs = sb("rstd_t", [128, 1], F32, p1)
        nmr_t, B_nm = sb("nmr_t", [128, 1], F32, p1)
        whq, wlq, B_wh, B_wl = [], [], [], []
        wfq0, B_wf0 = sb("wfq0", [128, 1024], F32, p1)
        wfq, B_wf = [wfq0, wfq0], [B_wf0, B_wf0]
        for i in range(2):
            t2, b2 = sb(f"whq{i}", [128, 1024], F32R, p1)
            t3, b3 = sb(f"wlq{i}", [128, 1024], F32R, p1)
            whq.append(t2); wlq.append(t3)
            B_wh.append(b2); B_wl.append(b3)
        evh, B_evh = sb("evh", [128, TH], F32R, p1)
        evl, B_evl = sb("evl", [128, TH], F32R, p1)
        evn, B_evn = sb("evn", [128, TH], F32, p1)
        evf, B_evf = evh, B_evh
        sqn, B_sqn = evn, B_evn
        nrm2, B_nr2 = sb("nrm2", [1, 1], F32, p1)
        ps_pr, B_pp = zip(*[pst(f"ps_pr{i}", [128, TH], p1) for i in range(2)])
        ps_tr, B_pt = zip(*[pst(f"ps_tr{i}", [128, TH], p1) for i in range(2)])
        ps_nm, B_pn = pst("ps_nm", [1, TH], p1)

        def ln_into(src_rows, half, dsth, dstl, B_dh, B_dl, chunk_cols, extra_w=()):
            for tt in range(4):
                r0 = half * TH + tt * 128
                b.emit("sync", lambda e, r0=r0: e.dma_start(out=xt_t[:], in_=src_rows[r0:r0 + 128, :]),
                       writes=[B_xt], extra=extra_w, sem="dx", inc_n=16)

                def bnall(e):
                    ii = None
                    for i in range(8):
                        ii = e.bn_stats(stats[:, i * 6:(i + 1) * 6], xt_t[:, i * 512:(i + 1) * 512])
                    return ii
                b.emit("vector", bnall, reads=[B_xt], writes=[B_st])
                b.emit("vector", lambda e: e.bn_aggr(mv_t[:], stats[:]), reads=[B_st], writes=[B_mv])
                b.emit("scalar", lambda e: e.activation(std_t[:], mv_t[:, 1:2], AF.Sqrt,
                                                        bias=eps_t[:], scale=1.0),
                       reads=[B_mv, B_eps], writes=[B_sd])
                b.emit("vector", lambda e: e.reciprocal(rstd_t[:], std_t[:]),
                       reads=[B_sd], writes=[B_rs])
                b.emit("vector", lambda e: e.tensor_scalar(nmr_t[:], mv_t[:, 0:1], rstd_t[:], -1.0,
                                                           op0=OP.mult, op1=OP.mult),
                       reads=[B_mv, B_rs], writes=[B_nm])
                b.emit("scalar", lambda e: e.activation(lnh_tm[:], xt_t[:], AF.Identity,
                                                        bias=nmr_t[:], scale=rstd_t[:]),
                       reads=[B_xt, B_nm, B_rs], writes=[B_htm])
                b.emit("vector", lambda e: e.scalar_tensor_tensor(
                    lnl_tm[:], xt_t[:], rstd_t[:], lnh_tm[:].bitcast(F32),
                    op0=OP.mult, op1=OP.subtract),
                    reads=[B_xt, B_rs, B_htm], writes=[B_ltm])
                b.emit("vector", lambda e: e.tensor_scalar(lnl_tm[:], lnl_tm[:], nmr_t[:], None,
                                                           op0=OP.add),
                       reads=[B_nm], writes=[B_ltm])
                for (src, B_src, dst, B_dst) in ((lnh_tm, B_htm, dsth, B_dh),
                                                 (lnl_tm, B_ltm, dstl, B_dl)):
                    for g in range(8):
                        pb, B_pb = ps_tr[g % 2], B_pt[g % 2]

                        def tr4(e, src=src, g=g, pb=pb):
                            ii = None
                            for k in range(4):
                                c = g * 4 + k
                                ii = e.transpose(pb[:, k * 128:(k + 1) * 128].bitcast(F32R),
                                                 src[:, c * 128:(c + 1) * 128], ident_r[:])
                            return ii
                        b.emit("tensor", tr4, reads=[B_src, B_idr], writes=[B_pb])

                        def ev4(e, dst=dst, g=g, tt=tt, pb=pb, cc=chunk_cols):
                            o = dst[:, 0:0]
                            ap_out = dst[:].rearrange("p (c t) -> p c t", t=cc)[
                                :, g * 4:(g + 1) * 4, tt * 128:tt * 128 + 128]
                            ap_in = pb[:].bitcast(F32R).rearrange("p (k t) -> p k t", t=128)
                            return e.copy(ap_out, ap_in)
                        b.emit("scalar", ev4, reads=[B_pb], writes=[B_dst], accum=(tt + g > 0))

        def stream_quarter(wap, eb, q, i, dma_sem="dw"):
            b.emit("sync", lambda e, eb=eb, q=q, i=i: e.dma_start(out=wfq[i][:], in_=wap[eb, q]),
                   reads=[], writes=[B_wf[i]], sem=dma_sem, inc_n=16)
            b.emit("scalar", lambda e, i=i: e.copy(whq[i][:], wfq[i][:]),
                   reads=[B_wf[i]], writes=[B_wh[i]])
            b.emit("vector", lambda e, i=i: e.tensor_tensor(wlq[i][:], wfq[i][:],
                                                            whq[i][:].bitcast(F32), OP.subtract),
                   reads=[B_wf[i], B_wh[i]], writes=[B_wl[i]])

        def proj_block(wap, eb, xh, xl, B_xh, B_xl, psum, B_psum):
            for q in range(NQ):
                i = q % 2
                stream_quarter(wap, eb, q, i)

                def mms(e, q=q, i=i):
                    ii = None
                    for cl in range(8):
                        c = q * 8 + cl
                        first = (q == 0 and cl == 0)
                        last = (q == NQ - 1 and cl == 7)
                        wh_ap = whq[i][:, cl * 128:(cl + 1) * 128]
                        wl_ap = wlq[i][:, cl * 128:(cl + 1) * 128]
                        xh_ap = xh[:, c * TH:(c + 1) * TH]
                        xl_ap = xl[:, c * TH:(c + 1) * TH]
                        e.matmul(psum[:], wh_ap, xh_ap, start=first, stop=False)
                        e.matmul(psum[:], wh_ap, xl_ap, start=False, stop=False)
                        ii = e.matmul(psum[:], wl_ap, xh_ap, start=False, stop=last)
                    return ii
                b.emit("tensor", mms, reads=[B_wh[i], B_wl[i], B_xh, B_xl],
                       writes=[B_psum], accum=(q > 0))

        def evac_split_store(psum, B_psum, dsth_ap, dstl_ap, B_dsth, B_dstl,
                             norm_buf, norm_col, first_hl):
            b.emit("scalar", lambda e: e.copy(evh[:], psum[:]), reads=[B_psum], writes=[B_evh])
            b.emit("vector", lambda e: e.tensor_tensor(evl[:], psum[:],
                                                       evh[:].bitcast(F32), OP.subtract),
                   reads=[B_psum, B_evh], writes=[B_evl])
            b.emit("gpsimd", lambda e: e.dma_start(out=dsth_ap, in_=evh[:]),
                   reads=[B_evh], writes=[B_dsth], sem="gd", inc_n=16, accum=not first_hl)
            b.emit("gpsimd", lambda e: e.dma_start(out=dstl_ap, in_=evl[:]),
                   reads=[B_evl], writes=[B_dstl], sem="gd", inc_n=16, accum=not first_hl)
            b.emit("scalar", lambda e: e.activation(sqn[:], evh[:].bitcast(F32), AF.Square,
                                                    scale=1.0),
                   reads=[B_evh], writes=[B_sqn])
            b.emit("tensor", lambda e: e.matmul(ps_nm[:], ones_r[:], sqn[:], start=True, stop=True),
                   reads=[B_sqn, B_onr], writes=[B_pn])
            b.emit("vector", lambda e: e.tensor_reduce(nrm2[:], ps_nm[:], axis=AX.X, op=OP.max),
                   reads=[B_pn], writes=[B_nr2])
            nb, col = norm_buf
            b.emit("vector", lambda e, col=norm_col: e.tensor_tensor(
                nb[:, col:col + 1], nb[:, col:col + 1], nrm2[:], OP.max),
                reads=[B_nr2], writes=[norm_buf[1]] if False else [])

        def norm_rmw(nb_t, B_nb, col):
            b.emit("vector", lambda e, col=col: e.tensor_tensor(
                nb_t[:, col:col + 1], nb_t[:, col:col + 1], nrm2[:], OP.max),
                reads=[B_nr2, B_nb], writes=[B_nb])

        if until_phase >= 1:
            for half in range(2):
                ln_into(x_s, half, lnxh, lnxl, B_lh, B_ll, TH)
                for wname in ("k", "v", "q"):
                    wap = w_hbm[f"w{wname}_b"]
                    for eb in range(EB):
                        psum, B_psum = ps_pr[eb % 2], B_pp[eb % 2]
                        proj_block(wap, eb, lnxh, lnxl, B_lh, B_ll, psum, B_psum)
                        if wname in ("k", "q"):
                            dh = kh_sh if wname == "k" else qh_d
                            dl = kl_sh if wname == "k" else ql_d
                            B_dh = dB["kh_sh"] if wname == "k" else dB["qh_d"]
                            B_dl = dB["kl_sh"] if wname == "k" else dB["ql_d"]
                            first_hl = (half == 0 and eb == 0)
                            b.emit("scalar", lambda e, p=psum: e.copy(evh[:], p[:]),
                                   reads=[B_psum], writes=[B_evh])
                            b.emit("vector", lambda e, p=psum: e.tensor_tensor(
                                evl[:], p[:], evh[:].bitcast(F32), OP.subtract),
                                reads=[B_psum, B_evh], writes=[B_evl])
                            b.emit("gpsimd", lambda e, d=dh, eb=eb, h=half: e.dma_start(
                                out=d[eb * 128:(eb + 1) * 128, h * TH:(h + 1) * TH], in_=evh[:]),
                                reads=[B_evh], writes=[B_dh], sem="gd", inc_n=16,
                                accum=not first_hl)
                            b.emit("gpsimd", lambda e, d=dl, eb=eb, h=half: e.dma_start(
                                out=d[eb * 128:(eb + 1) * 128, h * TH:(h + 1) * TH], in_=evl[:]),
                                reads=[B_evl], writes=[B_dl], sem="gd", inc_n=16,
                                accum=not first_hl)
                            b.emit("scalar", lambda e: e.activation(sqn[:],
                                                                    evh[:].bitcast(F32),
                                                                    AF.Square, scale=1.0),
                                   reads=[B_evh], writes=[B_sqn])
                            b.emit("tensor", lambda e: e.matmul(ps_nm[:], ones_f[:],
                                                                sqn[:],
                                                                start=True, stop=True),
                                   reads=[B_sqn, B_onf], writes=[B_pn])
                            b.emit("vector", lambda e: e.tensor_reduce(nrm2[:], ps_nm[:],
                                                                       axis=AX.X, op=OP.max),
                                   reads=[B_pn], writes=[B_nr2])
                            if wname == "k":
                                norm_rmw(kn_sb, B_kn, eb)
                            else:
                                norm_rmw(qn_sb, B_qn, eb)
                        else:
                            b.emit("scalar", lambda e, p=psum: e.copy(evf[:].bitcast(F32), p[:]),
                                   reads=[B_psum], writes=[B_evf])
                            pb, B_pb = ps_tr[eb % 2], B_pt[eb % 2]

                            def tr4v(e, pb=pb):
                                ii = None
                                for j in range(4):
                                    ii = e.transpose(pb[:, j * 128:(j + 1) * 128].bitcast(F32),
                                                     evf[:, j * 128:(j + 1) * 128].bitcast(F32),
                                                     ident_f[:])
                                return ii
                            b.emit("tensor", tr4v, reads=[B_evf, B_idf], writes=[B_pb])
                            b.emit("vector", lambda e, pb=pb: e.tensor_copy(evn[:], pb[:]),
                                   reads=[B_pb], writes=[B_evn])
                            b.emit("gpsimd", lambda e, eb=eb, h=half: e.dma_start(
                                out=v_sh[h * TH:(h + 1) * TH, eb * 128:(eb + 1) * 128]
                                .rearrange("(j p) c -> p j c", p=128),
                                in_=evn[:].rearrange("p (j c) -> p j c", c=128)),
                                reads=[B_evn], writes=[dB["v_sh"]], sem="gd", inc_n=16,
                                accum=not (half == 0 and eb == 0))
                if half == 1:
                    b.emit("gpsimd", lambda e: e.dma_start(out=kn_in, in_=kn_sb[:]),
                           reads=[B_kn], writes=[dB["kn_io"]], sem="gd", inc_n=16)
                    b.emit("gpsimd", lambda e: e.collective_compute(
                        "AllReduce", OP.max, replica_groups=[[0, 1, 2, 3, 4, 5, 6, 7]],
                        ins=[kn_in.opt()], outs=[kn_out.opt()]),
                        reads=[dB["kn_io"]], writes=[dB["kn_io"]], sem="cc")
                    b.emit("gpsimd", lambda e: e.collective_compute(
                        "AllGather", OP.bypass, replica_groups=[[0, 1, 2, 3], [4, 5, 6, 7]],
                        ins=[kh_sh.opt()], outs=[kh_g.opt()]),
                        reads=[dB["kh_sh"]], writes=[dB["kh_g"]], sem="cc")
                    b.emit("gpsimd", lambda e: e.collective_compute(
                        "AllGather", OP.bypass, replica_groups=[[0, 1, 2, 3], [4, 5, 6, 7]],
                        ins=[kl_sh.opt()], outs=[kl_g.opt()]),
                        reads=[dB["kl_sh"]], writes=[dB["kl_g"]], sem="cc")
                    b.emit("gpsimd", lambda e: e.collective_compute(
                        "AllGather", OP.bypass, replica_groups=[[0, 1, 2, 3], [4, 5, 6, 7]],
                        ins=[v_sh.opt()], outs=[v_g.opt()]),
                        reads=[dB["v_sh"]], writes=[dB["v_g"]], sem="cc")

        if dbg == 1 or until_phase < 2:
            p1.close()
            b.fence()
            for (sap, map_) in dbg_mirrors:
                b.emit("gpsimd", lambda e, s=sap, m=map_: e.dma_start(out=m, in_=s),
                       sem="gd", inc_n=16)
            b.fence()
            b.emit("gpsimd", lambda e: e.engine_nop())
            b.finalize()
            return nc

        p1.close()
        b.fence()
        p2 = es.enter_context(ExitStack())
        khb, klb, vbb, qhb, qlb = [], [], [], [], []
        B_khb, B_klb, B_vbb, B_qhb, B_qlb = [], [], [], [], []
        for i in range(2):
            t1, b1 = sb(f"khb{i}", [128, S], F32R, p2); khb.append(t1); B_khb.append(b1)
            t2, b2 = sb(f"klb{i}", [128, S], F32R, p2); klb.append(t2); B_klb.append(b2)
            t3, b3 = sb(f"vbb{i}", [128, 32 * 129], F32, p2); vbb.append(t3); B_vbb.append(b3)
            t4, b4 = sb(f"qhb{i}", [128, T], F32R, p2); qhb.append(t4); B_qhb.append(b4)
            t5, b5 = sb(f"qlb{i}", [128, T], F32R, p2); qlb.append(t5); B_qlb.append(b5)
        ptile, B_ptile = zip(*[sb(f"ptile{i}", [128, TH], F32, p2) for i in range(3)])
        otm, B_otm = sb("otm", [128, TH], F32, p2)
        ohh, B_ohh = sb("ohh", [128, TH], F32R, p2)
        ohl, B_ohl = sb("ohl", [128, TH], F32R, p2)
        rz_t, B_rz = sb("rz_t", [128, 4], F32, p2)
        kng, B_kng = sb("kng", [1, H], F32, p2)
        mh_t, B_mh = sb("mh_t", [1, 1], F32, p2)
        mhb, B_mhb = sb("mhb", [128, 1], F32, p2)
        ps_s, B_ps = zip(*[pst(f"ps_s{i}", [128, TH], p2) for i in range(2)])
        ps_av, B_pav = zip(*[pst(f"ps_av{i}", [128, 129], p2) for i in range(4)])
        ps_ot, B_pot = pst("ps_ot", [128, TH], p2)
        ps_bc2, B_pbc2 = pst("ps_bc2", [128, 1], p2)

        for i in range(2):
            b.emit("vector", lambda e, i=i: e.memset(
                vbb[i][:].rearrange("p (c j) -> p c j", j=129)[:, :, 128:129], 1.0),
                writes=[B_vbb[i]])
        b.emit("sync", lambda e: e.dma_start(out=kng[:], in_=kn_out),
               reads=[dB["kn_io"]], writes=[B_kng], sem="dkg", inc_n=16)

        def load_head(h, i):
            for r in range(4):
                b.emit("sync", lambda e, h=h, i=i, r=r: e.dma_start(
                    out=khb[i][:, r * T:(r + 1) * T],
                    in_=kh_g[r * D + h * 128: r * D + (h + 1) * 128, :]),
                    reads=[dB["kh_g"]], writes=[B_khb[i]], sem=f"ka{i}", inc_n=16,
                    accum=(r > 0))
                b.emit("sync", lambda e, h=h, i=i, r=r: e.dma_start(
                    out=klb[i][:, r * T:(r + 1) * T],
                    in_=kl_g[r * D + h * 128: r * D + (h + 1) * 128, :]),
                    reads=[dB["kl_g"]], writes=[B_klb[i]], sem=f"kb{i}", inc_n=16,
                    accum=(r > 0))
            b.emit("sync", lambda e, h=h, i=i: e.dma_start(
                out=qhb[i][:], in_=qh_d[h * 128:(h + 1) * 128, :]),
                reads=[dB["qh_d"]], writes=[B_qhb[i]], sem=f"qa{i}", inc_n=16)
            b.emit("sync", lambda e, h=h, i=i: e.dma_start(
                out=qlb[i][:], in_=ql_d[h * 128:(h + 1) * 128, :]),
                reads=[dB["ql_d"]], writes=[B_qlb[i]], sem=f"qb{i}", inc_n=16)

            def ldv(e, h=h, i=i):
                return e.dma_start(
                    out=vbb[i][:].rearrange("p (c j) -> p c j", j=129)[:, :, 0:128],
                    in_=v_g[:, h * 128:(h + 1) * 128].rearrange("(c p) j -> p c j", p=128))
            b.emit("sync", ldv, reads=[dB["v_g"]], writes=[B_vbb[i]], sem=f"va{i}",
                   inc_n=16)

        def emit_mh(h):
            b.emit("vector", lambda e, h=h: e.tensor_tensor(
                mh_t[:], qn_sb[:, h:h + 1], kng[:, h:h + 1], OP.mult),
                reads=[B_qn, B_kng], writes=[B_mh])
            b.emit("scalar", lambda e: e.activation(mh_t[:], mh_t[:], AF.Sqrt,
                                                    scale=1.0002 / 128.0),
                   reads=[B_mh], writes=[B_mh])
            b.emit("vector", lambda e: e.tensor_scalar(mh_t[:], mh_t[:], -1.0, None, op0=OP.mult),
                   reads=[B_mh], writes=[B_mh])
            b.emit("tensor", lambda e: e.matmul(ps_bc2[:], onesc[:], mh_t[:],
                                                start=True, stop=True),
                   reads=[B_mh, B_onc], writes=[B_pbc2])
            b.emit("vector", lambda e: e.tensor_copy(mhb[:], ps_bc2[:]),
                   reads=[B_pbc2], writes=[B_mhb])

        def emit_smm(i, qi, c):
            sps, B_sps = ps_s[c % 2], B_ps[c % 2]

            def smm(e, i=i, c=c, qi=qi, sps=sps):
                e.matmul(sps[:], khb[i][:, c * 128:(c + 1) * 128],
                         qhb[i][:, qi * TH:(qi + 1) * TH], start=True, stop=False)
                e.matmul(sps[:], khb[i][:, c * 128:(c + 1) * 128],
                         qlb[i][:, qi * TH:(qi + 1) * TH], start=False, stop=False)
                return e.matmul(sps[:], klb[i][:, c * 128:(c + 1) * 128],
                                qhb[i][:, qi * TH:(qi + 1) * TH], start=False, stop=True)
            b.emit("tensor", smm, reads=[B_khb[i], B_klb[i], B_qhb[i], B_qlb[i]],
                   writes=[B_sps])

        def emit_exp(c):
            sps, B_sps = ps_s[c % 2], B_ps[c % 2]
            pt, B_pt2 = ptile[c % 3], B_ptile[c % 3]
            b.emit("scalar", lambda e, sps=sps, pt=pt: e.activation(
                pt[:], sps[:], AF.Exp, bias=mhb[:], scale=INV_SQRT_DK),
                reads=[B_sps, B_mhb], writes=[B_pt2])

        def emit_avm(i, c):
            pt, B_pt2 = ptile[c % 3], B_ptile[c % 3]

            def avm(e, i=i, c=c, pt=pt):
                ii = None
                for j in range(4):
                    ii = e.matmul(ps_av[j][:], pt[:, j * 128:(j + 1) * 128],
                                  vbb[i][:, c * 129:(c + 1) * 129],
                                  start=(c == 0), stop=(c == 31))
                return ii
            b.emit("tensor", avm, reads=[B_pt2, B_vbb[i]],
                   writes=list(B_pav), accum=(c > 0))

        def emit_evac(h, qi):
            for j in range(4):
                b.emit("vector", lambda e, j=j: e.reciprocal(rz_t[:, j:j + 1],
                                                             ps_av[j][:, 128:129]),
                       reads=[B_pav[j]], writes=[B_rz], accum=(j > 0))
                b.emit("vector", lambda e, j=j: e.tensor_scalar(
                    otm[:, j * 128:(j + 1) * 128], ps_av[j][:, 0:128],
                    rz_t[:, j:j + 1], None, op0=OP.mult),
                    reads=[B_pav[j], B_rz], writes=[B_otm], accum=(j > 0))

            def tr4o(e):
                ii = None
                for j in range(4):
                    ii = e.transpose(ps_ot[:, j * 128:(j + 1) * 128].bitcast(F32),
                                     otm[:, j * 128:(j + 1) * 128], ident_f[:])
                return ii
            b.emit("tensor", tr4o, reads=[B_otm, B_idf], writes=[B_pot])
            b.emit("scalar", lambda e: e.copy(ohh[:], ps_ot[:]), reads=[B_pot], writes=[B_ohh])
            b.emit("vector", lambda e: e.tensor_tensor(ohl[:], ps_ot[:],
                                                       ohh[:].bitcast(F32), OP.subtract),
                   reads=[B_pot, B_ohh], writes=[B_ohl])
            b.emit("gpsimd", lambda e, h=h, qi=qi: e.dma_start(
                out=oh_d[h * 128:(h + 1) * 128, qi * TH:(qi + 1) * TH], in_=ohh[:]),
                reads=[B_ohh], writes=[dB["oh_d"]], sem="gd", inc_n=16,
                accum=not (h == 0 and qi == 0))
            b.emit("gpsimd", lambda e, h=h, qi=qi: e.dma_start(
                out=ol_d[h * 128:(h + 1) * 128, qi * TH:(qi + 1) * TH], in_=ohl[:]),
                reads=[B_ohl], writes=[dB["ol_d"]], sem="gd", inc_n=16,
                accum=not (h == 0 and qi == 0))

        load_head(0, 0)
        pending_evac = None
        for h in range(H):
            i = h % 2
            if h + 1 < H:
                load_head(h + 1, (h + 1) % 2)
            emit_mh(h)
            for qi in range(2):
                for c in range(32):
                    emit_smm(i, qi, c)
                    if c == 0:
                        emit_exp(c)
                        if pending_evac is not None:
                            emit_evac(*pending_evac)
                            pending_evac = None
                    else:
                        emit_avm(i, c - 1)
                        emit_exp(c)
                emit_avm(i, 31)
                pending_evac = (h, qi)
        emit_evac(*pending_evac)

        if dbg == 2 or until_phase < 3:
            p2.close()
            b.fence()
            for (sap, map_) in dbg_mirrors:
                b.emit("gpsimd", lambda e, s=sap, m=map_: e.dma_start(out=m, in_=s),
                       sem="gd", inc_n=16)
            b.fence()
            b.emit("gpsimd", lambda e: e.engine_nop())
            b.finalize()
            return nc

        p2.close()
        b.fence()
        p3 = es.enter_context(ExitStack())
        o_hh, B_oh = sb("o_hh", [128, DC * TH], F32R, p3)
        o_ll, B_ol = sb("o_ll", [128, DC * TH], F32R, p3)
        wfq3, whq3, wlq3, B_wf3, B_wh3, B_wl3 = [], [], [], [], [], []
        for i in range(2):
            t1, b1 = sb(f"wfq3{i}", [128, 1024], F32, p3)
            t2, b2 = sb(f"whq3{i}", [128, 1024], F32R, p3)
            t3, b3 = sb(f"wlq3{i}", [128, 1024], F32R, p3)
            wfq3.append(t1); whq3.append(t2); wlq3.append(t3)
            B_wf3.append(b1); B_wh3.append(b2); B_wl3.append(b3)
        of_t, B_oft = sb("of_t", [128, TH], F32, p3)
        x_t4, B_xt4 = sb("x_t4", [128, TH], F32, p3)
        x1_t, B_x1t = sb("x1_t", [128, TH], F32, p3)
        ps_pr3, B_pp3 = zip(*[pst(f"ps_pr3{i}", [128, TH], p3) for i in range(2)])
        ps_tr3, B_pt3 = zip(*[pst(f"ps_tr3{i}", [128, TH], p3) for i in range(2)])

        def proj_block3(wap, eb, xh, xl, B_xh, B_xl, psum, B_psum,
                        wf, wh, wl, B_wfL, B_whL, B_wlL):
            for q in range(NQ):
                i = q % 2
                b.emit("sync", lambda e, eb=eb, q=q, i=i: e.dma_start(out=wf[i][:], in_=wap[eb, q]),
                       writes=[B_wfL[i]], sem=f"dw3{i}", inc_n=16)
                b.emit("scalar", lambda e, i=i: e.copy(wh[i][:], wf[i][:]),
                       reads=[B_wfL[i]], writes=[B_whL[i]])
                b.emit("vector", lambda e, i=i: e.tensor_tensor(wl[i][:], wf[i][:],
                                                                wh[i][:].bitcast(F32), OP.subtract),
                       reads=[B_wfL[i], B_whL[i]], writes=[B_wlL[i]])

                def mms(e, q=q, i=i):
                    ii = None
                    for cl in range(8):
                        c = q * 8 + cl
                        first = (q == 0 and cl == 0)
                        last = (q == NQ - 1 and cl == 7)
                        wh_ap = wh[i][:, cl * 128:(cl + 1) * 128]
                        wl_ap = wl[i][:, cl * 128:(cl + 1) * 128]
                        xh_ap = xh[:, c * TH:(c + 1) * TH]
                        xl_ap = xl[:, c * TH:(c + 1) * TH]
                        e.matmul(psum[:], wh_ap, xh_ap, start=first, stop=False)
                        e.matmul(psum[:], wh_ap, xl_ap, start=False, stop=False)
                        ii = e.matmul(psum[:], wl_ap, xh_ap, start=False, stop=last)
                    return ii
                b.emit("tensor", mms, reads=[B_whL[i], B_wlL[i], B_xh, B_xl],
                       writes=[B_psum], accum=(q > 0))

        for half in range(2):
            b.emit("sync", lambda e, h=half: e.dma_start(
                out=o_hh[:].rearrange("p (c t) -> p c t", t=TH),
                in_=oh_d[:, h * TH:(h + 1) * TH].rearrange("(c p) t -> p c t", p=128)),
                reads=[dB["oh_d"]], writes=[B_oh], sem="do", inc_n=16)
            b.emit("sync", lambda e, h=half: e.dma_start(
                out=o_ll[:].rearrange("p (c t) -> p c t", t=TH),
                in_=ol_d[:, h * TH:(h + 1) * TH].rearrange("(c p) t -> p c t", p=128)),
                reads=[dB["ol_d"]], writes=[B_ol], sem="do", inc_n=16)
            for eb in range(EB):
                psum, B_psum = ps_pr3[eb % 2], B_pp3[eb % 2]
                proj_block3(w_hbm["wo_b"], eb, o_hh, o_ll, B_oh, B_ol, psum, B_psum,
                            wfq3, whq3, wlq3, B_wf3, B_wh3, B_wl3)
                b.emit("scalar", lambda e, p=psum: e.copy(of_t[:], p[:]),
                       reads=[B_psum], writes=[B_oft])
                b.emit("sync", lambda e, eb=eb, h=half: e.dma_start(
                    out=x_t4[:].rearrange("p (s j) -> p s j", j=128),
                    in_=x_s[h * TH:(h + 1) * TH, eb * 128:(eb + 1) * 128]
                    .rearrange("(s p) j -> p s j", p=128)),
                    writes=[B_xt4], sem="dx4", inc_n=16)
                pb, B_pb = ps_tr3[eb % 2], B_pt3[eb % 2]

                def tr4p(e, pb=pb):
                    ii = None
                    for j in range(4):
                        ii = e.transpose(pb[:, j * 128:(j + 1) * 128].bitcast(F32),
                                         of_t[:, j * 128:(j + 1) * 128], ident_f[:])
                    return ii
                b.emit("tensor", tr4p, reads=[B_oft, B_idf], writes=[B_pb])
                b.emit("vector", lambda e, pb=pb: e.tensor_tensor(x1_t[:], pb[:], x_t4[:], OP.add),
                       reads=[B_pb, B_xt4], writes=[B_x1t])
                b.emit("gpsimd", lambda e, eb=eb, h=half: e.dma_start(
                    out=x1_d[h * TH:(h + 1) * TH, eb * 128:(eb + 1) * 128]
                    .rearrange("(s p) j -> p s j", p=128),
                    in_=x1_t[:].rearrange("p (s j) -> p s j", j=128)),
                    reads=[B_x1t], writes=[dB["x1_d"]], sem="gd", inc_n=16,
                    accum=not (half == 0 and eb == 0))

        p3.close()
        b.fence()
        p3 = es.enter_context(ExitStack())
        ps_tr3, B_pt3 = zip(*[pst(f"ps_trb{i}", [128, TH], p3) for i in range(2)])
        cent_f, B_ctf = sb("cent_f", [128, DC * NCL], F32, p3)
        cent_h, B_cth = sb("cent_h", [128, DC * NCL], F32R, p3)
        cent_l, B_ctl = sb("cent_l", [128, DC * NCL], F32R, p3)
        b.emit("sync", lambda e: e.dma_start(out=cent_f[:], in_=cent_in),
               writes=[B_ctf], sem="dct", inc_n=16)
        b.emit("scalar", lambda e: e.copy(cent_h[:], cent_f[:]), reads=[B_ctf], writes=[B_cth])
        b.emit("vector", lambda e: e.tensor_tensor(cent_l[:], cent_f[:],
                                                   cent_h[:].bitcast(F32), OP.subtract),
               reads=[B_ctf, B_cth], writes=[B_ctl])
        ln2h_t, B_l2h = sb("ln2h_t", [128, DC * 128], F32R, p3)
        ln2l_t, B_l2l = sb("ln2l_t", [128, DC * 128], F32R, p3)
        xt2, B_xt2 = sb("xt2", [128, D], F32, p3)
        l2h_tm, B_h2 = sb("l2h_tm", [128, D], F32R, p3)
        l2l_tm, B_l2 = sb("l2l_tm", [128, D], F32R, p3)
        st2, B_st2 = sb("st2", [128, 48], F32, p3)
        mv2, B_mv2 = sb("mv2", [128, 2], F32, p3)
        sd2, B_sd2 = sb("sd2", [128, 1], F32, p3)
        rs2, B_rs2 = sb("rs2", [128, 1], F32, p3)
        nm2, B_nm2 = sb("nm2", [128, 1], F32, p3)
        dex, B_dex = sb("dex", [128, NCL], F32, p3)
        dmx, B_dmx = sb("dmx", [128, 1], F32, p3)
        dnb, B_dnb = sb("dnb", [128, 1], F32, p3)
        dzs, B_dzs = sb("dzs", [128, 1], F32, p3)
        dth, B_dth = sb("dth", [128, 1], F32, p3)
        hits, B_hit = sb("hits", [128, NCL], F32R, p3)
        ps_dt, B_pdt = pst("ps_dt", [128, NCL], p3)
        ps_ct, B_pct = pst("ps_ct", [1, NCL], p3)

        for ti in range(8):
            b.emit("sync", lambda e, ti=ti: e.dma_start(out=xt2[:],
                                                        in_=x1_d[ti * 128:(ti + 1) * 128, :]),
                   reads=[dB["x1_d"]], writes=[B_xt2], sem="dx", inc_n=16)

            def bn2(e):
                ii = None
                for i in range(8):
                    ii = e.bn_stats(st2[:, i * 6:(i + 1) * 6], xt2[:, i * 512:(i + 1) * 512])
                return ii
            b.emit("vector", bn2, reads=[B_xt2], writes=[B_st2])
            b.emit("vector", lambda e: e.bn_aggr(mv2[:], st2[:]), reads=[B_st2], writes=[B_mv2])
            b.emit("scalar", lambda e: e.activation(sd2[:], mv2[:, 1:2], AF.Sqrt,
                                                    bias=eps_t[:], scale=1.0),
                   reads=[B_mv2, B_eps], writes=[B_sd2])
            b.emit("vector", lambda e: e.reciprocal(rs2[:], sd2[:]), reads=[B_sd2], writes=[B_rs2])
            b.emit("vector", lambda e: e.tensor_scalar(nm2[:], mv2[:, 0:1], rs2[:], -1.0,
                                                       op0=OP.mult, op1=OP.mult),
                   reads=[B_mv2, B_rs2], writes=[B_nm2])
            b.emit("scalar", lambda e: e.activation(l2h_tm[:], xt2[:], AF.Identity,
                                                    bias=nm2[:], scale=rs2[:]),
                   reads=[B_xt2, B_nm2, B_rs2], writes=[B_h2])
            b.emit("vector", lambda e: e.scalar_tensor_tensor(
                l2l_tm[:], xt2[:], rs2[:], l2h_tm[:].bitcast(F32), op0=OP.mult, op1=OP.subtract),
                reads=[B_xt2, B_rs2, B_h2], writes=[B_l2])
            b.emit("vector", lambda e: e.tensor_scalar(l2l_tm[:], l2l_tm[:], nm2[:], None,
                                                       op0=OP.add),
                   reads=[B_nm2], writes=[B_l2])
            for (src, B_src, dst, B_dst) in ((l2h_tm, B_h2, ln2h_t, B_l2h),
                                             (l2l_tm, B_l2, ln2l_t, B_l2l)):
                for g in range(8):
                    pb, B_pb2 = ps_tr3[g % 2], B_pt3[g % 2]

                    def tr4l(e, src=src, g=g, pb=pb):
                        ii = None
                        for k in range(4):
                            c = g * 4 + k
                            ii = e.transpose(pb[:, k * 128:(k + 1) * 128].bitcast(F32R),
                                             src[:, c * 128:(c + 1) * 128], ident_r[:])
                        return ii
                    b.emit("tensor", tr4l, reads=[B_src, B_idr], writes=[B_pb2])
                    b.emit("scalar", lambda e, dst=dst, g=g, pb=pb: e.copy(
                        dst[:, g * 512:(g + 1) * 512], pb[:].bitcast(F32R)),
                        reads=[B_pb2], writes=[B_dst], accum=(g > 0))
            b.emit("gpsimd", lambda e, ti=ti: e.dma_start(
                out=lnh_d[:, ti * 128:(ti + 1) * 128].rearrange("(c p) t -> p c t", p=128),
                in_=ln2h_t[:].rearrange("p (c t) -> p c t", t=128)),
                reads=[B_l2h], writes=[dB["lnh_d"]], sem="gd", inc_n=16, accum=(ti > 0))
            b.emit("gpsimd", lambda e, ti=ti: e.dma_start(
                out=lnl_d[:, ti * 128:(ti + 1) * 128].rearrange("(c p) t -> p c t", p=128),
                in_=ln2l_t[:].rearrange("p (c t) -> p c t", t=128)),
                reads=[B_l2l], writes=[dB["lnl_d"]], sem="gd", inc_n=16, accum=(ti > 0))

            def dotsmm(e):
                ii = None
                for c in range(DC):
                    first = (c == 0)
                    last = (c == DC - 1)
                    e.matmul(ps_dt[:], ln2h_t[:, c * 128:(c + 1) * 128],
                             cent_h[:, c * NCL:(c + 1) * NCL], start=first, stop=False)
                    e.matmul(ps_dt[:], ln2h_t[:, c * 128:(c + 1) * 128],
                             cent_l[:, c * NCL:(c + 1) * NCL], start=False, stop=False)
                    ii = e.matmul(ps_dt[:], ln2l_t[:, c * 128:(c + 1) * 128],
                                  cent_h[:, c * NCL:(c + 1) * NCL], start=False, stop=last)
                return ii
            b.emit("tensor", dotsmm, reads=[B_l2h, B_l2l, B_cth, B_ctl], writes=[B_pdt])
            b.emit("vector", lambda e: e.tensor_reduce(dmx[:], ps_dt[:], axis=AX.X, op=OP.max),
                   reads=[B_pdt], writes=[B_dmx])
            b.emit("vector", lambda e: e.tensor_scalar(dnb[:], dmx[:], -10.0, None, op0=OP.mult),
                   reads=[B_dmx], writes=[B_dnb])
            b.emit("scalar", lambda e: e.activation(dex[:], ps_dt[:], AF.Exp,
                                                    bias=dnb[:], scale=10.0, accum_out=dzs[:]),
                   reads=[B_pdt, B_dnb], writes=[B_dex, B_dzs])
            b.emit("vector", lambda e: e.tensor_scalar(dth[:], dzs[:], 0.01, None, op0=OP.mult),
                   reads=[B_dzs], writes=[B_dth])
            b.emit("vector", lambda e: e.tensor_scalar(hits[:], dex[:], dth[:], None, op0=OP.is_gt),
                   reads=[B_dex, B_dth], writes=[B_hit])
            b.emit("vector", lambda e, ti=ti: e.tensor_reduce(
                qsel_sb[:, ti:ti + 1], hits[:].bitcast(F32), axis=AX.X, op=OP.max),
                reads=[B_hit], writes=[B_qs], accum=(ti > 0))
            b.emit("tensor", lambda e: e.matmul(ps_ct[:], ones_r[:], hits[:],
                                                start=True, stop=True),
                   reads=[B_hit, B_onr], writes=[B_pct])
            b.emit("vector", lambda e: e.tensor_tensor(cnt_sb[:], cnt_sb[:], ps_ct[:], OP.add),
                   reads=[B_pct, B_cnt], writes=[B_cnt])

        b.emit("gpsimd", lambda e: e.dma_start(out=cnt_in, in_=cnt_sb[:]),
               reads=[B_cnt], writes=[dB["cnt_io"]], sem="gd", inc_n=16)
        b.emit("gpsimd", lambda e: e.collective_compute(
            "AllReduce", OP.add, replica_groups=[[0, 1, 2, 3, 4, 5, 6, 7]],
            ins=[cnt_in.opt()], outs=[cnt_out.opt()]),
            reads=[dB["cnt_io"]], writes=[dB["cnt_io"]], sem="cc")

        if dbg == 3 or until_phase < 4:
            p3.close()
            b.fence()
            for (sap, map_) in dbg_mirrors:
                b.emit("gpsimd", lambda e, s=sap, m=map_: e.dma_start(out=m, in_=s),
                       sem="gd", inc_n=16)
            b.fence()
            b.emit("gpsimd", lambda e: e.engine_nop())
            b.finalize()
            return nc

        p3.close()
        b.fence()
        p4 = es.enter_context(ExitStack())
        bias_b, B_bb = sb("bias_b", [128, D], F32, p4)
        rsel_b, B_rb = sb("rsel_b", [128, D], F32, p4)
        p4a = es.enter_context(ExitStack())
        bias_r, B_br = sb("bias_r", [1, D], F32, p4a)
        asg_s, B_as = sb("asg_s", [1, D], I32, p4a)
        asg_f, B_af = sb("asg_f", [1, D], F32, p4a)
        asg_b, B_ab = sb("asg_b", [64, D], F32, p4a)
        i64, B_i64 = sb("i64", [64, 1], I32, p4a)
        i64f, B_i64f = sb("i64f", [64, 1], F32, p4a)
        onehot, B_oh1 = sb("onehot", [64, D], F32R, p4a)
        cnt_t, B_ct2 = sb("cnt_t", [64, 1], F32, p4a)
        csel, B_cs = sb("csel", [64, 1], F32R, p4a)
        rsel_r, B_rr = sb("rsel_r", [1, D], F32, p4a)
        ps_rs, B_prs = pst("ps_rs", [1, TH], p4a)
        ps_bc4, B_pbc4 = pst("ps_bc4", [128, TH], p4a)

        def bcast_mm(dst, B_dst, row_t, B_row, nparts, ncols):
            for q0 in range(0, ncols, TH):
                b.emit("tensor", lambda e, q0=q0: e.matmul(
                    ps_bc4[0:nparts, 0:TH], onesc[:, 0:nparts], row_t[:, q0:q0 + TH],
                    start=True, stop=True),
                    reads=[B_row, B_onc], writes=[B_pbc4])
                b.emit("vector", lambda e, q0=q0: e.tensor_copy(
                    dst[0:nparts, q0:q0 + TH], ps_bc4[0:nparts, 0:TH]),
                    reads=[B_pbc4], writes=[B_dst], accum=(q0 > 0))

        b.emit("sync", lambda e: e.dma_start(out=bias_r[:], in_=bias_in),
               writes=[B_br], sem="db1", inc_n=16)
        b.emit("sync", lambda e: e.dma_start(out=asg_s[:], in_=asg_in),
               writes=[B_as], sem="db2", inc_n=16)
        bcast_mm(bias_b, B_bb, bias_r, B_br, 128, D)
        b.emit("vector", lambda e: e.tensor_copy(asg_f[:], asg_s[:]), reads=[B_as], writes=[B_af])
        bcast_mm(asg_b, B_ab, asg_f, B_af, 64, D)
        b.emit("gpsimd", lambda e: e.iota(i64[:], pattern=[[0, 1]], base=0, channel_multiplier=1),
               writes=[B_i64])
        b.emit("vector", lambda e: e.tensor_copy(i64f[:], i64[:]), reads=[B_i64], writes=[B_i64f])
        b.emit("vector", lambda e: e.tensor_scalar(onehot[:], asg_b[:], i64f[:], None,
                                                   op0=OP.is_equal),
               reads=[B_ab, B_i64f], writes=[B_oh1])
        b.emit("sync", lambda e: e.dma_start(out=cnt_t[:], in_=cnt_out.rearrange("a b -> b a")),
               reads=[dB["cnt_io"]], writes=[B_ct2], sem="db3", inc_n=16)
        b.emit("vector", lambda e: e.tensor_scalar(csel[:], cnt_t[:], 0.5, None, op0=OP.is_gt),
               reads=[B_ct2], writes=[B_cs])
        for q in range(8):
            b.emit("tensor", lambda e, q=q: e.matmul(
                ps_rs[:], csel[:], onehot[:, q * TH:(q + 1) * TH], start=True, stop=True),
                reads=[B_cs, B_oh1], writes=[B_prs])
            b.emit("vector", lambda e, q=q: e.tensor_copy(rsel_r[:, q * TH:(q + 1) * TH], ps_rs[:]),
                   reads=[B_prs], writes=[B_rr], accum=(q > 0))
        bcast_mm(rsel_b, B_rb, rsel_r, B_rr, 128, D)

        p4a.close()
        b.fence()
        l1h, B_1h = sb("l1h", [128, DC * TH], F32R, p4)
        l1l, B_1l = sb("l1l", [128, DC * TH], F32R, p4)
        wfq4, whq4, wlq4, B_wf4, B_wh4, B_wl4 = [], [], [], [], [], []
        wfq40, B_wf40 = sb("wfq40", [128, 1024], F32, p4)
        wfq4, B_wf4 = [wfq40, wfq40], [B_wf40, B_wf40]
        for i in range(2):
            t2, b2 = sb(f"whq4{i}", [128, 1024], F32R, p4)
            t3, b3 = sb(f"wlq4{i}", [128, 1024], F32R, p4)
            whq4.append(t2); wlq4.append(t3)
            B_wh4.append(b2); B_wl4.append(b3)
        ff_t, B_fft = sb("ff_t", [128, TH], F32, p4)
        x1r_t, B_x1r = sb("x1r_t", [128, TH], F32, p4)
        out_t, B_out4 = sb("out_t", [128, TH], F32, p4)
        B_outs = [Buf() for _ in range(4)]
        ps_pr4, B_pp4 = zip(*[pst(f"ps_pr4{i}", [128, TH], p4) for i in range(2)])
        ps_tr4, B_pt4 = zip(*[pst(f"ps_tr4{i}", [128, TH], p4) for i in range(2)])

        for half in range(2):
            b.emit("sync", lambda e, h=half: e.dma_start(
                out=l1h[:].rearrange("p (c t) -> p c t", t=TH),
                in_=lnh_d[:, h * TH:(h + 1) * TH].rearrange("(c p) t -> p c t", p=128)),
                reads=[dB["lnh_d"]], writes=[B_1h], sem="dl", inc_n=16)
            b.emit("sync", lambda e, h=half: e.dma_start(
                out=l1l[:].rearrange("p (c t) -> p c t", t=TH),
                in_=lnl_d[:, h * TH:(h + 1) * TH].rearrange("(c p) t -> p c t", p=128)),
                reads=[dB["lnl_d"]], writes=[B_1l], sem="dl", inc_n=16)
            for eb in range(EB):
                psum, B_psum = ps_pr4[eb % 2], B_pp4[eb % 2]
                for q in range(NQ):
                    i = q % 2
                    b.emit("sync", lambda e, eb=eb, q=q, i=i: e.dma_start(
                        out=wfq4[i][:], in_=w_hbm["wf_b"][eb, q]),
                        writes=[B_wf4[i]], sem="dw", inc_n=16)
                    b.emit("scalar", lambda e, i=i: e.copy(whq4[i][:], wfq4[i][:]),
                           reads=[B_wf4[i]], writes=[B_wh4[i]])
                    b.emit("vector", lambda e, i=i: e.tensor_tensor(
                        wlq4[i][:], wfq4[i][:], whq4[i][:].bitcast(F32), OP.subtract),
                        reads=[B_wf4[i], B_wh4[i]], writes=[B_wl4[i]])

                    def mms4(e, q=q, i=i, psum=psum):
                        ii = None
                        for cl in range(8):
                            c = q * 8 + cl
                            first = (q == 0 and cl == 0)
                            last = (q == NQ - 1 and cl == 7)
                            wh_ap = whq4[i][:, cl * 128:(cl + 1) * 128]
                            wl_ap = wlq4[i][:, cl * 128:(cl + 1) * 128]
                            xh_ap = l1h[:, c * TH:(c + 1) * TH]
                            xl_ap = l1l[:, c * TH:(c + 1) * TH]
                            e.matmul(psum[:], wh_ap, xh_ap, start=first, stop=False)
                            e.matmul(psum[:], wh_ap, xl_ap, start=False, stop=False)
                            ii = e.matmul(psum[:], wl_ap, xh_ap, start=False, stop=last)
                        return ii
                    b.emit("tensor", mms4, reads=[B_wh4[i], B_wl4[i], B_1h, B_1l],
                           writes=[B_psum], accum=(q > 0))
                b.emit("scalar", lambda e, p=psum: e.copy(ff_t[:], p[:]),
                       reads=[B_psum], writes=[B_fft])
                b.emit("sync", lambda e, eb=eb, h=half: e.dma_start(
                    out=x1r_t[:].rearrange("p (s j) -> p s j", j=128),
                    in_=x1_d[h * TH:(h + 1) * TH, eb * 128:(eb + 1) * 128]
                    .rearrange("(s p) j -> p s j", p=128)),
                    reads=[dB["x1_d"]], writes=[B_x1r], sem="dx4", inc_n=16)
                pb, B_pb4 = ps_tr4[eb % 2], B_pt4[eb % 2]

                def tr4f(e, pb=pb):
                    ii = None
                    for j in range(4):
                        ii = e.transpose(pb[:, j * 128:(j + 1) * 128].bitcast(F32),
                                         ff_t[:, j * 128:(j + 1) * 128], ident_f[:])
                    return ii
                b.emit("tensor", tr4f, reads=[B_fft, B_idf], writes=[B_pb4])
                for j in range(4):
                    b.emit("vector", lambda e, j=j, pb=pb, eb=eb: e.tensor_tensor(
                        out_t[:, j * 128:(j + 1) * 128], pb[:, j * 128:(j + 1) * 128],
                        bias_b[:, eb * 128:(eb + 1) * 128], OP.add),
                        reads=[B_pb4, B_bb], writes=[B_outs[j]])
                    b.emit("vector", lambda e, j=j, eb=eb: e.tensor_tensor(
                        out_t[:, j * 128:(j + 1) * 128], out_t[:, j * 128:(j + 1) * 128],
                        rsel_b[:, eb * 128:(eb + 1) * 128], OP.mult),
                        reads=[B_rb, B_outs[j]], writes=[B_outs[j]])
                    b.emit("vector", lambda e, j=j, h=half: e.scalar_tensor_tensor(
                        out_t[:, j * 128:(j + 1) * 128], out_t[:, j * 128:(j + 1) * 128],
                        qsel_sb[:, h * 4 + j:h * 4 + j + 1], x1r_t[:, j * 128:(j + 1) * 128],
                        op0=OP.mult, op1=OP.add),
                        reads=[B_qs, B_x1r, B_outs[j]], writes=[B_outs[j]])
                b.emit("gpsimd", lambda e, eb=eb, h=half: e.dma_start(
                    out=y_s[h * TH:(h + 1) * TH, eb * 128:(eb + 1) * 128]
                    .rearrange("(s p) j -> p s j", p=128),
                    in_=out_t[:].rearrange("p (s j) -> p s j", j=128)),
                    reads=B_outs, writes=[dB["y"]], sem="gd", inc_n=16,
                    accum=not (half == 0 and eb == 0))

        p4.close()
        b.fence()
        for (sap, map_) in dbg_mirrors:
            b.emit("gpsimd", lambda e, s=sap, m=map_: e.dma_start(out=m, in_=s),
                   sem="gd", inc_n=16)
        if dbg_mirrors:
            b.fence()
        b.emit("gpsimd", lambda e: e.engine_nop())
        b.finalize()
    return nc


_NC_CACHE = {}


def _get_program():
    if "nc" not in _NC_CACHE:
        _NC_CACHE["nc"] = build_program()
    return _NC_CACHE["nc"]


def _block_weights(w):
    wt = np.ascontiguousarray(w.T)
    v = wt.reshape(NQ, 8, 128, EB, 128)
    v = v.transpose(3, 0, 2, 1, 4)
    return np.ascontiguousarray(v.reshape(EB, NQ, 128, 1024))


def kernel(L0_attn_x, L0_attn_w_q, L0_attn_w_k, L0_attn_w_v, L0_attn_w_o,
           L0_ffn_weight, L0_ffn_bias, L0_ffn_centroids, L0_ffn_assignments):
    x = np.ascontiguousarray(np.asarray(L0_attn_x, dtype=np.float32))
    wq = _block_weights(np.asarray(L0_attn_w_q, dtype=np.float32))
    wk = _block_weights(np.asarray(L0_attn_w_k, dtype=np.float32))
    wv = _block_weights(np.asarray(L0_attn_w_v, dtype=np.float32))
    wo = _block_weights(np.asarray(L0_attn_w_o, dtype=np.float32))
    wf = _block_weights(np.asarray(L0_ffn_weight, dtype=np.float32))
    ct = np.ascontiguousarray(np.asarray(L0_ffn_centroids, dtype=np.float32).T)
    cb = np.ascontiguousarray(ct.reshape(DC, 128, NCL).transpose(1, 0, 2)
                              .reshape(128, DC * NCL))
    bias = np.ascontiguousarray(np.asarray(L0_ffn_bias, dtype=np.float32).reshape(1, D))
    asg = np.ascontiguousarray(np.asarray(L0_ffn_assignments, dtype=np.int32).reshape(1, D))

    xf = x.reshape(2 * S, D)
    nc = _get_program()
    in_maps = []
    for c in range(NCORES):
        in_maps.append({
            "x_s": np.ascontiguousarray(xf[c * T:(c + 1) * T, :]),
            "wq_b": wq, "wk_b": wk, "wv_b": wv, "wo_b": wo, "wf_b": wf,
            "cent_b": cb, "bias_in": bias, "asg_in": asg,
        })
    res = run_bass_kernel_spmd(nc, in_maps, core_ids=list(range(NCORES)))
    out = np.concatenate([res.results[c]["y_s"] for c in range(NCORES)], axis=0)
    return out.reshape(2, S, D).astype(np.float32)


# version 04
# speedup vs baseline: 1.2245x; 1.0564x over previous
import numpy as np
from contextlib import ExitStack

import concourse.bass as bass
import concourse.mybir as mybir
from concourse.bass_utils import run_bass_kernel_spmd
from concourse.alu_op_type import AluOpType as OP
from concourse import library_config

F32 = mybir.dt.float32
F32R = mybir.dt.float32r
I32 = mybir.dt.int32
AF = mybir.ActivationFunctionType
AX = mybir.AxisListType

NCORES = 8
D = 4096
T = 1024
TH = 512
S = 4096
H = 32
DC = 32
EB = 32
NQ = 4
NCL = 64
INV_SQRT_DK = float(1.0 / np.sqrt(np.float32(128.0)))
ENGS = ("sync", "scalar", "vector", "tensor", "gpsimd")
ENG_SEM = {"sync": "dh", "scalar": "sc", "vector": "dv", "tensor": "pe",
           "gpsimd": "gp"}


class Buf:

    __slots__ = ("w", "rs")

    def __init__(self):
        self.w = []
        self.rs = []


def _prune(toks):
    best = {}
    for (s, v) in toks:
        if best.get(s, -1) < v:
            best[s] = v
    return [(s, v) for s, v in best.items()]


class B:
    def __init__(self, nc, es):
        self.nc = nc
        self.es = es
        self.ops = {e: [] for e in ENGS}
        self.sems = {}
        self.fence_waits = []

    def _sem(self, name):
        if name not in self.sems:
            h = self.es.enter_context(self.nc.semaphore(name))
            self.sems[name] = [h, 0]
        return self.sems[name]

    def emit(self, eng, fn, reads=(), writes=(), extra=(), sem=None,
             inc_n=1, accum=False):
        sem = sem or ENG_SEM[eng]
        rec = self._sem(sem)
        waits = list(extra) + list(self.fence_waits)
        for bf in reads:
            waits += bf.w
        for bf in writes:
            if not accum:
                waits += bf.w + bf.rs
        rec[1] += inc_n
        tok = (sem, rec[1])
        for bf in reads:
            bf.rs = _prune(bf.rs + [tok])
        for bf in writes:
            if accum:
                bf.w = _prune(bf.w + [tok])
            else:
                bf.w = [tok]
                bf.rs = []
        self.ops[eng].append((fn, _prune(waits), (sem, inc_n)))
        return tok

    def fence(self):
        self.fence_waits = _prune(
            [(n, rec[1]) for n, rec in self.sems.items() if rec[1] > 0])

    def finalize(self):
        with self.nc.Block() as block:
            for eng in ENGS:
                oplist = self.ops[eng]
                if not oplist:
                    continue

                def mk(oplist):
                    def body(e):
                        seen = {}
                        for fn, waits, inc in oplist:
                            for (sn, v) in waits:
                                if seen.get(sn, -1) >= v:
                                    continue
                                e.wait_ge(self.sems[sn][0], v)
                                seen[sn] = v
                            instr = fn(e)
                            if instr is not None:
                                instr.then_inc(self.sems[inc[0]][0], inc[1])
                    return body

                getattr(block, eng)(mk(oplist))


def build_program(until_phase=99, dbg=None, dbg_out=()):
    nc = bass.Bass("TRN2", target_bir_lowering=False)

    dbg_mirrors = []

    def scratch(name, shape, dt):
        ap = nc.dram_tensor(name, shape, dt).ap()
        if name in dbg_out:
            m = nc.dram_tensor(name + "_dbg", shape, dt, kind="ExternalOutput").ap()
            dbg_mirrors.append((ap, m))
        return ap

    x_s = nc.dram_tensor("x_s", [T, D], F32, kind="ExternalInput").ap()
    w_hbm = {}
    for nm in ("wq_b", "wk_b", "wv_b", "wo_b", "wf_b"):
        w_hbm[nm] = nc.dram_tensor(nm, [EB, NQ, 128, 1024], F32,
                                   kind="ExternalInput").ap()
    cent_in = nc.dram_tensor("cent_b", [128, DC * NCL], F32, kind="ExternalInput").ap()
    bias_in = nc.dram_tensor("bias_in", [1, D], F32, kind="ExternalInput").ap()
    asg_in = nc.dram_tensor("asg_in", [1, D], I32, kind="ExternalInput").ap()
    y_s = nc.dram_tensor("y_s", [T, D], F32, kind="ExternalOutput").ap()

    kh_sh = scratch("kh_sh", [D, T], F32R)
    kl_sh = scratch("kl_sh", [D, T], F32R)
    v_sh = scratch("v_sh", [T, D], F32)
    kh_g = scratch("kh_g", [4 * D, T], F32R)
    kl_g = scratch("kl_g", [4 * D, T], F32R)
    v_g = scratch("v_g", [S, D], F32)
    qh_d = scratch("qh_d", [D, T], F32R)
    ql_d = scratch("ql_d", [D, T], F32R)
    oh_d = scratch("oh_d", [D, T], F32R)
    ol_d = scratch("ol_d", [D, T], F32R)
    x1_d = scratch("x1_d", [T, D], F32)
    lnh_d = scratch("lnh_d", [D, T], F32R)
    lnl_d = scratch("lnl_d", [D, T], F32R)
    kn_in = scratch("kn_in", [1, H], F32)
    kn_out = scratch("kn_out", [1, H], F32)
    cnt_in = scratch("cnt_in", [1, NCL], F32)
    cnt_out = scratch("cnt_out", [1, NCL], F32)

    dB = {nm: Buf() for nm in ("kh_sh", "kl_sh", "v_sh", "kh_g", "kl_g", "v_g",
                               "qh_d", "ql_d", "oh_d", "ol_d", "x1_d",
                               "lnh_d", "lnl_d", "kn_io", "cnt_io", "y")}

    es = ExitStack()
    b = B(nc, es)

    def sb(name, shape, dt=F32, st=None):
        stk = st or es
        t = stk.enter_context(nc.sbuf_tensor(name, shape, dt))
        return t, Buf()

    def pst(name, shape, st=None, dt=F32):
        stk = st or es
        t = stk.enter_context(nc.psum_tensor(name, shape, dt))
        return t, Buf()

    with es:
        ident_r, B_idr = sb("ident_r", [128, 128], F32R)
        ident_f, B_idf = sb("ident_f", [128, 128], F32)
        ones_r, B_onr = sb("ones_r", [128, 1], F32R)
        ones_f, B_onf = sb("ones_f", [128, 1], F32)
        onesc, B_onc = sb("onesc", [1, 128], F32)
        eps_t, B_eps = sb("eps_t", [128, 1], F32)
        qn_sb, B_qn = sb("qn_sb", [1, H], F32)
        kn_sb, B_kn = sb("kn_sb", [1, H], F32)
        qsel_sb, B_qs = sb("qsel_sb", [128, 8], F32)
        cnt_sb, B_cnt = sb("cnt_sb", [1, NCL], F32)

        p0 = es.enter_context(ExitStack())
        i128, B_i1 = sb("i128", [128, 1], I32, p0)
        irow, B_i2 = sb("irow", [1, 128], I32, p0)
        i128f, B_i1f = sb("i128f", [128, 1], F32, p0)
        irowf, B_i2f = sb("irowf", [1, 128], F32, p0)
        irowb, B_i2b = sb("irowb", [128, 128], F32, p0)

        b.emit("gpsimd", lambda e: e.iota(i128[:], pattern=[[0, 1]], base=0,
                                          channel_multiplier=1), writes=[B_i1])
        b.emit("gpsimd", lambda e: e.iota(irow[:], pattern=[[1, 128]], base=0,
                                          channel_multiplier=0), writes=[B_i2])
        b.emit("vector", lambda e: e.tensor_copy(i128f[:], i128[:]),
               reads=[B_i1], writes=[B_i1f])
        b.emit("vector", lambda e: e.tensor_copy(irowf[:], irow[:]),
               reads=[B_i2], writes=[B_i2f])
        ps_bc0, B_pbc0 = pst("ps_bc0", [128, 128], p0)
        b.emit("vector", lambda e: e.memset(onesc[:], 1.0), writes=[B_onc])
        b.emit("tensor", lambda e: e.matmul(ps_bc0[:], onesc[:], irowf[:],
                                            start=True, stop=True),
               reads=[B_i2f, B_onc], writes=[B_pbc0])
        b.emit("vector", lambda e: e.tensor_copy(irowb[:], ps_bc0[:]),
               reads=[B_pbc0], writes=[B_i2b])
        b.emit("vector", lambda e: e.tensor_scalar(ident_f[:], irowb[:], i128f[:], None,
                                                   op0=OP.is_equal),
               reads=[B_i2b, B_i1f], writes=[B_idf])
        b.emit("vector", lambda e: e.tensor_copy(ident_r[:], ident_f[:]),
               reads=[B_idf], writes=[B_idr])
        b.emit("vector", lambda e: e.memset(ones_f[:], 1.0), writes=[B_onf])
        b.emit("scalar", lambda e: e.copy(ones_r[:], ones_f[:]), reads=[B_onf], writes=[B_onr])
        b.emit("vector", lambda e: e.memset(eps_t[:], 1e-5), writes=[B_eps])
        b.emit("vector", lambda e: e.memset(qn_sb[:], 0.0), writes=[B_qn])
        b.emit("vector", lambda e: e.memset(kn_sb[:], 0.0), writes=[B_kn])
        b.emit("vector", lambda e: e.memset(cnt_sb[:], 0.0), writes=[B_cnt])

        p0.close()
        b.fence()

        p1 = es.enter_context(ExitStack())
        lnxh, B_lh = sb("lnxh", [128, DC * TH], F32R, p1)
        lnxl, B_ll = sb("lnxl", [128, DC * TH], F32R, p1)
        xt_t, B_xt = sb("xt_t", [128, D], F32, p1)
        lnh_tm, B_htm = sb("lnh_tm", [128, D], F32R, p1)
        lnl_tm, B_ltm = sb("lnl_tm", [128, D], F32R, p1)
        stats, B_st = sb("stats", [128, 48], F32, p1)
        mv_t, B_mv = sb("mv_t", [128, 2], F32, p1)
        std_t, B_sd = sb("std_t", [128, 1], F32, p1)
        rstd_t, B_rs = sb("rstd_t", [128, 1], F32, p1)
        nmr_t, B_nm = sb("nmr_t", [128, 1], F32, p1)
        whq, wlq, B_wh, B_wl = [], [], [], []
        wfq0, B_wf0 = sb("wfq0", [128, 1024], F32, p1)
        wfq, B_wf = [wfq0, wfq0], [B_wf0, B_wf0]
        for i in range(2):
            t2, b2 = sb(f"whq{i}", [128, 1024], F32R, p1)
            t3, b3 = sb(f"wlq{i}", [128, 1024], F32R, p1)
            whq.append(t2); wlq.append(t3)
            B_wh.append(b2); B_wl.append(b3)
        evh, B_evh = sb("evh", [128, TH], F32R, p1)
        evl, B_evl = sb("evl", [128, TH], F32R, p1)
        evn, B_evn = sb("evn", [128, TH], F32, p1)
        evf, B_evf = evh, B_evh
        sqn, B_sqn = evn, B_evn
        nrm2, B_nr2 = sb("nrm2", [1, 1], F32, p1)
        ps_pr, B_pp = zip(*[pst(f"ps_pr{i}", [128, TH], p1) for i in range(2)])
        ps_tr, B_pt = zip(*[pst(f"ps_tr{i}", [128, TH], p1) for i in range(2)])
        ps_nm, B_pn = pst("ps_nm", [1, TH], p1)

        def ln_into(src_rows, half, dsth, dstl, B_dh, B_dl, chunk_cols, extra_w=()):
            for tt in range(4):
                r0 = half * TH + tt * 128
                b.emit("sync", lambda e, r0=r0: e.dma_start(out=xt_t[:], in_=src_rows[r0:r0 + 128, :]),
                       writes=[B_xt], extra=extra_w, sem="dx", inc_n=16)

                def bnall(e):
                    ii = None
                    for i in range(8):
                        ii = e.bn_stats(stats[:, i * 6:(i + 1) * 6], xt_t[:, i * 512:(i + 1) * 512])
                    return ii
                b.emit("vector", bnall, reads=[B_xt], writes=[B_st])
                b.emit("vector", lambda e: e.bn_aggr(mv_t[:], stats[:]), reads=[B_st], writes=[B_mv])
                b.emit("scalar", lambda e: e.activation(std_t[:], mv_t[:, 1:2], AF.Sqrt,
                                                        bias=eps_t[:], scale=1.0),
                       reads=[B_mv, B_eps], writes=[B_sd])
                b.emit("vector", lambda e: e.reciprocal(rstd_t[:], std_t[:]),
                       reads=[B_sd], writes=[B_rs])
                b.emit("vector", lambda e: e.tensor_scalar(nmr_t[:], mv_t[:, 0:1], rstd_t[:], -1.0,
                                                           op0=OP.mult, op1=OP.mult),
                       reads=[B_mv, B_rs], writes=[B_nm])
                b.emit("scalar", lambda e: e.activation(lnh_tm[:], xt_t[:], AF.Identity,
                                                        bias=nmr_t[:], scale=rstd_t[:]),
                       reads=[B_xt, B_nm, B_rs], writes=[B_htm])
                b.emit("vector", lambda e: e.scalar_tensor_tensor(
                    lnl_tm[:], xt_t[:], rstd_t[:], lnh_tm[:].bitcast(F32),
                    op0=OP.mult, op1=OP.subtract),
                    reads=[B_xt, B_rs, B_htm], writes=[B_ltm])
                b.emit("vector", lambda e: e.tensor_scalar(lnl_tm[:], lnl_tm[:], nmr_t[:], None,
                                                           op0=OP.add),
                       reads=[B_nm], writes=[B_ltm])
                for (src, B_src, dst, B_dst) in ((lnh_tm, B_htm, dsth, B_dh),
                                                 (lnl_tm, B_ltm, dstl, B_dl)):
                    for g in range(8):
                        pb, B_pb = ps_tr[g % 2], B_pt[g % 2]

                        def tr4(e, src=src, g=g, pb=pb):
                            ii = None
                            for k in range(4):
                                c = g * 4 + k
                                ii = e.transpose(pb[:, k * 128:(k + 1) * 128].bitcast(F32R),
                                                 src[:, c * 128:(c + 1) * 128], ident_r[:])
                            return ii
                        b.emit("tensor", tr4, reads=[B_src, B_idr], writes=[B_pb])

                        def ev4(e, dst=dst, g=g, tt=tt, pb=pb, cc=chunk_cols):
                            o = dst[:, 0:0]
                            ap_out = dst[:].rearrange("p (c t) -> p c t", t=cc)[
                                :, g * 4:(g + 1) * 4, tt * 128:tt * 128 + 128]
                            ap_in = pb[:].bitcast(F32R).rearrange("p (k t) -> p k t", t=128)
                            return e.copy(ap_out, ap_in)
                        b.emit("scalar", ev4, reads=[B_pb], writes=[B_dst], accum=(tt + g > 0))

        def stream_quarter(wap, eb, q, i, dma_sem="dw"):
            b.emit("sync", lambda e, eb=eb, q=q, i=i: e.dma_start(out=wfq[i][:], in_=wap[eb, q]),
                   reads=[], writes=[B_wf[i]], sem=dma_sem, inc_n=16)
            b.emit("scalar", lambda e, i=i: e.copy(whq[i][:], wfq[i][:]),
                   reads=[B_wf[i]], writes=[B_wh[i]])
            b.emit("vector", lambda e, i=i: e.tensor_tensor(wlq[i][:], wfq[i][:],
                                                            whq[i][:].bitcast(F32), OP.subtract),
                   reads=[B_wf[i], B_wh[i]], writes=[B_wl[i]])

        def proj_block(wap, eb, xh, xl, B_xh, B_xl, psum, B_psum):
            for q in range(NQ):
                i = q % 2
                stream_quarter(wap, eb, q, i)

                def mms(e, q=q, i=i):
                    ii = None
                    for cl in range(8):
                        c = q * 8 + cl
                        first = (q == 0 and cl == 0)
                        last = (q == NQ - 1 and cl == 7)
                        wh_ap = whq[i][:, cl * 128:(cl + 1) * 128]
                        wl_ap = wlq[i][:, cl * 128:(cl + 1) * 128]
                        xh_ap = xh[:, c * TH:(c + 1) * TH]
                        xl_ap = xl[:, c * TH:(c + 1) * TH]
                        e.matmul(psum[:], wh_ap, xh_ap, start=first, stop=False)
                        e.matmul(psum[:], wh_ap, xl_ap, start=False, stop=False)
                        ii = e.matmul(psum[:], wl_ap, xh_ap, start=False, stop=last)
                    return ii
                b.emit("tensor", mms, reads=[B_wh[i], B_wl[i], B_xh, B_xl],
                       writes=[B_psum], accum=(q > 0))

        def evac_split_store(psum, B_psum, dsth_ap, dstl_ap, B_dsth, B_dstl,
                             norm_buf, norm_col, first_hl):
            b.emit("scalar", lambda e: e.copy(evh[:], psum[:]), reads=[B_psum], writes=[B_evh])
            b.emit("vector", lambda e: e.tensor_tensor(evl[:], psum[:],
                                                       evh[:].bitcast(F32), OP.subtract),
                   reads=[B_psum, B_evh], writes=[B_evl])
            b.emit("gpsimd", lambda e: e.dma_start(out=dsth_ap, in_=evh[:]),
                   reads=[B_evh], writes=[B_dsth], sem="gd", inc_n=16, accum=not first_hl)
            b.emit("gpsimd", lambda e: e.dma_start(out=dstl_ap, in_=evl[:]),
                   reads=[B_evl], writes=[B_dstl], sem="gd", inc_n=16, accum=not first_hl)
            b.emit("scalar", lambda e: e.activation(sqn[:], evh[:].bitcast(F32), AF.Square,
                                                    scale=1.0),
                   reads=[B_evh], writes=[B_sqn])
            b.emit("tensor", lambda e: e.matmul(ps_nm[:], ones_r[:], sqn[:], start=True, stop=True),
                   reads=[B_sqn, B_onr], writes=[B_pn])
            b.emit("vector", lambda e: e.tensor_reduce(nrm2[:], ps_nm[:], axis=AX.X, op=OP.max),
                   reads=[B_pn], writes=[B_nr2])
            nb, col = norm_buf
            b.emit("vector", lambda e, col=norm_col: e.tensor_tensor(
                nb[:, col:col + 1], nb[:, col:col + 1], nrm2[:], OP.max),
                reads=[B_nr2], writes=[norm_buf[1]] if False else [])

        def norm_rmw(nb_t, B_nb, col):
            b.emit("vector", lambda e, col=col: e.tensor_tensor(
                nb_t[:, col:col + 1], nb_t[:, col:col + 1], nrm2[:], OP.max),
                reads=[B_nr2, B_nb], writes=[B_nb])

        if until_phase >= 1:
            for half in range(2):
                ln_into(x_s, half, lnxh, lnxl, B_lh, B_ll, TH)
                for wname in ("k", "v", "q"):
                    wap = w_hbm[f"w{wname}_b"]
                    for eb in range(EB):
                        psum, B_psum = ps_pr[eb % 2], B_pp[eb % 2]
                        proj_block(wap, eb, lnxh, lnxl, B_lh, B_ll, psum, B_psum)
                        if wname in ("k", "q"):
                            dh = kh_sh if wname == "k" else qh_d
                            dl = kl_sh if wname == "k" else ql_d
                            B_dh = dB["kh_sh"] if wname == "k" else dB["qh_d"]
                            B_dl = dB["kl_sh"] if wname == "k" else dB["ql_d"]
                            first_hl = (half == 0 and eb == 0)
                            b.emit("scalar", lambda e, p=psum: e.copy(evh[:], p[:]),
                                   reads=[B_psum], writes=[B_evh])
                            b.emit("vector", lambda e, p=psum: e.tensor_tensor(
                                evl[:], p[:], evh[:].bitcast(F32), OP.subtract),
                                reads=[B_psum, B_evh], writes=[B_evl])
                            b.emit("gpsimd", lambda e, d=dh, eb=eb, h=half: e.dma_start(
                                out=d[eb * 128:(eb + 1) * 128, h * TH:(h + 1) * TH], in_=evh[:]),
                                reads=[B_evh], writes=[B_dh], sem="gd", inc_n=16,
                                accum=not first_hl)
                            b.emit("gpsimd", lambda e, d=dl, eb=eb, h=half: e.dma_start(
                                out=d[eb * 128:(eb + 1) * 128, h * TH:(h + 1) * TH], in_=evl[:]),
                                reads=[B_evl], writes=[B_dl], sem="gd", inc_n=16,
                                accum=not first_hl)
                            b.emit("scalar", lambda e: e.activation(sqn[:],
                                                                    evh[:].bitcast(F32),
                                                                    AF.Square, scale=1.0),
                                   reads=[B_evh], writes=[B_sqn])
                            b.emit("tensor", lambda e: e.matmul(ps_nm[:], ones_f[:],
                                                                sqn[:],
                                                                start=True, stop=True),
                                   reads=[B_sqn, B_onf], writes=[B_pn])
                            b.emit("vector", lambda e: e.tensor_reduce(nrm2[:], ps_nm[:],
                                                                       axis=AX.X, op=OP.max),
                                   reads=[B_pn], writes=[B_nr2])
                            if wname == "k":
                                norm_rmw(kn_sb, B_kn, eb)
                            else:
                                norm_rmw(qn_sb, B_qn, eb)
                        else:
                            b.emit("scalar", lambda e, p=psum: e.copy(evf[:].bitcast(F32), p[:]),
                                   reads=[B_psum], writes=[B_evf])
                            pb, B_pb = ps_tr[eb % 2], B_pt[eb % 2]

                            def tr4v(e, pb=pb):
                                ii = None
                                for j in range(4):
                                    ii = e.transpose(pb[:, j * 128:(j + 1) * 128].bitcast(F32),
                                                     evf[:, j * 128:(j + 1) * 128].bitcast(F32),
                                                     ident_f[:])
                                return ii
                            b.emit("tensor", tr4v, reads=[B_evf, B_idf], writes=[B_pb])
                            b.emit("vector", lambda e, pb=pb: e.tensor_copy(evn[:], pb[:]),
                                   reads=[B_pb], writes=[B_evn])
                            b.emit("gpsimd", lambda e, eb=eb, h=half: e.dma_start(
                                out=v_sh[h * TH:(h + 1) * TH, eb * 128:(eb + 1) * 128]
                                .rearrange("(j p) c -> p j c", p=128),
                                in_=evn[:].rearrange("p (j c) -> p j c", c=128)),
                                reads=[B_evn], writes=[dB["v_sh"]], sem="gd", inc_n=16,
                                accum=not (half == 0 and eb == 0))
                    if half == 1 and wname == "k":
                        b.emit("gpsimd", lambda e: e.dma_start(out=kn_in, in_=kn_sb[:]),
                               reads=[B_kn], writes=[dB["kn_io"]], sem="gd", inc_n=16)
                        b.emit("gpsimd", lambda e: e.collective_compute(
                            "AllReduce", OP.max, replica_groups=[[0, 1, 2, 3, 4, 5, 6, 7]],
                            ins=[kn_in.opt()], outs=[kn_out.opt()]),
                            reads=[dB["kn_io"]], writes=[dB["kn_io"]], sem="cc")
                        b.emit("gpsimd", lambda e: e.collective_compute(
                            "AllGather", OP.bypass, replica_groups=[[0, 1, 2, 3], [4, 5, 6, 7]],
                            ins=[kh_sh.opt()], outs=[kh_g.opt()]),
                            reads=[dB["kh_sh"]], writes=[dB["kh_g"]], sem="cc")
                        b.emit("gpsimd", lambda e: e.collective_compute(
                            "AllGather", OP.bypass, replica_groups=[[0, 1, 2, 3], [4, 5, 6, 7]],
                            ins=[kl_sh.opt()], outs=[kl_g.opt()]),
                            reads=[dB["kl_sh"]], writes=[dB["kl_g"]], sem="cc")
                    if half == 1 and wname == "v":
                        b.emit("gpsimd", lambda e: e.collective_compute(
                            "AllGather", OP.bypass, replica_groups=[[0, 1, 2, 3], [4, 5, 6, 7]],
                            ins=[v_sh.opt()], outs=[v_g.opt()]),
                            reads=[dB["v_sh"]], writes=[dB["v_g"]], sem="cc")

        if dbg == 1 or until_phase < 2:
            p1.close()
            b.fence()
            for (sap, map_) in dbg_mirrors:
                b.emit("gpsimd", lambda e, s=sap, m=map_: e.dma_start(out=m, in_=s),
                       sem="gd", inc_n=16)
            b.fence()
            b.emit("gpsimd", lambda e: e.engine_nop())
            b.finalize()
            return nc

        p1.close()
        b.fence()
        p2 = es.enter_context(ExitStack())
        khb, klb, vbb, qhb, qlb = [], [], [], [], []
        B_khb, B_klb, B_vbb, B_qhb, B_qlb = [], [], [], [], []
        for i in range(2):
            t1, b1 = sb(f"khb{i}", [128, S], F32R, p2); khb.append(t1); B_khb.append(b1)
            t2, b2 = sb(f"klb{i}", [128, S], F32R, p2); klb.append(t2); B_klb.append(b2)
            t3, b3 = sb(f"vbb{i}", [128, 32 * 129], F32, p2); vbb.append(t3); B_vbb.append(b3)
            t4, b4 = sb(f"qhb{i}", [128, T], F32R, p2); qhb.append(t4); B_qhb.append(b4)
            t5, b5 = sb(f"qlb{i}", [128, T], F32R, p2); qlb.append(t5); B_qlb.append(b5)
        ptile, B_ptile = zip(*[sb(f"ptile{i}", [128, TH], F32, p2) for i in range(3)])
        otm, B_otm = sb("otm", [128, TH], F32, p2)
        ohh, B_ohh = sb("ohh", [128, TH], F32R, p2)
        ohl, B_ohl = sb("ohl", [128, TH], F32R, p2)
        rz_t, B_rz = sb("rz_t", [128, 4], F32, p2)
        kng, B_kng = sb("kng", [1, H], F32, p2)
        mh_t, B_mh = sb("mh_t", [1, 1], F32, p2)
        mhb, B_mhb = sb("mhb", [128, 1], F32, p2)
        ps_s, B_ps = zip(*[pst(f"ps_s{i}", [128, TH], p2) for i in range(2)])
        ps_av, B_pav = zip(*[pst(f"ps_av{i}", [128, 129], p2) for i in range(4)])
        ps_ot, B_pot = pst("ps_ot", [128, TH], p2)
        ps_bc2, B_pbc2 = pst("ps_bc2", [128, 1], p2)

        for i in range(2):
            b.emit("vector", lambda e, i=i: e.memset(
                vbb[i][:].rearrange("p (c j) -> p c j", j=129)[:, :, 128:129], 1.0),
                writes=[B_vbb[i]])
        b.emit("sync", lambda e: e.dma_start(out=kng[:], in_=kn_out),
               reads=[dB["kn_io"]], writes=[B_kng], sem="dkg", inc_n=16)

        def load_head(h, i):
            for r in range(4):
                b.emit("sync", lambda e, h=h, i=i, r=r: e.dma_start(
                    out=khb[i][:, r * T:(r + 1) * T],
                    in_=kh_g[r * D + h * 128: r * D + (h + 1) * 128, :]),
                    reads=[dB["kh_g"]], writes=[B_khb[i]], sem=f"ka{i}", inc_n=16,
                    accum=(r > 0))
                b.emit("sync", lambda e, h=h, i=i, r=r: e.dma_start(
                    out=klb[i][:, r * T:(r + 1) * T],
                    in_=kl_g[r * D + h * 128: r * D + (h + 1) * 128, :]),
                    reads=[dB["kl_g"]], writes=[B_klb[i]], sem=f"kb{i}", inc_n=16,
                    accum=(r > 0))
            b.emit("sync", lambda e, h=h, i=i: e.dma_start(
                out=qhb[i][:], in_=qh_d[h * 128:(h + 1) * 128, :]),
                reads=[dB["qh_d"]], writes=[B_qhb[i]], sem=f"qa{i}", inc_n=16)
            b.emit("sync", lambda e, h=h, i=i: e.dma_start(
                out=qlb[i][:], in_=ql_d[h * 128:(h + 1) * 128, :]),
                reads=[dB["ql_d"]], writes=[B_qlb[i]], sem=f"qb{i}", inc_n=16)

            def ldv(e, h=h, i=i):
                return e.dma_start(
                    out=vbb[i][:].rearrange("p (c j) -> p c j", j=129)[:, :, 0:128],
                    in_=v_g[:, h * 128:(h + 1) * 128].rearrange("(c p) j -> p c j", p=128))
            b.emit("sync", ldv, reads=[dB["v_g"]], writes=[B_vbb[i]], sem=f"va{i}",
                   inc_n=16)

        def emit_mh(h):
            b.emit("vector", lambda e, h=h: e.tensor_tensor(
                mh_t[:], qn_sb[:, h:h + 1], kng[:, h:h + 1], OP.mult),
                reads=[B_qn, B_kng], writes=[B_mh])
            b.emit("scalar", lambda e: e.activation(mh_t[:], mh_t[:], AF.Sqrt,
                                                    scale=1.0002 / 128.0),
                   reads=[B_mh], writes=[B_mh])
            b.emit("vector", lambda e: e.tensor_scalar(mh_t[:], mh_t[:], -1.0, None, op0=OP.mult),
                   reads=[B_mh], writes=[B_mh])
            b.emit("tensor", lambda e: e.matmul(ps_bc2[:], onesc[:], mh_t[:],
                                                start=True, stop=True),
                   reads=[B_mh, B_onc], writes=[B_pbc2])
            b.emit("vector", lambda e: e.tensor_copy(mhb[:], ps_bc2[:]),
                   reads=[B_pbc2], writes=[B_mhb])

        def emit_smm(i, qi, c):
            sps, B_sps = ps_s[c % 2], B_ps[c % 2]

            def smm(e, i=i, c=c, qi=qi, sps=sps):
                e.matmul(sps[:], khb[i][:, c * 128:(c + 1) * 128],
                         qhb[i][:, qi * TH:(qi + 1) * TH], start=True, stop=False)
                e.matmul(sps[:], khb[i][:, c * 128:(c + 1) * 128],
                         qlb[i][:, qi * TH:(qi + 1) * TH], start=False, stop=False)
                return e.matmul(sps[:], klb[i][:, c * 128:(c + 1) * 128],
                                qhb[i][:, qi * TH:(qi + 1) * TH], start=False, stop=True)
            b.emit("tensor", smm, reads=[B_khb[i], B_klb[i], B_qhb[i], B_qlb[i]],
                   writes=[B_sps])

        def emit_exp(c):
            sps, B_sps = ps_s[c % 2], B_ps[c % 2]
            pt, B_pt2 = ptile[c % 3], B_ptile[c % 3]
            b.emit("scalar", lambda e, sps=sps, pt=pt: e.activation(
                pt[:], sps[:], AF.Exp, bias=mhb[:], scale=INV_SQRT_DK),
                reads=[B_sps, B_mhb], writes=[B_pt2])

        def emit_avm(i, c):
            pt, B_pt2 = ptile[c % 3], B_ptile[c % 3]

            def avm(e, i=i, c=c, pt=pt):
                ii = None
                for j in range(4):
                    ii = e.matmul(ps_av[j][:], pt[:, j * 128:(j + 1) * 128],
                                  vbb[i][:, c * 129:(c + 1) * 129],
                                  start=(c == 0), stop=(c == 31))
                return ii
            b.emit("tensor", avm, reads=[B_pt2, B_vbb[i]],
                   writes=list(B_pav), accum=(c > 0))

        def emit_evac(h, qi):
            for j in range(4):
                b.emit("vector", lambda e, j=j: e.reciprocal(rz_t[:, j:j + 1],
                                                             ps_av[j][:, 128:129]),
                       reads=[B_pav[j]], writes=[B_rz], accum=(j > 0))
                b.emit("vector", lambda e, j=j: e.tensor_scalar(
                    otm[:, j * 128:(j + 1) * 128], ps_av[j][:, 0:128],
                    rz_t[:, j:j + 1], None, op0=OP.mult),
                    reads=[B_pav[j], B_rz], writes=[B_otm], accum=(j > 0))

            def tr4o(e):
                ii = None
                for j in range(4):
                    ii = e.transpose(ps_ot[:, j * 128:(j + 1) * 128].bitcast(F32),
                                     otm[:, j * 128:(j + 1) * 128], ident_f[:])
                return ii
            b.emit("tensor", tr4o, reads=[B_otm, B_idf], writes=[B_pot])
            b.emit("scalar", lambda e: e.copy(ohh[:], ps_ot[:]), reads=[B_pot], writes=[B_ohh])
            b.emit("vector", lambda e: e.tensor_tensor(ohl[:], ps_ot[:],
                                                       ohh[:].bitcast(F32), OP.subtract),
                   reads=[B_pot, B_ohh], writes=[B_ohl])
            b.emit("gpsimd", lambda e, h=h, qi=qi: e.dma_start(
                out=oh_d[h * 128:(h + 1) * 128, qi * TH:(qi + 1) * TH], in_=ohh[:]),
                reads=[B_ohh], writes=[dB["oh_d"]], sem="gd", inc_n=16,
                accum=not (h == 0 and qi == 0))
            b.emit("gpsimd", lambda e, h=h, qi=qi: e.dma_start(
                out=ol_d[h * 128:(h + 1) * 128, qi * TH:(qi + 1) * TH], in_=ohl[:]),
                reads=[B_ohl], writes=[dB["ol_d"]], sem="gd", inc_n=16,
                accum=not (h == 0 and qi == 0))

        load_head(0, 0)
        pending_evac = None
        for h in range(H):
            i = h % 2
            if h + 1 < H:
                load_head(h + 1, (h + 1) % 2)
            emit_mh(h)
            for qi in range(2):
                for c in range(32):
                    emit_smm(i, qi, c)
                    if c == 0:
                        emit_exp(c)
                        if pending_evac is not None:
                            emit_evac(*pending_evac)
                            pending_evac = None
                    else:
                        emit_avm(i, c - 1)
                        emit_exp(c)
                emit_avm(i, 31)
                pending_evac = (h, qi)
        emit_evac(*pending_evac)

        if dbg == 2 or until_phase < 3:
            p2.close()
            b.fence()
            for (sap, map_) in dbg_mirrors:
                b.emit("gpsimd", lambda e, s=sap, m=map_: e.dma_start(out=m, in_=s),
                       sem="gd", inc_n=16)
            b.fence()
            b.emit("gpsimd", lambda e: e.engine_nop())
            b.finalize()
            return nc

        p2.close()
        b.fence()
        p3 = es.enter_context(ExitStack())
        o_hh, B_oh = sb("o_hh", [128, DC * TH], F32R, p3)
        o_ll, B_ol = sb("o_ll", [128, DC * TH], F32R, p3)
        wfq3, whq3, wlq3, B_wf3, B_wh3, B_wl3 = [], [], [], [], [], []
        for i in range(2):
            t1, b1 = sb(f"wfq3{i}", [128, 1024], F32, p3)
            t2, b2 = sb(f"whq3{i}", [128, 1024], F32R, p3)
            t3, b3 = sb(f"wlq3{i}", [128, 1024], F32R, p3)
            wfq3.append(t1); whq3.append(t2); wlq3.append(t3)
            B_wf3.append(b1); B_wh3.append(b2); B_wl3.append(b3)
        of_t, B_oft = sb("of_t", [128, TH], F32, p3)
        x_t4, B_xt4 = sb("x_t4", [128, TH], F32, p3)
        x1_t, B_x1t = sb("x1_t", [128, TH], F32, p3)
        ps_pr3, B_pp3 = zip(*[pst(f"ps_pr3{i}", [128, TH], p3) for i in range(2)])
        ps_tr3, B_pt3 = zip(*[pst(f"ps_tr3{i}", [128, TH], p3) for i in range(2)])

        def proj_block3(wap, eb, xh, xl, B_xh, B_xl, psum, B_psum,
                        wf, wh, wl, B_wfL, B_whL, B_wlL):
            for q in range(NQ):
                i = q % 2
                b.emit("sync", lambda e, eb=eb, q=q, i=i: e.dma_start(out=wf[i][:], in_=wap[eb, q]),
                       writes=[B_wfL[i]], sem=f"dw3{i}", inc_n=16)
                b.emit("scalar", lambda e, i=i: e.copy(wh[i][:], wf[i][:]),
                       reads=[B_wfL[i]], writes=[B_whL[i]])
                b.emit("vector", lambda e, i=i: e.tensor_tensor(wl[i][:], wf[i][:],
                                                                wh[i][:].bitcast(F32), OP.subtract),
                       reads=[B_wfL[i], B_whL[i]], writes=[B_wlL[i]])

                def mms(e, q=q, i=i):
                    ii = None
                    for cl in range(8):
                        c = q * 8 + cl
                        first = (q == 0 and cl == 0)
                        last = (q == NQ - 1 and cl == 7)
                        wh_ap = wh[i][:, cl * 128:(cl + 1) * 128]
                        wl_ap = wl[i][:, cl * 128:(cl + 1) * 128]
                        xh_ap = xh[:, c * TH:(c + 1) * TH]
                        xl_ap = xl[:, c * TH:(c + 1) * TH]
                        e.matmul(psum[:], wh_ap, xh_ap, start=first, stop=False)
                        e.matmul(psum[:], wh_ap, xl_ap, start=False, stop=False)
                        ii = e.matmul(psum[:], wl_ap, xh_ap, start=False, stop=last)
                    return ii
                b.emit("tensor", mms, reads=[B_whL[i], B_wlL[i], B_xh, B_xl],
                       writes=[B_psum], accum=(q > 0))

        for half in range(2):
            b.emit("sync", lambda e, h=half: e.dma_start(
                out=o_hh[:].rearrange("p (c t) -> p c t", t=TH),
                in_=oh_d[:, h * TH:(h + 1) * TH].rearrange("(c p) t -> p c t", p=128)),
                reads=[dB["oh_d"]], writes=[B_oh], sem="do", inc_n=16)
            b.emit("sync", lambda e, h=half: e.dma_start(
                out=o_ll[:].rearrange("p (c t) -> p c t", t=TH),
                in_=ol_d[:, h * TH:(h + 1) * TH].rearrange("(c p) t -> p c t", p=128)),
                reads=[dB["ol_d"]], writes=[B_ol], sem="do", inc_n=16)
            for eb in range(EB):
                psum, B_psum = ps_pr3[eb % 2], B_pp3[eb % 2]
                proj_block3(w_hbm["wo_b"], eb, o_hh, o_ll, B_oh, B_ol, psum, B_psum,
                            wfq3, whq3, wlq3, B_wf3, B_wh3, B_wl3)
                b.emit("scalar", lambda e, p=psum: e.copy(of_t[:], p[:]),
                       reads=[B_psum], writes=[B_oft])
                b.emit("sync", lambda e, eb=eb, h=half: e.dma_start(
                    out=x_t4[:].rearrange("p (s j) -> p s j", j=128),
                    in_=x_s[h * TH:(h + 1) * TH, eb * 128:(eb + 1) * 128]
                    .rearrange("(s p) j -> p s j", p=128)),
                    writes=[B_xt4], sem="dx4", inc_n=16)
                pb, B_pb = ps_tr3[eb % 2], B_pt3[eb % 2]

                def tr4p(e, pb=pb):
                    ii = None
                    for j in range(4):
                        ii = e.transpose(pb[:, j * 128:(j + 1) * 128].bitcast(F32),
                                         of_t[:, j * 128:(j + 1) * 128], ident_f[:])
                    return ii
                b.emit("tensor", tr4p, reads=[B_oft, B_idf], writes=[B_pb])
                b.emit("vector", lambda e, pb=pb: e.tensor_tensor(x1_t[:], pb[:], x_t4[:], OP.add),
                       reads=[B_pb, B_xt4], writes=[B_x1t])
                b.emit("gpsimd", lambda e, eb=eb, h=half: e.dma_start(
                    out=x1_d[h * TH:(h + 1) * TH, eb * 128:(eb + 1) * 128]
                    .rearrange("(s p) j -> p s j", p=128),
                    in_=x1_t[:].rearrange("p (s j) -> p s j", j=128)),
                    reads=[B_x1t], writes=[dB["x1_d"]], sem="gd", inc_n=16,
                    accum=not (half == 0 and eb == 0))

        p3.close()
        b.fence()
        p3 = es.enter_context(ExitStack())
        ps_tr3, B_pt3 = zip(*[pst(f"ps_trb{i}", [128, TH], p3) for i in range(2)])
        cent_f, B_ctf = sb("cent_f", [128, DC * NCL], F32, p3)
        cent_h, B_cth = sb("cent_h", [128, DC * NCL], F32R, p3)
        cent_l, B_ctl = sb("cent_l", [128, DC * NCL], F32R, p3)
        b.emit("sync", lambda e: e.dma_start(out=cent_f[:], in_=cent_in),
               writes=[B_ctf], sem="dct", inc_n=16)
        b.emit("scalar", lambda e: e.copy(cent_h[:], cent_f[:]), reads=[B_ctf], writes=[B_cth])
        b.emit("vector", lambda e: e.tensor_tensor(cent_l[:], cent_f[:],
                                                   cent_h[:].bitcast(F32), OP.subtract),
               reads=[B_ctf, B_cth], writes=[B_ctl])
        ln2h_t, B_l2h = sb("ln2h_t", [128, DC * 128], F32R, p3)
        ln2l_t, B_l2l = sb("ln2l_t", [128, DC * 128], F32R, p3)
        xt2, B_xt2 = sb("xt2", [128, D], F32, p3)
        l2h_tm, B_h2 = sb("l2h_tm", [128, D], F32R, p3)
        l2l_tm, B_l2 = sb("l2l_tm", [128, D], F32R, p3)
        st2, B_st2 = sb("st2", [128, 48], F32, p3)
        mv2, B_mv2 = sb("mv2", [128, 2], F32, p3)
        sd2, B_sd2 = sb("sd2", [128, 1], F32, p3)
        rs2, B_rs2 = sb("rs2", [128, 1], F32, p3)
        nm2, B_nm2 = sb("nm2", [128, 1], F32, p3)
        dex, B_dex = sb("dex", [128, NCL], F32, p3)
        dmx, B_dmx = sb("dmx", [128, 1], F32, p3)
        dnb, B_dnb = sb("dnb", [128, 1], F32, p3)
        dzs, B_dzs = sb("dzs", [128, 1], F32, p3)
        dth, B_dth = sb("dth", [128, 1], F32, p3)
        hits, B_hit = sb("hits", [128, NCL], F32R, p3)
        ps_dt, B_pdt = pst("ps_dt", [128, NCL], p3)
        ps_ct, B_pct = pst("ps_ct", [1, NCL], p3)

        for ti in range(8):
            b.emit("sync", lambda e, ti=ti: e.dma_start(out=xt2[:],
                                                        in_=x1_d[ti * 128:(ti + 1) * 128, :]),
                   reads=[dB["x1_d"]], writes=[B_xt2], sem="dx", inc_n=16)

            def bn2(e):
                ii = None
                for i in range(8):
                    ii = e.bn_stats(st2[:, i * 6:(i + 1) * 6], xt2[:, i * 512:(i + 1) * 512])
                return ii
            b.emit("vector", bn2, reads=[B_xt2], writes=[B_st2])
            b.emit("vector", lambda e: e.bn_aggr(mv2[:], st2[:]), reads=[B_st2], writes=[B_mv2])
            b.emit("scalar", lambda e: e.activation(sd2[:], mv2[:, 1:2], AF.Sqrt,
                                                    bias=eps_t[:], scale=1.0),
                   reads=[B_mv2, B_eps], writes=[B_sd2])
            b.emit("vector", lambda e: e.reciprocal(rs2[:], sd2[:]), reads=[B_sd2], writes=[B_rs2])
            b.emit("vector", lambda e: e.tensor_scalar(nm2[:], mv2[:, 0:1], rs2[:], -1.0,
                                                       op0=OP.mult, op1=OP.mult),
                   reads=[B_mv2, B_rs2], writes=[B_nm2])
            b.emit("scalar", lambda e: e.activation(l2h_tm[:], xt2[:], AF.Identity,
                                                    bias=nm2[:], scale=rs2[:]),
                   reads=[B_xt2, B_nm2, B_rs2], writes=[B_h2])
            b.emit("vector", lambda e: e.scalar_tensor_tensor(
                l2l_tm[:], xt2[:], rs2[:], l2h_tm[:].bitcast(F32), op0=OP.mult, op1=OP.subtract),
                reads=[B_xt2, B_rs2, B_h2], writes=[B_l2])
            b.emit("vector", lambda e: e.tensor_scalar(l2l_tm[:], l2l_tm[:], nm2[:], None,
                                                       op0=OP.add),
                   reads=[B_nm2], writes=[B_l2])
            for (src, B_src, dst, B_dst) in ((l2h_tm, B_h2, ln2h_t, B_l2h),
                                             (l2l_tm, B_l2, ln2l_t, B_l2l)):
                for g in range(8):
                    pb, B_pb2 = ps_tr3[g % 2], B_pt3[g % 2]

                    def tr4l(e, src=src, g=g, pb=pb):
                        ii = None
                        for k in range(4):
                            c = g * 4 + k
                            ii = e.transpose(pb[:, k * 128:(k + 1) * 128].bitcast(F32R),
                                             src[:, c * 128:(c + 1) * 128], ident_r[:])
                        return ii
                    b.emit("tensor", tr4l, reads=[B_src, B_idr], writes=[B_pb2])
                    b.emit("scalar", lambda e, dst=dst, g=g, pb=pb: e.copy(
                        dst[:, g * 512:(g + 1) * 512], pb[:].bitcast(F32R)),
                        reads=[B_pb2], writes=[B_dst], accum=(g > 0))
            b.emit("gpsimd", lambda e, ti=ti: e.dma_start(
                out=lnh_d[:, ti * 128:(ti + 1) * 128].rearrange("(c p) t -> p c t", p=128),
                in_=ln2h_t[:].rearrange("p (c t) -> p c t", t=128)),
                reads=[B_l2h], writes=[dB["lnh_d"]], sem="gd", inc_n=16, accum=(ti > 0))
            b.emit("gpsimd", lambda e, ti=ti: e.dma_start(
                out=lnl_d[:, ti * 128:(ti + 1) * 128].rearrange("(c p) t -> p c t", p=128),
                in_=ln2l_t[:].rearrange("p (c t) -> p c t", t=128)),
                reads=[B_l2l], writes=[dB["lnl_d"]], sem="gd", inc_n=16, accum=(ti > 0))

            def dotsmm(e):
                ii = None
                for c in range(DC):
                    first = (c == 0)
                    last = (c == DC - 1)
                    e.matmul(ps_dt[:], ln2h_t[:, c * 128:(c + 1) * 128],
                             cent_h[:, c * NCL:(c + 1) * NCL], start=first, stop=False)
                    e.matmul(ps_dt[:], ln2h_t[:, c * 128:(c + 1) * 128],
                             cent_l[:, c * NCL:(c + 1) * NCL], start=False, stop=False)
                    ii = e.matmul(ps_dt[:], ln2l_t[:, c * 128:(c + 1) * 128],
                                  cent_h[:, c * NCL:(c + 1) * NCL], start=False, stop=last)
                return ii
            b.emit("tensor", dotsmm, reads=[B_l2h, B_l2l, B_cth, B_ctl], writes=[B_pdt])
            b.emit("vector", lambda e: e.tensor_reduce(dmx[:], ps_dt[:], axis=AX.X, op=OP.max),
                   reads=[B_pdt], writes=[B_dmx])
            b.emit("vector", lambda e: e.tensor_scalar(dnb[:], dmx[:], -10.0, None, op0=OP.mult),
                   reads=[B_dmx], writes=[B_dnb])
            b.emit("scalar", lambda e: e.activation(dex[:], ps_dt[:], AF.Exp,
                                                    bias=dnb[:], scale=10.0, accum_out=dzs[:]),
                   reads=[B_pdt, B_dnb], writes=[B_dex, B_dzs])
            b.emit("vector", lambda e: e.tensor_scalar(dth[:], dzs[:], 0.01, None, op0=OP.mult),
                   reads=[B_dzs], writes=[B_dth])
            b.emit("vector", lambda e: e.tensor_scalar(hits[:], dex[:], dth[:], None, op0=OP.is_gt),
                   reads=[B_dex, B_dth], writes=[B_hit])
            b.emit("vector", lambda e, ti=ti: e.tensor_reduce(
                qsel_sb[:, ti:ti + 1], hits[:].bitcast(F32), axis=AX.X, op=OP.max),
                reads=[B_hit], writes=[B_qs], accum=(ti > 0))
            b.emit("tensor", lambda e: e.matmul(ps_ct[:], ones_r[:], hits[:],
                                                start=True, stop=True),
                   reads=[B_hit, B_onr], writes=[B_pct])
            b.emit("vector", lambda e: e.tensor_tensor(cnt_sb[:], cnt_sb[:], ps_ct[:], OP.add),
                   reads=[B_pct, B_cnt], writes=[B_cnt])

        b.emit("gpsimd", lambda e: e.dma_start(out=cnt_in, in_=cnt_sb[:]),
               reads=[B_cnt], writes=[dB["cnt_io"]], sem="gd", inc_n=16)
        b.emit("gpsimd", lambda e: e.collective_compute(
            "AllReduce", OP.add, replica_groups=[[0, 1, 2, 3, 4, 5, 6, 7]],
            ins=[cnt_in.opt()], outs=[cnt_out.opt()]),
            reads=[dB["cnt_io"]], writes=[dB["cnt_io"]], sem="cc")

        if dbg == 3 or until_phase < 4:
            p3.close()
            b.fence()
            for (sap, map_) in dbg_mirrors:
                b.emit("gpsimd", lambda e, s=sap, m=map_: e.dma_start(out=m, in_=s),
                       sem="gd", inc_n=16)
            b.fence()
            b.emit("gpsimd", lambda e: e.engine_nop())
            b.finalize()
            return nc

        p3.close()
        b.fence()
        p4 = es.enter_context(ExitStack())
        bias_b, B_bb = sb("bias_b", [128, D], F32, p4)
        rsel_b, B_rb = sb("rsel_b", [128, D], F32, p4)
        p4a = es.enter_context(ExitStack())
        bias_r, B_br = sb("bias_r", [1, D], F32, p4a)
        asg_s, B_as = sb("asg_s", [1, D], I32, p4a)
        asg_f, B_af = sb("asg_f", [1, D], F32, p4a)
        asg_b, B_ab = sb("asg_b", [64, D], F32, p4a)
        i64, B_i64 = sb("i64", [64, 1], I32, p4a)
        i64f, B_i64f = sb("i64f", [64, 1], F32, p4a)
        onehot, B_oh1 = sb("onehot", [64, D], F32R, p4a)
        cnt_t, B_ct2 = sb("cnt_t", [64, 1], F32, p4a)
        csel, B_cs = sb("csel", [64, 1], F32R, p4a)
        rsel_r, B_rr = sb("rsel_r", [1, D], F32, p4a)
        ps_rs, B_prs = pst("ps_rs", [1, TH], p4a)
        ps_bc4, B_pbc4 = pst("ps_bc4", [128, TH], p4a)

        def bcast_mm(dst, B_dst, row_t, B_row, nparts, ncols):
            for q0 in range(0, ncols, TH):
                b.emit("tensor", lambda e, q0=q0: e.matmul(
                    ps_bc4[0:nparts, 0:TH], onesc[:, 0:nparts], row_t[:, q0:q0 + TH],
                    start=True, stop=True),
                    reads=[B_row, B_onc], writes=[B_pbc4])
                b.emit("vector", lambda e, q0=q0: e.tensor_copy(
                    dst[0:nparts, q0:q0 + TH], ps_bc4[0:nparts, 0:TH]),
                    reads=[B_pbc4], writes=[B_dst], accum=(q0 > 0))

        b.emit("sync", lambda e: e.dma_start(out=bias_r[:], in_=bias_in),
               writes=[B_br], sem="db1", inc_n=16)
        b.emit("sync", lambda e: e.dma_start(out=asg_s[:], in_=asg_in),
               writes=[B_as], sem="db2", inc_n=16)
        bcast_mm(bias_b, B_bb, bias_r, B_br, 128, D)
        b.emit("vector", lambda e: e.tensor_copy(asg_f[:], asg_s[:]), reads=[B_as], writes=[B_af])
        bcast_mm(asg_b, B_ab, asg_f, B_af, 64, D)
        b.emit("gpsimd", lambda e: e.iota(i64[:], pattern=[[0, 1]], base=0, channel_multiplier=1),
               writes=[B_i64])
        b.emit("vector", lambda e: e.tensor_copy(i64f[:], i64[:]), reads=[B_i64], writes=[B_i64f])
        b.emit("vector", lambda e: e.tensor_scalar(onehot[:], asg_b[:], i64f[:], None,
                                                   op0=OP.is_equal),
               reads=[B_ab, B_i64f], writes=[B_oh1])
        b.emit("sync", lambda e: e.dma_start(out=cnt_t[:], in_=cnt_out.rearrange("a b -> b a")),
               reads=[dB["cnt_io"]], writes=[B_ct2], sem="db3", inc_n=16)
        b.emit("vector", lambda e: e.tensor_scalar(csel[:], cnt_t[:], 0.5, None, op0=OP.is_gt),
               reads=[B_ct2], writes=[B_cs])
        for q in range(8):
            b.emit("tensor", lambda e, q=q: e.matmul(
                ps_rs[:], csel[:], onehot[:, q * TH:(q + 1) * TH], start=True, stop=True),
                reads=[B_cs, B_oh1], writes=[B_prs])
            b.emit("vector", lambda e, q=q: e.tensor_copy(rsel_r[:, q * TH:(q + 1) * TH], ps_rs[:]),
                   reads=[B_prs], writes=[B_rr], accum=(q > 0))
        bcast_mm(rsel_b, B_rb, rsel_r, B_rr, 128, D)

        p4a.close()
        b.fence()
        l1h, B_1h = sb("l1h", [128, DC * TH], F32R, p4)
        l1l, B_1l = sb("l1l", [128, DC * TH], F32R, p4)
        wfq4, whq4, wlq4, B_wf4, B_wh4, B_wl4 = [], [], [], [], [], []
        wfq40, B_wf40 = sb("wfq40", [128, 1024], F32, p4)
        wfq4, B_wf4 = [wfq40, wfq40], [B_wf40, B_wf40]
        for i in range(2):
            t2, b2 = sb(f"whq4{i}", [128, 1024], F32R, p4)
            t3, b3 = sb(f"wlq4{i}", [128, 1024], F32R, p4)
            whq4.append(t2); wlq4.append(t3)
            B_wh4.append(b2); B_wl4.append(b3)
        ff_t, B_fft = sb("ff_t", [128, TH], F32, p4)
        x1r_t, B_x1r = sb("x1r_t", [128, TH], F32, p4)
        out_t, B_out4 = sb("out_t", [128, TH], F32, p4)
        B_outs = [Buf() for _ in range(4)]
        ps_pr4, B_pp4 = zip(*[pst(f"ps_pr4{i}", [128, TH], p4) for i in range(2)])
        ps_tr4, B_pt4 = zip(*[pst(f"ps_tr4{i}", [128, TH], p4) for i in range(2)])

        for half in range(2):
            b.emit("sync", lambda e, h=half: e.dma_start(
                out=l1h[:].rearrange("p (c t) -> p c t", t=TH),
                in_=lnh_d[:, h * TH:(h + 1) * TH].rearrange("(c p) t -> p c t", p=128)),
                reads=[dB["lnh_d"]], writes=[B_1h], sem="dl", inc_n=16)
            b.emit("sync", lambda e, h=half: e.dma_start(
                out=l1l[:].rearrange("p (c t) -> p c t", t=TH),
                in_=lnl_d[:, h * TH:(h + 1) * TH].rearrange("(c p) t -> p c t", p=128)),
                reads=[dB["lnl_d"]], writes=[B_1l], sem="dl", inc_n=16)
            for eb in range(EB):
                psum, B_psum = ps_pr4[eb % 2], B_pp4[eb % 2]
                for q in range(NQ):
                    i = q % 2
                    b.emit("sync", lambda e, eb=eb, q=q, i=i: e.dma_start(
                        out=wfq4[i][:], in_=w_hbm["wf_b"][eb, q]),
                        writes=[B_wf4[i]], sem="dw", inc_n=16)
                    b.emit("scalar", lambda e, i=i: e.copy(whq4[i][:], wfq4[i][:]),
                           reads=[B_wf4[i]], writes=[B_wh4[i]])
                    b.emit("vector", lambda e, i=i: e.tensor_tensor(
                        wlq4[i][:], wfq4[i][:], whq4[i][:].bitcast(F32), OP.subtract),
                        reads=[B_wf4[i], B_wh4[i]], writes=[B_wl4[i]])

                    def mms4(e, q=q, i=i, psum=psum):
                        ii = None
                        for cl in range(8):
                            c = q * 8 + cl
                            first = (q == 0 and cl == 0)
                            last = (q == NQ - 1 and cl == 7)
                            wh_ap = whq4[i][:, cl * 128:(cl + 1) * 128]
                            wl_ap = wlq4[i][:, cl * 128:(cl + 1) * 128]
                            xh_ap = l1h[:, c * TH:(c + 1) * TH]
                            xl_ap = l1l[:, c * TH:(c + 1) * TH]
                            e.matmul(psum[:], wh_ap, xh_ap, start=first, stop=False)
                            e.matmul(psum[:], wh_ap, xl_ap, start=False, stop=False)
                            ii = e.matmul(psum[:], wl_ap, xh_ap, start=False, stop=last)
                        return ii
                    b.emit("tensor", mms4, reads=[B_wh4[i], B_wl4[i], B_1h, B_1l],
                           writes=[B_psum], accum=(q > 0))
                b.emit("scalar", lambda e, p=psum: e.copy(ff_t[:], p[:]),
                       reads=[B_psum], writes=[B_fft])
                b.emit("sync", lambda e, eb=eb, h=half: e.dma_start(
                    out=x1r_t[:].rearrange("p (s j) -> p s j", j=128),
                    in_=x1_d[h * TH:(h + 1) * TH, eb * 128:(eb + 1) * 128]
                    .rearrange("(s p) j -> p s j", p=128)),
                    reads=[dB["x1_d"]], writes=[B_x1r], sem="dx4", inc_n=16)
                pb, B_pb4 = ps_tr4[eb % 2], B_pt4[eb % 2]

                def tr4f(e, pb=pb):
                    ii = None
                    for j in range(4):
                        ii = e.transpose(pb[:, j * 128:(j + 1) * 128].bitcast(F32),
                                         ff_t[:, j * 128:(j + 1) * 128], ident_f[:])
                    return ii
                b.emit("tensor", tr4f, reads=[B_fft, B_idf], writes=[B_pb4])
                for j in range(4):
                    b.emit("vector", lambda e, j=j, pb=pb, eb=eb: e.tensor_tensor(
                        out_t[:, j * 128:(j + 1) * 128], pb[:, j * 128:(j + 1) * 128],
                        bias_b[:, eb * 128:(eb + 1) * 128], OP.add),
                        reads=[B_pb4, B_bb], writes=[B_outs[j]])
                    b.emit("vector", lambda e, j=j, eb=eb: e.tensor_tensor(
                        out_t[:, j * 128:(j + 1) * 128], out_t[:, j * 128:(j + 1) * 128],
                        rsel_b[:, eb * 128:(eb + 1) * 128], OP.mult),
                        reads=[B_rb, B_outs[j]], writes=[B_outs[j]])
                    b.emit("vector", lambda e, j=j, h=half: e.scalar_tensor_tensor(
                        out_t[:, j * 128:(j + 1) * 128], out_t[:, j * 128:(j + 1) * 128],
                        qsel_sb[:, h * 4 + j:h * 4 + j + 1], x1r_t[:, j * 128:(j + 1) * 128],
                        op0=OP.mult, op1=OP.add),
                        reads=[B_qs, B_x1r, B_outs[j]], writes=[B_outs[j]])
                b.emit("gpsimd", lambda e, eb=eb, h=half: e.dma_start(
                    out=y_s[h * TH:(h + 1) * TH, eb * 128:(eb + 1) * 128]
                    .rearrange("(s p) j -> p s j", p=128),
                    in_=out_t[:].rearrange("p (s j) -> p s j", j=128)),
                    reads=B_outs, writes=[dB["y"]], sem="gd", inc_n=16,
                    accum=not (half == 0 and eb == 0))

        p4.close()
        b.fence()
        for (sap, map_) in dbg_mirrors:
            b.emit("gpsimd", lambda e, s=sap, m=map_: e.dma_start(out=m, in_=s),
                   sem="gd", inc_n=16)
        if dbg_mirrors:
            b.fence()
        b.emit("gpsimd", lambda e: e.engine_nop())
        b.finalize()
    return nc


_NC_CACHE = {}


def _get_program():
    if "nc" not in _NC_CACHE:
        _NC_CACHE["nc"] = build_program()
    return _NC_CACHE["nc"]


def _block_weights(w):
    wt = np.ascontiguousarray(w.T)
    v = wt.reshape(NQ, 8, 128, EB, 128)
    v = v.transpose(3, 0, 2, 1, 4)
    return np.ascontiguousarray(v.reshape(EB, NQ, 128, 1024))


def kernel(L0_attn_x, L0_attn_w_q, L0_attn_w_k, L0_attn_w_v, L0_attn_w_o,
           L0_ffn_weight, L0_ffn_bias, L0_ffn_centroids, L0_ffn_assignments):
    x = np.ascontiguousarray(np.asarray(L0_attn_x, dtype=np.float32))
    wq = _block_weights(np.asarray(L0_attn_w_q, dtype=np.float32))
    wk = _block_weights(np.asarray(L0_attn_w_k, dtype=np.float32))
    wv = _block_weights(np.asarray(L0_attn_w_v, dtype=np.float32))
    wo = _block_weights(np.asarray(L0_attn_w_o, dtype=np.float32))
    wf = _block_weights(np.asarray(L0_ffn_weight, dtype=np.float32))
    ct = np.ascontiguousarray(np.asarray(L0_ffn_centroids, dtype=np.float32).T)
    cb = np.ascontiguousarray(ct.reshape(DC, 128, NCL).transpose(1, 0, 2)
                              .reshape(128, DC * NCL))
    bias = np.ascontiguousarray(np.asarray(L0_ffn_bias, dtype=np.float32).reshape(1, D))
    asg = np.ascontiguousarray(np.asarray(L0_ffn_assignments, dtype=np.int32).reshape(1, D))

    xf = x.reshape(2 * S, D)
    nc = _get_program()
    in_maps = []
    for c in range(NCORES):
        in_maps.append({
            "x_s": np.ascontiguousarray(xf[c * T:(c + 1) * T, :]),
            "wq_b": wq, "wk_b": wk, "wv_b": wv, "wo_b": wo, "wf_b": wf,
            "cent_b": cb, "bias_in": bias, "asg_in": asg,
        })
    res = run_bass_kernel_spmd(nc, in_maps, core_ids=list(range(NCORES)))
    out = np.concatenate([res.results[c]["y_s"] for c in range(NCORES)], axis=0)
    return out.reshape(2, S, D).astype(np.float32)


# version 05
# speedup vs baseline: 1.2661x; 1.0340x over previous
import numpy as np
from contextlib import ExitStack

import concourse.bass as bass
import concourse.mybir as mybir
from concourse.bass_utils import run_bass_kernel_spmd
from concourse.alu_op_type import AluOpType as OP
from concourse import library_config

F32 = mybir.dt.float32
F32R = mybir.dt.float32r
I32 = mybir.dt.int32
AF = mybir.ActivationFunctionType
AX = mybir.AxisListType

NCORES = 8
D = 4096
T = 1024
TH = 512
S = 4096
H = 32
DC = 32
EB = 32
NQ = 4
NCL = 64
INV_SQRT_DK = float(1.0 / np.sqrt(np.float32(128.0)))
ENGS = ("sync", "scalar", "vector", "tensor", "gpsimd")
ENG_SEM = {"sync": "dh", "scalar": "sc", "vector": "dv", "tensor": "pe",
           "gpsimd": "gp"}


class Buf:

    __slots__ = ("w", "rs")

    def __init__(self):
        self.w = []
        self.rs = []


def _prune(toks):
    best = {}
    for (s, v) in toks:
        if best.get(s, -1) < v:
            best[s] = v
    return [(s, v) for s, v in best.items()]


class B:
    def __init__(self, nc, es):
        self.nc = nc
        self.es = es
        self.ops = {e: [] for e in ENGS}
        self.sems = {}
        self.fence_waits = []

    def _sem(self, name):
        if name not in self.sems:
            h = self.es.enter_context(self.nc.semaphore(name))
            self.sems[name] = [h, 0]
        return self.sems[name]

    def emit(self, eng, fn, reads=(), writes=(), extra=(), sem=None,
             inc_n=1, accum=False):
        sem = sem or ENG_SEM[eng]
        rec = self._sem(sem)
        waits = list(extra) + list(self.fence_waits)
        for bf in reads:
            waits += bf.w
        for bf in writes:
            if not accum:
                waits += bf.w + bf.rs
        rec[1] += inc_n
        tok = (sem, rec[1])
        for bf in reads:
            bf.rs = _prune(bf.rs + [tok])
        for bf in writes:
            if accum:
                bf.w = _prune(bf.w + [tok])
            else:
                bf.w = [tok]
                bf.rs = []
        self.ops[eng].append((fn, _prune(waits), (sem, inc_n)))
        return tok

    def fence(self):
        self.fence_waits = _prune(
            [(n, rec[1]) for n, rec in self.sems.items() if rec[1] > 0])

    def finalize(self):
        with self.nc.Block() as block:
            for eng in ENGS:
                oplist = self.ops[eng]
                if not oplist:
                    continue

                def mk(oplist):
                    def body(e):
                        seen = {}
                        for fn, waits, inc in oplist:
                            for (sn, v) in waits:
                                if seen.get(sn, -1) >= v:
                                    continue
                                e.wait_ge(self.sems[sn][0], v)
                                seen[sn] = v
                            instr = fn(e)
                            if instr is not None:
                                instr.then_inc(self.sems[inc[0]][0], inc[1])
                    return body

                getattr(block, eng)(mk(oplist))


def build_program(until_phase=99, dbg=None, dbg_out=()):
    nc = bass.Bass("TRN2", target_bir_lowering=False)

    dbg_mirrors = []

    def scratch(name, shape, dt):
        ap = nc.dram_tensor(name, shape, dt).ap()
        if name in dbg_out:
            m = nc.dram_tensor(name + "_dbg", shape, dt, kind="ExternalOutput").ap()
            dbg_mirrors.append((ap, m))
        return ap

    x_s = nc.dram_tensor("x_s", [T, D], F32, kind="ExternalInput").ap()
    w_hbm = {}
    for nm in ("wq_b", "wk_b", "wv_b", "wo_b", "wf_b"):
        w_hbm[nm] = nc.dram_tensor(nm, [EB, NQ, 128, 1024], F32,
                                   kind="ExternalInput").ap()
    cent_in = nc.dram_tensor("cent_b", [128, DC * NCL], F32, kind="ExternalInput").ap()
    bias_in = nc.dram_tensor("bias_in", [1, D], F32, kind="ExternalInput").ap()
    asg_in = nc.dram_tensor("asg_in", [1, D], I32, kind="ExternalInput").ap()
    y_s = nc.dram_tensor("y_s", [T, D], F32, kind="ExternalOutput").ap()

    kh_sh = scratch("kh_sh", [D, T], F32R)
    kl_sh = scratch("kl_sh", [D, T], F32R)
    v_sh = scratch("v_sh", [T, D], F32)
    kh_g = scratch("kh_g", [4 * D, T], F32R)
    kl_g = scratch("kl_g", [4 * D, T], F32R)
    v_g = scratch("v_g", [S, D], F32)
    qh_d = scratch("qh_d", [D, T], F32R)
    ql_d = scratch("ql_d", [D, T], F32R)
    oh_d = scratch("oh_d", [D, T], F32R)
    ol_d = scratch("ol_d", [D, T], F32R)
    x1_d = scratch("x1_d", [T, D], F32)
    lnh_d = scratch("lnh_d", [D, T], F32R)
    lnl_d = scratch("lnl_d", [D, T], F32R)
    kn_in = scratch("kn_in", [1, H], F32)
    kn_out = scratch("kn_out", [1, H], F32)
    cnt_in = scratch("cnt_in", [1, NCL], F32)
    cnt_out = scratch("cnt_out", [1, NCL], F32)

    dB = {nm: Buf() for nm in ("kh_sh", "kl_sh", "v_sh", "kh_g", "kl_g", "v_g",
                               "qh_d", "ql_d", "oh_d", "ol_d", "x1_d",
                               "lnh_d", "lnl_d", "kn_io", "cnt_io", "y")}

    es = ExitStack()
    b = B(nc, es)

    def sb(name, shape, dt=F32, st=None):
        stk = st or es
        t = stk.enter_context(nc.sbuf_tensor(name, shape, dt))
        return t, Buf()

    def pst(name, shape, st=None, dt=F32):
        stk = st or es
        t = stk.enter_context(nc.psum_tensor(name, shape, dt))
        return t, Buf()

    with es:
        ident_r, B_idr = sb("ident_r", [128, 128], F32R)
        ident_f, B_idf = sb("ident_f", [128, 128], F32)
        ones_r, B_onr = sb("ones_r", [128, 1], F32R)
        ones_f, B_onf = sb("ones_f", [128, 1], F32)
        onesc, B_onc = sb("onesc", [1, 128], F32)
        eps_t, B_eps = sb("eps_t", [128, 1], F32)
        qn_sb, B_qn = sb("qn_sb", [1, H], F32)
        kn_sb, B_kn = sb("kn_sb", [1, H], F32)
        qsel_sb, B_qs = sb("qsel_sb", [128, 8], F32)
        cnt_sb, B_cnt = sb("cnt_sb", [1, NCL], F32)

        p0 = es.enter_context(ExitStack())
        i128, B_i1 = sb("i128", [128, 1], I32, p0)
        irow, B_i2 = sb("irow", [1, 128], I32, p0)
        i128f, B_i1f = sb("i128f", [128, 1], F32, p0)
        irowf, B_i2f = sb("irowf", [1, 128], F32, p0)
        irowb, B_i2b = sb("irowb", [128, 128], F32, p0)

        b.emit("gpsimd", lambda e: e.iota(i128[:], pattern=[[0, 1]], base=0,
                                          channel_multiplier=1), writes=[B_i1])
        b.emit("gpsimd", lambda e: e.iota(irow[:], pattern=[[1, 128]], base=0,
                                          channel_multiplier=0), writes=[B_i2])
        b.emit("vector", lambda e: e.tensor_copy(i128f[:], i128[:]),
               reads=[B_i1], writes=[B_i1f])
        b.emit("vector", lambda e: e.tensor_copy(irowf[:], irow[:]),
               reads=[B_i2], writes=[B_i2f])
        ps_bc0, B_pbc0 = pst("ps_bc0", [128, 128], p0)
        b.emit("vector", lambda e: e.memset(onesc[:], 1.0), writes=[B_onc])
        b.emit("tensor", lambda e: e.matmul(ps_bc0[:], onesc[:], irowf[:],
                                            start=True, stop=True),
               reads=[B_i2f, B_onc], writes=[B_pbc0])
        b.emit("vector", lambda e: e.tensor_copy(irowb[:], ps_bc0[:]),
               reads=[B_pbc0], writes=[B_i2b])
        b.emit("vector", lambda e: e.tensor_scalar(ident_f[:], irowb[:], i128f[:], None,
                                                   op0=OP.is_equal),
               reads=[B_i2b, B_i1f], writes=[B_idf])
        b.emit("vector", lambda e: e.tensor_copy(ident_r[:], ident_f[:]),
               reads=[B_idf], writes=[B_idr])
        b.emit("vector", lambda e: e.memset(ones_f[:], 1.0), writes=[B_onf])
        b.emit("scalar", lambda e: e.copy(ones_r[:], ones_f[:]), reads=[B_onf], writes=[B_onr])
        b.emit("vector", lambda e: e.memset(eps_t[:], 1e-5), writes=[B_eps])
        b.emit("vector", lambda e: e.memset(qn_sb[:], 0.0), writes=[B_qn])
        b.emit("vector", lambda e: e.memset(kn_sb[:], 0.0), writes=[B_kn])
        b.emit("vector", lambda e: e.memset(cnt_sb[:], 0.0), writes=[B_cnt])

        p0.close()
        b.fence()

        p1 = es.enter_context(ExitStack())
        lnxh, B_lh = sb("lnxh", [128, DC * TH], F32R, p1)
        lnxl, B_ll = sb("lnxl", [128, DC * TH], F32R, p1)
        xt_t, B_xt = sb("xt_t", [128, D], F32, p1)
        lnh_tm, B_htm = sb("lnh_tm", [128, D // 2], F32R, p1)
        lnl_tm, B_ltm = sb("lnl_tm", [128, D // 2], F32R, p1)
        stats, B_st = sb("stats", [128, 48], F32, p1)
        mv_t, B_mv = sb("mv_t", [128, 2], F32, p1)
        std_t, B_sd = sb("std_t", [128, 1], F32, p1)
        rstd_t, B_rs = sb("rstd_t", [128, 1], F32, p1)
        nmr_t, B_nm = sb("nmr_t", [128, 1], F32, p1)
        whq, wlq, B_wh, B_wl, wfq, B_wf = [], [], [], [], [], []
        for i in range(2):
            t1, b1 = sb(f"wfq{i}", [128, 1024], F32, p1)
            t2, b2 = sb(f"whq{i}", [128, 1024], F32R, p1)
            t3, b3 = sb(f"wlq{i}", [128, 1024], F32R, p1)
            wfq.append(t1); whq.append(t2); wlq.append(t3)
            B_wf.append(b1); B_wh.append(b2); B_wl.append(b3)
        evh, B_evh = sb("evh", [128, TH], F32R, p1)
        evl, B_evl = sb("evl", [128, TH], F32R, p1)
        evn, B_evn = sb("evn", [128, TH], F32, p1)
        evf, B_evf = evh, B_evh
        sqn, B_sqn = evn, B_evn
        nrm2, B_nr2 = sb("nrm2", [1, 1], F32, p1)
        ps_pr, B_pp = zip(*[pst(f"ps_pr{i}", [128, TH], p1) for i in range(2)])
        ps_tr, B_pt = zip(*[pst(f"ps_tr{i}", [128, TH], p1) for i in range(2)])
        ps_nm, B_pn = pst("ps_nm", [1, TH], p1)

        def ln_into(src_rows, half, dsth, dstl, B_dh, B_dl, chunk_cols, extra_w=()):
            for tt in range(4):
                r0 = half * TH + tt * 128
                b.emit("sync", lambda e, r0=r0: e.dma_start(out=xt_t[:], in_=src_rows[r0:r0 + 128, :]),
                       writes=[B_xt], extra=extra_w, sem="dx", inc_n=16)

                def bnall(e):
                    ii = None
                    for i in range(8):
                        ii = e.bn_stats(stats[:, i * 6:(i + 1) * 6], xt_t[:, i * 512:(i + 1) * 512])
                    return ii
                b.emit("vector", bnall, reads=[B_xt], writes=[B_st])
                b.emit("vector", lambda e: e.bn_aggr(mv_t[:], stats[:]), reads=[B_st], writes=[B_mv])
                b.emit("scalar", lambda e: e.activation(std_t[:], mv_t[:, 1:2], AF.Sqrt,
                                                        bias=eps_t[:], scale=1.0),
                       reads=[B_mv, B_eps], writes=[B_sd])
                b.emit("vector", lambda e: e.reciprocal(rstd_t[:], std_t[:]),
                       reads=[B_sd], writes=[B_rs])
                b.emit("vector", lambda e: e.tensor_scalar(nmr_t[:], mv_t[:, 0:1], rstd_t[:], -1.0,
                                                           op0=OP.mult, op1=OP.mult),
                       reads=[B_mv, B_rs], writes=[B_nm])
                for seg in range(2):
                    s0 = seg * (D // 2)
                    b.emit("scalar", lambda e, s0=s0: e.activation(
                        lnh_tm[:], xt_t[:, s0:s0 + D // 2], AF.Identity,
                        bias=nmr_t[:], scale=rstd_t[:]),
                        reads=[B_xt, B_nm, B_rs], writes=[B_htm])
                    b.emit("vector", lambda e, s0=s0: e.scalar_tensor_tensor(
                        lnl_tm[:], xt_t[:, s0:s0 + D // 2], rstd_t[:], lnh_tm[:].bitcast(F32),
                        op0=OP.mult, op1=OP.subtract),
                        reads=[B_xt, B_rs, B_htm], writes=[B_ltm])
                    b.emit("vector", lambda e: e.tensor_scalar(lnl_tm[:], lnl_tm[:], nmr_t[:], None,
                                                               op0=OP.add),
                           reads=[B_nm], writes=[B_ltm])
                    for (src, B_src, dst, B_dst) in ((lnh_tm, B_htm, dsth, B_dh),
                                                     (lnl_tm, B_ltm, dstl, B_dl)):
                        for g in range(4):
                            pb, B_pb = ps_tr[g % 2], B_pt[g % 2]

                            def tr4(e, src=src, g=g, pb=pb):
                                ii = None
                                for k in range(4):
                                    c = g * 4 + k
                                    ii = e.transpose(pb[:, k * 128:(k + 1) * 128].bitcast(F32R),
                                                     src[:, c * 128:(c + 1) * 128], ident_r[:])
                                return ii
                            b.emit("tensor", tr4, reads=[B_src, B_idr], writes=[B_pb])

                            def ev4(e, dst=dst, g=g, tt=tt, pb=pb, cc=chunk_cols, seg=seg):
                                ap_out = dst[:].rearrange("p (c t) -> p c t", t=cc)[
                                    :, seg * 16 + g * 4:seg * 16 + (g + 1) * 4,
                                    tt * 128:tt * 128 + 128]
                                ap_in = pb[:].bitcast(F32R).rearrange("p (k t) -> p k t", t=128)
                                return e.copy(ap_out, ap_in)
                            b.emit("scalar", ev4, reads=[B_pb], writes=[B_dst],
                                   accum=(tt + seg + g > 0))

        def stream_quarter(wap, eb, q, i):
            b.emit("sync", lambda e, eb=eb, q=q, i=i: e.dma_start(out=wfq[i][:], in_=wap[eb, q]),
                   reads=[], writes=[B_wf[i]], sem=f"dw{i}", inc_n=16)
            b.emit("scalar", lambda e, i=i: e.copy(whq[i][:], wfq[i][:]),
                   reads=[B_wf[i]], writes=[B_wh[i]])
            b.emit("vector", lambda e, i=i: e.tensor_tensor(wlq[i][:], wfq[i][:],
                                                            whq[i][:].bitcast(F32), OP.subtract),
                   reads=[B_wf[i], B_wh[i]], writes=[B_wl[i]])

        def proj_block(wap, eb, xh, xl, B_xh, B_xl, psum, B_psum):
            for q in range(NQ):
                i = q % 2
                stream_quarter(wap, eb, q, i)

                def mms(e, q=q, i=i):
                    ii = None
                    for cl in range(8):
                        c = q * 8 + cl
                        first = (q == 0 and cl == 0)
                        last = (q == NQ - 1 and cl == 7)
                        wh_ap = whq[i][:, cl * 128:(cl + 1) * 128]
                        wl_ap = wlq[i][:, cl * 128:(cl + 1) * 128]
                        xh_ap = xh[:, c * TH:(c + 1) * TH]
                        xl_ap = xl[:, c * TH:(c + 1) * TH]
                        e.matmul(psum[:], wh_ap, xh_ap, start=first, stop=False)
                        e.matmul(psum[:], wh_ap, xl_ap, start=False, stop=False)
                        ii = e.matmul(psum[:], wl_ap, xh_ap, start=False, stop=last)
                    return ii
                b.emit("tensor", mms, reads=[B_wh[i], B_wl[i], B_xh, B_xl],
                       writes=[B_psum], accum=(q > 0))

        def evac_split_store(psum, B_psum, dsth_ap, dstl_ap, B_dsth, B_dstl,
                             norm_buf, norm_col, first_hl):
            b.emit("scalar", lambda e: e.copy(evh[:], psum[:]), reads=[B_psum], writes=[B_evh])
            b.emit("vector", lambda e: e.tensor_tensor(evl[:], psum[:],
                                                       evh[:].bitcast(F32), OP.subtract),
                   reads=[B_psum, B_evh], writes=[B_evl])
            b.emit("gpsimd", lambda e: e.dma_start(out=dsth_ap, in_=evh[:]),
                   reads=[B_evh], writes=[B_dsth], sem="gd", inc_n=16, accum=not first_hl)
            b.emit("gpsimd", lambda e: e.dma_start(out=dstl_ap, in_=evl[:]),
                   reads=[B_evl], writes=[B_dstl], sem="gd", inc_n=16, accum=not first_hl)
            b.emit("scalar", lambda e: e.activation(sqn[:], evh[:].bitcast(F32), AF.Square,
                                                    scale=1.0),
                   reads=[B_evh], writes=[B_sqn])
            b.emit("tensor", lambda e: e.matmul(ps_nm[:], ones_r[:], sqn[:], start=True, stop=True),
                   reads=[B_sqn, B_onr], writes=[B_pn])
            b.emit("vector", lambda e: e.tensor_reduce(nrm2[:], ps_nm[:], axis=AX.X, op=OP.max),
                   reads=[B_pn], writes=[B_nr2])
            nb, col = norm_buf
            b.emit("vector", lambda e, col=norm_col: e.tensor_tensor(
                nb[:, col:col + 1], nb[:, col:col + 1], nrm2[:], OP.max),
                reads=[B_nr2], writes=[norm_buf[1]] if False else [])

        def norm_rmw(nb_t, B_nb, col):
            b.emit("vector", lambda e, col=col: e.tensor_tensor(
                nb_t[:, col:col + 1], nb_t[:, col:col + 1], nrm2[:], OP.max),
                reads=[B_nr2, B_nb], writes=[B_nb])

        if until_phase >= 1:
            for half in range(2):
                ln_into(x_s, half, lnxh, lnxl, B_lh, B_ll, TH)
                for wname in ("k", "v", "q"):
                    wap = w_hbm[f"w{wname}_b"]
                    for eb in range(EB):
                        psum, B_psum = ps_pr[eb % 2], B_pp[eb % 2]
                        proj_block(wap, eb, lnxh, lnxl, B_lh, B_ll, psum, B_psum)
                        if wname in ("k", "q"):
                            dh = kh_sh if wname == "k" else qh_d
                            dl = kl_sh if wname == "k" else ql_d
                            B_dh = dB["kh_sh"] if wname == "k" else dB["qh_d"]
                            B_dl = dB["kl_sh"] if wname == "k" else dB["ql_d"]
                            first_hl = (half == 0 and eb == 0)
                            b.emit("scalar", lambda e, p=psum: e.copy(evh[:], p[:]),
                                   reads=[B_psum], writes=[B_evh])
                            b.emit("vector", lambda e, p=psum: e.tensor_tensor(
                                evl[:], p[:], evh[:].bitcast(F32), OP.subtract),
                                reads=[B_psum, B_evh], writes=[B_evl])
                            b.emit("gpsimd", lambda e, d=dh, eb=eb, h=half: e.dma_start(
                                out=d[eb * 128:(eb + 1) * 128, h * TH:(h + 1) * TH], in_=evh[:]),
                                reads=[B_evh], writes=[B_dh], sem="gd", inc_n=16,
                                accum=not first_hl)
                            b.emit("gpsimd", lambda e, d=dl, eb=eb, h=half: e.dma_start(
                                out=d[eb * 128:(eb + 1) * 128, h * TH:(h + 1) * TH], in_=evl[:]),
                                reads=[B_evl], writes=[B_dl], sem="gd", inc_n=16,
                                accum=not first_hl)
                            b.emit("scalar", lambda e: e.activation(sqn[:],
                                                                    evh[:].bitcast(F32),
                                                                    AF.Square, scale=1.0),
                                   reads=[B_evh], writes=[B_sqn])
                            b.emit("tensor", lambda e: e.matmul(ps_nm[:], ones_f[:],
                                                                sqn[:],
                                                                start=True, stop=True),
                                   reads=[B_sqn, B_onf], writes=[B_pn])
                            b.emit("vector", lambda e: e.tensor_reduce(nrm2[:], ps_nm[:],
                                                                       axis=AX.X, op=OP.max),
                                   reads=[B_pn], writes=[B_nr2])
                            if wname == "k":
                                norm_rmw(kn_sb, B_kn, eb)
                            else:
                                norm_rmw(qn_sb, B_qn, eb)
                        else:
                            b.emit("scalar", lambda e, p=psum: e.copy(evf[:].bitcast(F32), p[:]),
                                   reads=[B_psum], writes=[B_evf])
                            pb, B_pb = ps_tr[eb % 2], B_pt[eb % 2]

                            def tr4v(e, pb=pb):
                                ii = None
                                for j in range(4):
                                    ii = e.transpose(pb[:, j * 128:(j + 1) * 128].bitcast(F32),
                                                     evf[:, j * 128:(j + 1) * 128].bitcast(F32),
                                                     ident_f[:])
                                return ii
                            b.emit("tensor", tr4v, reads=[B_evf, B_idf], writes=[B_pb])
                            b.emit("vector", lambda e, pb=pb: e.tensor_copy(evn[:], pb[:]),
                                   reads=[B_pb], writes=[B_evn])
                            b.emit("gpsimd", lambda e, eb=eb, h=half: e.dma_start(
                                out=v_sh[h * TH:(h + 1) * TH, eb * 128:(eb + 1) * 128]
                                .rearrange("(j p) c -> p j c", p=128),
                                in_=evn[:].rearrange("p (j c) -> p j c", c=128)),
                                reads=[B_evn], writes=[dB["v_sh"]], sem="gd", inc_n=16,
                                accum=not (half == 0 and eb == 0))
                    if half == 1 and wname == "k":
                        b.emit("gpsimd", lambda e: e.dma_start(out=kn_in, in_=kn_sb[:]),
                               reads=[B_kn], writes=[dB["kn_io"]], sem="gd", inc_n=16)
                        b.emit("gpsimd", lambda e: e.collective_compute(
                            "AllReduce", OP.max, replica_groups=[[0, 1, 2, 3, 4, 5, 6, 7]],
                            ins=[kn_in.opt()], outs=[kn_out.opt()]),
                            reads=[dB["kn_io"]], writes=[dB["kn_io"]], sem="cc")
                        b.emit("gpsimd", lambda e: e.collective_compute(
                            "AllGather", OP.bypass, replica_groups=[[0, 1, 2, 3], [4, 5, 6, 7]],
                            ins=[kh_sh.opt()], outs=[kh_g.opt()]),
                            reads=[dB["kh_sh"]], writes=[dB["kh_g"]], sem="cc")
                        b.emit("gpsimd", lambda e: e.collective_compute(
                            "AllGather", OP.bypass, replica_groups=[[0, 1, 2, 3], [4, 5, 6, 7]],
                            ins=[kl_sh.opt()], outs=[kl_g.opt()]),
                            reads=[dB["kl_sh"]], writes=[dB["kl_g"]], sem="cc")
                    if half == 1 and wname == "v":
                        b.emit("gpsimd", lambda e: e.collective_compute(
                            "AllGather", OP.bypass, replica_groups=[[0, 1, 2, 3], [4, 5, 6, 7]],
                            ins=[v_sh.opt()], outs=[v_g.opt()]),
                            reads=[dB["v_sh"]], writes=[dB["v_g"]], sem="cc")

        if dbg == 1 or until_phase < 2:
            p1.close()
            b.fence()
            for (sap, map_) in dbg_mirrors:
                b.emit("gpsimd", lambda e, s=sap, m=map_: e.dma_start(out=m, in_=s),
                       sem="gd", inc_n=16)
            b.fence()
            b.emit("gpsimd", lambda e: e.engine_nop())
            b.finalize()
            return nc

        p1.close()
        b.fence()
        p2 = es.enter_context(ExitStack())
        khb, klb, vbb, qhb, qlb = [], [], [], [], []
        B_khb, B_klb, B_vbb, B_qhb, B_qlb = [], [], [], [], []
        for i in range(2):
            t1, b1 = sb(f"khb{i}", [128, S], F32R, p2); khb.append(t1); B_khb.append(b1)
            t2, b2 = sb(f"klb{i}", [128, S], F32R, p2); klb.append(t2); B_klb.append(b2)
            t3, b3 = sb(f"vbb{i}", [128, 32 * 129], F32, p2); vbb.append(t3); B_vbb.append(b3)
            t4, b4 = sb(f"qhb{i}", [128, T], F32R, p2); qhb.append(t4); B_qhb.append(b4)
            t5, b5 = sb(f"qlb{i}", [128, T], F32R, p2); qlb.append(t5); B_qlb.append(b5)
        ptile, B_ptile = zip(*[sb(f"ptile{i}", [128, TH], F32, p2) for i in range(3)])
        otm, B_otm = sb("otm", [128, TH], F32, p2)
        ohh, B_ohh = sb("ohh", [128, TH], F32R, p2)
        ohl, B_ohl = sb("ohl", [128, TH], F32R, p2)
        rz_t, B_rz = sb("rz_t", [128, 4], F32, p2)
        kng, B_kng = sb("kng", [1, H], F32, p2)
        mh_t, B_mh = sb("mh_t", [1, 1], F32, p2)
        mhb, B_mhb = sb("mhb", [128, 1], F32, p2)
        ps_s, B_ps = zip(*[pst(f"ps_s{i}", [128, TH], p2) for i in range(2)])
        ps_av, B_pav = zip(*[pst(f"ps_av{i}", [128, 129], p2) for i in range(4)])
        ps_ot, B_pot = pst("ps_ot", [128, TH], p2)
        ps_bc2, B_pbc2 = pst("ps_bc2", [128, 1], p2)

        for i in range(2):
            b.emit("vector", lambda e, i=i: e.memset(
                vbb[i][:].rearrange("p (c j) -> p c j", j=129)[:, :, 128:129], 1.0),
                writes=[B_vbb[i]])
        b.emit("sync", lambda e: e.dma_start(out=kng[:], in_=kn_out),
               reads=[dB["kn_io"]], writes=[B_kng], sem="dkg", inc_n=16)

        def load_head(h, i):
            for r in range(4):
                b.emit("sync", lambda e, h=h, i=i, r=r: e.dma_start(
                    out=khb[i][:, r * T:(r + 1) * T],
                    in_=kh_g[r * D + h * 128: r * D + (h + 1) * 128, :]),
                    reads=[dB["kh_g"]], writes=[B_khb[i]], sem=f"ka{i}", inc_n=16,
                    accum=(r > 0))
                b.emit("sync", lambda e, h=h, i=i, r=r: e.dma_start(
                    out=klb[i][:, r * T:(r + 1) * T],
                    in_=kl_g[r * D + h * 128: r * D + (h + 1) * 128, :]),
                    reads=[dB["kl_g"]], writes=[B_klb[i]], sem=f"kb{i}", inc_n=16,
                    accum=(r > 0))
            b.emit("sync", lambda e, h=h, i=i: e.dma_start(
                out=qhb[i][:], in_=qh_d[h * 128:(h + 1) * 128, :]),
                reads=[dB["qh_d"]], writes=[B_qhb[i]], sem=f"qa{i}", inc_n=16)
            b.emit("sync", lambda e, h=h, i=i: e.dma_start(
                out=qlb[i][:], in_=ql_d[h * 128:(h + 1) * 128, :]),
                reads=[dB["ql_d"]], writes=[B_qlb[i]], sem=f"qb{i}", inc_n=16)

            def ldv(e, h=h, i=i):
                return e.dma_start(
                    out=vbb[i][:].rearrange("p (c j) -> p c j", j=129)[:, :, 0:128],
                    in_=v_g[:, h * 128:(h + 1) * 128].rearrange("(c p) j -> p c j", p=128))
            b.emit("sync", ldv, reads=[dB["v_g"]], writes=[B_vbb[i]], sem=f"va{i}",
                   inc_n=16)

        def emit_mh(h):
            b.emit("vector", lambda e, h=h: e.tensor_tensor(
                mh_t[:], qn_sb[:, h:h + 1], kng[:, h:h + 1], OP.mult),
                reads=[B_qn, B_kng], writes=[B_mh])
            b.emit("scalar", lambda e: e.activation(mh_t[:], mh_t[:], AF.Sqrt,
                                                    scale=1.0002 / 128.0),
                   reads=[B_mh], writes=[B_mh])
            b.emit("vector", lambda e: e.tensor_scalar(mh_t[:], mh_t[:], -1.0, None, op0=OP.mult),
                   reads=[B_mh], writes=[B_mh])
            b.emit("tensor", lambda e: e.matmul(ps_bc2[:], onesc[:], mh_t[:],
                                                start=True, stop=True),
                   reads=[B_mh, B_onc], writes=[B_pbc2])
            b.emit("vector", lambda e: e.tensor_copy(mhb[:], ps_bc2[:]),
                   reads=[B_pbc2], writes=[B_mhb])

        def emit_smm(i, qi, c):
            sps, B_sps = ps_s[c % 2], B_ps[c % 2]

            def smm(e, i=i, c=c, qi=qi, sps=sps):
                e.matmul(sps[:], khb[i][:, c * 128:(c + 1) * 128],
                         qhb[i][:, qi * TH:(qi + 1) * TH], start=True, stop=False)
                e.matmul(sps[:], khb[i][:, c * 128:(c + 1) * 128],
                         qlb[i][:, qi * TH:(qi + 1) * TH], start=False, stop=False)
                return e.matmul(sps[:], klb[i][:, c * 128:(c + 1) * 128],
                                qhb[i][:, qi * TH:(qi + 1) * TH], start=False, stop=True)
            b.emit("tensor", smm, reads=[B_khb[i], B_klb[i], B_qhb[i], B_qlb[i]],
                   writes=[B_sps])

        def emit_exp(c):
            sps, B_sps = ps_s[c % 2], B_ps[c % 2]
            pt, B_pt2 = ptile[c % 3], B_ptile[c % 3]
            b.emit("scalar", lambda e, sps=sps, pt=pt: e.activation(
                pt[:], sps[:], AF.Exp, bias=mhb[:], scale=INV_SQRT_DK),
                reads=[B_sps, B_mhb], writes=[B_pt2])

        def emit_avm(i, c):
            pt, B_pt2 = ptile[c % 3], B_ptile[c % 3]

            def avm(e, i=i, c=c, pt=pt):
                ii = None
                for j in range(4):
                    ii = e.matmul(ps_av[j][:], pt[:, j * 128:(j + 1) * 128],
                                  vbb[i][:, c * 129:(c + 1) * 129],
                                  start=(c == 0), stop=(c == 31))
                return ii
            b.emit("tensor", avm, reads=[B_pt2, B_vbb[i]],
                   writes=list(B_pav), accum=(c > 0))

        def emit_evac(h, qi):
            for j in range(4):
                b.emit("vector", lambda e, j=j: e.reciprocal(rz_t[:, j:j + 1],
                                                             ps_av[j][:, 128:129]),
                       reads=[B_pav[j]], writes=[B_rz], accum=(j > 0))
                b.emit("vector", lambda e, j=j: e.tensor_scalar(
                    otm[:, j * 128:(j + 1) * 128], ps_av[j][:, 0:128],
                    rz_t[:, j:j + 1], None, op0=OP.mult),
                    reads=[B_pav[j], B_rz], writes=[B_otm], accum=(j > 0))

            def tr4o(e):
                ii = None
                for j in range(4):
                    ii = e.transpose(ps_ot[:, j * 128:(j + 1) * 128].bitcast(F32),
                                     otm[:, j * 128:(j + 1) * 128], ident_f[:])
                return ii
            b.emit("tensor", tr4o, reads=[B_otm, B_idf], writes=[B_pot])
            b.emit("scalar", lambda e: e.copy(ohh[:], ps_ot[:]), reads=[B_pot], writes=[B_ohh])
            b.emit("vector", lambda e: e.tensor_tensor(ohl[:], ps_ot[:],
                                                       ohh[:].bitcast(F32), OP.subtract),
                   reads=[B_pot, B_ohh], writes=[B_ohl])
            b.emit("gpsimd", lambda e, h=h, qi=qi: e.dma_start(
                out=oh_d[h * 128:(h + 1) * 128, qi * TH:(qi + 1) * TH], in_=ohh[:]),
                reads=[B_ohh], writes=[dB["oh_d"]], sem="gd", inc_n=16,
                accum=not (h == 0 and qi == 0))
            b.emit("gpsimd", lambda e, h=h, qi=qi: e.dma_start(
                out=ol_d[h * 128:(h + 1) * 128, qi * TH:(qi + 1) * TH], in_=ohl[:]),
                reads=[B_ohl], writes=[dB["ol_d"]], sem="gd", inc_n=16,
                accum=not (h == 0 and qi == 0))

        load_head(0, 0)
        pending_evac = None
        for h in range(H):
            i = h % 2
            if h + 1 < H:
                load_head(h + 1, (h + 1) % 2)
            emit_mh(h)
            for qi in range(2):
                for c in range(32):
                    emit_smm(i, qi, c)
                    if c == 0:
                        emit_exp(c)
                        if pending_evac is not None:
                            emit_evac(*pending_evac)
                            pending_evac = None
                    else:
                        emit_avm(i, c - 1)
                        emit_exp(c)
                emit_avm(i, 31)
                pending_evac = (h, qi)
        emit_evac(*pending_evac)

        if dbg == 2 or until_phase < 3:
            p2.close()
            b.fence()
            for (sap, map_) in dbg_mirrors:
                b.emit("gpsimd", lambda e, s=sap, m=map_: e.dma_start(out=m, in_=s),
                       sem="gd", inc_n=16)
            b.fence()
            b.emit("gpsimd", lambda e: e.engine_nop())
            b.finalize()
            return nc

        p2.close()
        b.fence()
        p3 = es.enter_context(ExitStack())
        o_hh, _Boh0 = sb("o_hh", [128, DC * TH], F32R, p3)
        o_ll, _Bol0 = sb("o_ll", [128, DC * TH], F32R, p3)
        B_oh = [Buf() for _ in range(NQ)]
        B_ol = [Buf() for _ in range(NQ)]
        wfq3, whq3, wlq3, B_wf3, B_wh3, B_wl3 = [], [], [], [], [], []
        for i in range(2):
            t1, b1 = sb(f"wfq3{i}", [128, 1024], F32, p3)
            t2, b2 = sb(f"whq3{i}", [128, 1024], F32R, p3)
            t3, b3 = sb(f"wlq3{i}", [128, 1024], F32R, p3)
            wfq3.append(t1); whq3.append(t2); wlq3.append(t3)
            B_wf3.append(b1); B_wh3.append(b2); B_wl3.append(b3)
        of_t, B_oft = sb("of_t", [128, TH], F32, p3)
        x_t4, B_xt4 = sb("x_t4", [128, TH], F32, p3)
        x1_t, B_x1t = sb("x1_t", [128, TH], F32, p3)
        ps_pr3, B_pp3 = zip(*[pst(f"ps_pr3{i}", [128, TH], p3) for i in range(2)])
        ps_tr3, B_pt3 = zip(*[pst(f"ps_tr3{i}", [128, TH], p3) for i in range(2)])

        def proj_block3(wap, eb, xh, xl, B_xh, B_xl, psum, B_psum,
                        wf, wh, wl, B_wfL, B_whL, B_wlL):
            for q in range(NQ):
                i = q % 2
                b.emit("sync", lambda e, eb=eb, q=q, i=i: e.dma_start(out=wf[i][:], in_=wap[eb, q]),
                       writes=[B_wfL[i]], sem=f"dw3{i}", inc_n=16)
                b.emit("scalar", lambda e, i=i: e.copy(wh[i][:], wf[i][:]),
                       reads=[B_wfL[i]], writes=[B_whL[i]])
                b.emit("vector", lambda e, i=i: e.tensor_tensor(wl[i][:], wf[i][:],
                                                                wh[i][:].bitcast(F32), OP.subtract),
                       reads=[B_wfL[i], B_whL[i]], writes=[B_wlL[i]])

                def mms(e, q=q, i=i):
                    ii = None
                    for cl in range(8):
                        c = q * 8 + cl
                        first = (q == 0 and cl == 0)
                        last = (q == NQ - 1 and cl == 7)
                        wh_ap = wh[i][:, cl * 128:(cl + 1) * 128]
                        wl_ap = wl[i][:, cl * 128:(cl + 1) * 128]
                        xh_ap = xh[:, c * TH:(c + 1) * TH]
                        xl_ap = xl[:, c * TH:(c + 1) * TH]
                        e.matmul(psum[:], wh_ap, xh_ap, start=first, stop=False)
                        e.matmul(psum[:], wh_ap, xl_ap, start=False, stop=False)
                        ii = e.matmul(psum[:], wl_ap, xh_ap, start=False, stop=last)
                    return ii
                b.emit("tensor", mms, reads=[B_whL[i], B_wlL[i], B_xh[q], B_xl[q]],
                       writes=[B_psum], accum=(q > 0))

        def load_act_pieces(dsth, dstl, B_h, B_l, srch, srcl, B_srch, B_srcl,
                            half, sem_pfx):
            for q in range(NQ):
                for (dst, src, B_dst, B_src) in ((dsth, srch, B_h, B_srch),
                                                 (dstl, srcl, B_l, B_srcl)):
                    b.emit("sync", lambda e, q=q, h=half, dst=dst, src=src: e.dma_start(
                        out=dst[:, q * 8 * TH:(q + 1) * 8 * TH]
                        .rearrange("p (c t) -> p c t", t=TH),
                        in_=src[q * 8 * 128:(q + 1) * 8 * 128, h * TH:(h + 1) * TH]
                        .rearrange("(c p) t -> p c t", p=128)),
                        reads=[B_src], writes=[B_dst[q]], sem=f"{sem_pfx}{q}", inc_n=16)

        for half in range(2):
            load_act_pieces(o_hh, o_ll, B_oh, B_ol, oh_d, ol_d,
                            dB["oh_d"], dB["ol_d"], half, "do")
            for eb in range(EB):
                psum, B_psum = ps_pr3[eb % 2], B_pp3[eb % 2]
                proj_block3(w_hbm["wo_b"], eb, o_hh, o_ll, B_oh, B_ol, psum, B_psum,
                            wfq3, whq3, wlq3, B_wf3, B_wh3, B_wl3)
                b.emit("scalar", lambda e, p=psum: e.copy(of_t[:], p[:]),
                       reads=[B_psum], writes=[B_oft])
                b.emit("sync", lambda e, eb=eb, h=half: e.dma_start(
                    out=x_t4[:].rearrange("p (s j) -> p s j", j=128),
                    in_=x_s[h * TH:(h + 1) * TH, eb * 128:(eb + 1) * 128]
                    .rearrange("(s p) j -> p s j", p=128)),
                    writes=[B_xt4], sem="dx4", inc_n=16)
                pb, B_pb = ps_tr3[eb % 2], B_pt3[eb % 2]

                def tr4p(e, pb=pb):
                    ii = None
                    for j in range(4):
                        ii = e.transpose(pb[:, j * 128:(j + 1) * 128].bitcast(F32),
                                         of_t[:, j * 128:(j + 1) * 128], ident_f[:])
                    return ii
                b.emit("tensor", tr4p, reads=[B_oft, B_idf], writes=[B_pb])
                b.emit("vector", lambda e, pb=pb: e.tensor_tensor(x1_t[:], pb[:], x_t4[:], OP.add),
                       reads=[B_pb, B_xt4], writes=[B_x1t])
                b.emit("gpsimd", lambda e, eb=eb, h=half: e.dma_start(
                    out=x1_d[h * TH:(h + 1) * TH, eb * 128:(eb + 1) * 128]
                    .rearrange("(s p) j -> p s j", p=128),
                    in_=x1_t[:].rearrange("p (s j) -> p s j", j=128)),
                    reads=[B_x1t], writes=[dB["x1_d"]], sem="gd", inc_n=16,
                    accum=not (half == 0 and eb == 0))

        p3.close()
        b.fence()
        p3 = es.enter_context(ExitStack())
        ps_tr3, B_pt3 = zip(*[pst(f"ps_trb{i}", [128, TH], p3) for i in range(2)])
        cent_f, B_ctf = sb("cent_f", [128, DC * NCL], F32, p3)
        cent_h, B_cth = sb("cent_h", [128, DC * NCL], F32R, p3)
        cent_l, B_ctl = sb("cent_l", [128, DC * NCL], F32R, p3)
        b.emit("sync", lambda e: e.dma_start(out=cent_f[:], in_=cent_in),
               writes=[B_ctf], sem="dct", inc_n=16)
        b.emit("scalar", lambda e: e.copy(cent_h[:], cent_f[:]), reads=[B_ctf], writes=[B_cth])
        b.emit("vector", lambda e: e.tensor_tensor(cent_l[:], cent_f[:],
                                                   cent_h[:].bitcast(F32), OP.subtract),
               reads=[B_ctf, B_cth], writes=[B_ctl])
        ln2h_t, B_l2h = sb("ln2h_t", [128, DC * 128], F32R, p3)
        ln2l_t, B_l2l = sb("ln2l_t", [128, DC * 128], F32R, p3)
        xt2, B_xt2 = sb("xt2", [128, D], F32, p3)
        l2h_tm, B_h2 = sb("l2h_tm", [128, D], F32R, p3)
        l2l_tm, B_l2 = sb("l2l_tm", [128, D], F32R, p3)
        st2, B_st2 = sb("st2", [128, 48], F32, p3)
        mv2, B_mv2 = sb("mv2", [128, 2], F32, p3)
        sd2, B_sd2 = sb("sd2", [128, 1], F32, p3)
        rs2, B_rs2 = sb("rs2", [128, 1], F32, p3)
        nm2, B_nm2 = sb("nm2", [128, 1], F32, p3)
        dex, B_dex = sb("dex", [128, NCL], F32, p3)
        dmx, B_dmx = sb("dmx", [128, 1], F32, p3)
        dnb, B_dnb = sb("dnb", [128, 1], F32, p3)
        dzs, B_dzs = sb("dzs", [128, 1], F32, p3)
        dth, B_dth = sb("dth", [128, 1], F32, p3)
        hits, B_hit = sb("hits", [128, NCL], F32R, p3)
        ps_dt, B_pdt = pst("ps_dt", [128, NCL], p3)
        ps_ct, B_pct = pst("ps_ct", [1, NCL], p3)

        for ti in range(8):
            b.emit("sync", lambda e, ti=ti: e.dma_start(out=xt2[:],
                                                        in_=x1_d[ti * 128:(ti + 1) * 128, :]),
                   reads=[dB["x1_d"]], writes=[B_xt2], sem="dx", inc_n=16)

            def bn2(e):
                ii = None
                for i in range(8):
                    ii = e.bn_stats(st2[:, i * 6:(i + 1) * 6], xt2[:, i * 512:(i + 1) * 512])
                return ii
            b.emit("vector", bn2, reads=[B_xt2], writes=[B_st2])
            b.emit("vector", lambda e: e.bn_aggr(mv2[:], st2[:]), reads=[B_st2], writes=[B_mv2])
            b.emit("scalar", lambda e: e.activation(sd2[:], mv2[:, 1:2], AF.Sqrt,
                                                    bias=eps_t[:], scale=1.0),
                   reads=[B_mv2, B_eps], writes=[B_sd2])
            b.emit("vector", lambda e: e.reciprocal(rs2[:], sd2[:]), reads=[B_sd2], writes=[B_rs2])
            b.emit("vector", lambda e: e.tensor_scalar(nm2[:], mv2[:, 0:1], rs2[:], -1.0,
                                                       op0=OP.mult, op1=OP.mult),
                   reads=[B_mv2, B_rs2], writes=[B_nm2])
            b.emit("scalar", lambda e: e.activation(l2h_tm[:], xt2[:], AF.Identity,
                                                    bias=nm2[:], scale=rs2[:]),
                   reads=[B_xt2, B_nm2, B_rs2], writes=[B_h2])
            b.emit("vector", lambda e: e.scalar_tensor_tensor(
                l2l_tm[:], xt2[:], rs2[:], l2h_tm[:].bitcast(F32), op0=OP.mult, op1=OP.subtract),
                reads=[B_xt2, B_rs2, B_h2], writes=[B_l2])
            b.emit("vector", lambda e: e.tensor_scalar(l2l_tm[:], l2l_tm[:], nm2[:], None,
                                                       op0=OP.add),
                   reads=[B_nm2], writes=[B_l2])
            for (src, B_src, dst, B_dst) in ((l2h_tm, B_h2, ln2h_t, B_l2h),
                                             (l2l_tm, B_l2, ln2l_t, B_l2l)):
                for g in range(8):
                    pb, B_pb2 = ps_tr3[g % 2], B_pt3[g % 2]

                    def tr4l(e, src=src, g=g, pb=pb):
                        ii = None
                        for k in range(4):
                            c = g * 4 + k
                            ii = e.transpose(pb[:, k * 128:(k + 1) * 128].bitcast(F32R),
                                             src[:, c * 128:(c + 1) * 128], ident_r[:])
                        return ii
                    b.emit("tensor", tr4l, reads=[B_src, B_idr], writes=[B_pb2])
                    b.emit("scalar", lambda e, dst=dst, g=g, pb=pb: e.copy(
                        dst[:, g * 512:(g + 1) * 512], pb[:].bitcast(F32R)),
                        reads=[B_pb2], writes=[B_dst], accum=(g > 0))
            b.emit("gpsimd", lambda e, ti=ti: e.dma_start(
                out=lnh_d[:, ti * 128:(ti + 1) * 128].rearrange("(c p) t -> p c t", p=128),
                in_=ln2h_t[:].rearrange("p (c t) -> p c t", t=128)),
                reads=[B_l2h], writes=[dB["lnh_d"]], sem="gd", inc_n=16, accum=(ti > 0))
            b.emit("gpsimd", lambda e, ti=ti: e.dma_start(
                out=lnl_d[:, ti * 128:(ti + 1) * 128].rearrange("(c p) t -> p c t", p=128),
                in_=ln2l_t[:].rearrange("p (c t) -> p c t", t=128)),
                reads=[B_l2l], writes=[dB["lnl_d"]], sem="gd", inc_n=16, accum=(ti > 0))

            def dotsmm(e):
                ii = None
                for c in range(DC):
                    first = (c == 0)
                    last = (c == DC - 1)
                    e.matmul(ps_dt[:], ln2h_t[:, c * 128:(c + 1) * 128],
                             cent_h[:, c * NCL:(c + 1) * NCL], start=first, stop=False)
                    e.matmul(ps_dt[:], ln2h_t[:, c * 128:(c + 1) * 128],
                             cent_l[:, c * NCL:(c + 1) * NCL], start=False, stop=False)
                    ii = e.matmul(ps_dt[:], ln2l_t[:, c * 128:(c + 1) * 128],
                                  cent_h[:, c * NCL:(c + 1) * NCL], start=False, stop=last)
                return ii
            b.emit("tensor", dotsmm, reads=[B_l2h, B_l2l, B_cth, B_ctl], writes=[B_pdt])
            b.emit("vector", lambda e: e.tensor_reduce(dmx[:], ps_dt[:], axis=AX.X, op=OP.max),
                   reads=[B_pdt], writes=[B_dmx])
            b.emit("vector", lambda e: e.tensor_scalar(dnb[:], dmx[:], -10.0, None, op0=OP.mult),
                   reads=[B_dmx], writes=[B_dnb])
            b.emit("scalar", lambda e: e.activation(dex[:], ps_dt[:], AF.Exp,
                                                    bias=dnb[:], scale=10.0, accum_out=dzs[:]),
                   reads=[B_pdt, B_dnb], writes=[B_dex, B_dzs])
            b.emit("vector", lambda e: e.tensor_scalar(dth[:], dzs[:], 0.01, None, op0=OP.mult),
                   reads=[B_dzs], writes=[B_dth])
            b.emit("vector", lambda e: e.tensor_scalar(hits[:], dex[:], dth[:], None, op0=OP.is_gt),
                   reads=[B_dex, B_dth], writes=[B_hit])
            b.emit("vector", lambda e, ti=ti: e.tensor_reduce(
                qsel_sb[:, ti:ti + 1], hits[:].bitcast(F32), axis=AX.X, op=OP.max),
                reads=[B_hit], writes=[B_qs], accum=(ti > 0))
            b.emit("tensor", lambda e: e.matmul(ps_ct[:], ones_r[:], hits[:],
                                                start=True, stop=True),
                   reads=[B_hit, B_onr], writes=[B_pct])
            b.emit("vector", lambda e: e.tensor_tensor(cnt_sb[:], cnt_sb[:], ps_ct[:], OP.add),
                   reads=[B_pct, B_cnt], writes=[B_cnt])

        b.emit("gpsimd", lambda e: e.dma_start(out=cnt_in, in_=cnt_sb[:]),
               reads=[B_cnt], writes=[dB["cnt_io"]], sem="gd", inc_n=16)
        b.emit("gpsimd", lambda e: e.collective_compute(
            "AllReduce", OP.add, replica_groups=[[0, 1, 2, 3, 4, 5, 6, 7]],
            ins=[cnt_in.opt()], outs=[cnt_out.opt()]),
            reads=[dB["cnt_io"]], writes=[dB["cnt_io"]], sem="cc")

        if dbg == 3 or until_phase < 4:
            p3.close()
            b.fence()
            for (sap, map_) in dbg_mirrors:
                b.emit("gpsimd", lambda e, s=sap, m=map_: e.dma_start(out=m, in_=s),
                       sem="gd", inc_n=16)
            b.fence()
            b.emit("gpsimd", lambda e: e.engine_nop())
            b.finalize()
            return nc

        p3.close()
        b.fence()
        p4 = es.enter_context(ExitStack())
        bias_b, B_bb = sb("bias_b", [128, D], F32, p4)
        rsel_b, B_rb = sb("rsel_b", [128, D], F32, p4)
        p4a = es.enter_context(ExitStack())
        bias_r, B_br = sb("bias_r", [1, D], F32, p4a)
        asg_s, B_as = sb("asg_s", [1, D], I32, p4a)
        asg_f, B_af = sb("asg_f", [1, D], F32, p4a)
        asg_b, B_ab = sb("asg_b", [64, D], F32, p4a)
        i64, B_i64 = sb("i64", [64, 1], I32, p4a)
        i64f, B_i64f = sb("i64f", [64, 1], F32, p4a)
        onehot, B_oh1 = sb("onehot", [64, D], F32R, p4a)
        cnt_t, B_ct2 = sb("cnt_t", [64, 1], F32, p4a)
        csel, B_cs = sb("csel", [64, 1], F32R, p4a)
        rsel_r, B_rr = sb("rsel_r", [1, D], F32, p4a)
        ps_rs, B_prs = pst("ps_rs", [1, TH], p4a)
        ps_bc4, B_pbc4 = pst("ps_bc4", [128, TH], p4a)

        def bcast_mm(dst, B_dst, row_t, B_row, nparts, ncols):
            for q0 in range(0, ncols, TH):
                b.emit("tensor", lambda e, q0=q0: e.matmul(
                    ps_bc4[0:nparts, 0:TH], onesc[:, 0:nparts], row_t[:, q0:q0 + TH],
                    start=True, stop=True),
                    reads=[B_row, B_onc], writes=[B_pbc4])
                b.emit("vector", lambda e, q0=q0: e.tensor_copy(
                    dst[0:nparts, q0:q0 + TH], ps_bc4[0:nparts, 0:TH]),
                    reads=[B_pbc4], writes=[B_dst], accum=(q0 > 0))

        b.emit("sync", lambda e: e.dma_start(out=bias_r[:], in_=bias_in),
               writes=[B_br], sem="db1", inc_n=16)
        b.emit("sync", lambda e: e.dma_start(out=asg_s[:], in_=asg_in),
               writes=[B_as], sem="db2", inc_n=16)
        bcast_mm(bias_b, B_bb, bias_r, B_br, 128, D)
        b.emit("vector", lambda e: e.tensor_copy(asg_f[:], asg_s[:]), reads=[B_as], writes=[B_af])
        bcast_mm(asg_b, B_ab, asg_f, B_af, 64, D)
        b.emit("gpsimd", lambda e: e.iota(i64[:], pattern=[[0, 1]], base=0, channel_multiplier=1),
               writes=[B_i64])
        b.emit("vector", lambda e: e.tensor_copy(i64f[:], i64[:]), reads=[B_i64], writes=[B_i64f])
        b.emit("vector", lambda e: e.tensor_scalar(onehot[:], asg_b[:], i64f[:], None,
                                                   op0=OP.is_equal),
               reads=[B_ab, B_i64f], writes=[B_oh1])
        b.emit("sync", lambda e: e.dma_start(out=cnt_t[:], in_=cnt_out.rearrange("a b -> b a")),
               reads=[dB["cnt_io"]], writes=[B_ct2], sem="db3", inc_n=16)
        b.emit("vector", lambda e: e.tensor_scalar(csel[:], cnt_t[:], 0.5, None, op0=OP.is_gt),
               reads=[B_ct2], writes=[B_cs])
        for q in range(8):
            b.emit("tensor", lambda e, q=q: e.matmul(
                ps_rs[:], csel[:], onehot[:, q * TH:(q + 1) * TH], start=True, stop=True),
                reads=[B_cs, B_oh1], writes=[B_prs])
            b.emit("vector", lambda e, q=q: e.tensor_copy(rsel_r[:, q * TH:(q + 1) * TH], ps_rs[:]),
                   reads=[B_prs], writes=[B_rr], accum=(q > 0))
        bcast_mm(rsel_b, B_rb, rsel_r, B_rr, 128, D)

        p4a.close()
        b.fence()
        l1h, _B1h0 = sb("l1h", [128, DC * TH], F32R, p4)
        l1l, _B1l0 = sb("l1l", [128, DC * TH], F32R, p4)
        B_1h = [Buf() for _ in range(NQ)]
        B_1l = [Buf() for _ in range(NQ)]
        wfq4, whq4, wlq4, B_wf4, B_wh4, B_wl4 = [], [], [], [], [], []
        for i in range(2):
            t1, b1 = sb(f"wfq4{i}", [128, 1024], F32, p4)
            t2, b2 = sb(f"whq4{i}", [128, 1024], F32R, p4)
            t3, b3 = sb(f"wlq4{i}", [128, 1024], F32R, p4)
            wfq4.append(t1); whq4.append(t2); wlq4.append(t3)
            B_wf4.append(b1); B_wh4.append(b2); B_wl4.append(b3)
        ff_t, B_fft = sb("ff_t", [128, TH], F32, p4)
        x1r_t, B_x1r = sb("x1r_t", [128, TH], F32, p4)
        out_t, B_out4 = sb("out_t", [128, TH], F32, p4)
        B_outs = [Buf() for _ in range(4)]
        ps_pr4, B_pp4 = zip(*[pst(f"ps_pr4{i}", [128, TH], p4) for i in range(2)])
        ps_tr4, B_pt4 = zip(*[pst(f"ps_tr4{i}", [128, TH], p4) for i in range(2)])

        for half in range(2):
            load_act_pieces(l1h, l1l, B_1h, B_1l, lnh_d, lnl_d,
                            dB["lnh_d"], dB["lnl_d"], half, "dl")
            for eb in range(EB):
                psum, B_psum = ps_pr4[eb % 2], B_pp4[eb % 2]
                for q in range(NQ):
                    i = q % 2
                    b.emit("sync", lambda e, eb=eb, q=q, i=i: e.dma_start(
                        out=wfq4[i][:], in_=w_hbm["wf_b"][eb, q]),
                        writes=[B_wf4[i]], sem=f"dw{i}", inc_n=16)
                    b.emit("scalar", lambda e, i=i: e.copy(whq4[i][:], wfq4[i][:]),
                           reads=[B_wf4[i]], writes=[B_wh4[i]])
                    b.emit("vector", lambda e, i=i: e.tensor_tensor(
                        wlq4[i][:], wfq4[i][:], whq4[i][:].bitcast(F32), OP.subtract),
                        reads=[B_wf4[i], B_wh4[i]], writes=[B_wl4[i]])

                    def mms4(e, q=q, i=i, psum=psum):
                        ii = None
                        for cl in range(8):
                            c = q * 8 + cl
                            first = (q == 0 and cl == 0)
                            last = (q == NQ - 1 and cl == 7)
                            wh_ap = whq4[i][:, cl * 128:(cl + 1) * 128]
                            wl_ap = wlq4[i][:, cl * 128:(cl + 1) * 128]
                            xh_ap = l1h[:, c * TH:(c + 1) * TH]
                            xl_ap = l1l[:, c * TH:(c + 1) * TH]
                            e.matmul(psum[:], wh_ap, xh_ap, start=first, stop=False)
                            e.matmul(psum[:], wh_ap, xl_ap, start=False, stop=False)
                            ii = e.matmul(psum[:], wl_ap, xh_ap, start=False, stop=last)
                        return ii
                    b.emit("tensor", mms4, reads=[B_wh4[i], B_wl4[i], B_1h[q], B_1l[q]],
                           writes=[B_psum], accum=(q > 0))
                b.emit("scalar", lambda e, p=psum: e.copy(ff_t[:], p[:]),
                       reads=[B_psum], writes=[B_fft])
                b.emit("sync", lambda e, eb=eb, h=half: e.dma_start(
                    out=x1r_t[:].rearrange("p (s j) -> p s j", j=128),
                    in_=x1_d[h * TH:(h + 1) * TH, eb * 128:(eb + 1) * 128]
                    .rearrange("(s p) j -> p s j", p=128)),
                    reads=[dB["x1_d"]], writes=[B_x1r], sem="dx4", inc_n=16)
                pb, B_pb4 = ps_tr4[eb % 2], B_pt4[eb % 2]

                def tr4f(e, pb=pb):
                    ii = None
                    for j in range(4):
                        ii = e.transpose(pb[:, j * 128:(j + 1) * 128].bitcast(F32),
                                         ff_t[:, j * 128:(j + 1) * 128], ident_f[:])
                    return ii
                b.emit("tensor", tr4f, reads=[B_fft, B_idf], writes=[B_pb4])
                for j in range(4):
                    b.emit("vector", lambda e, j=j, pb=pb, eb=eb: e.tensor_tensor(
                        out_t[:, j * 128:(j + 1) * 128], pb[:, j * 128:(j + 1) * 128],
                        bias_b[:, eb * 128:(eb + 1) * 128], OP.add),
                        reads=[B_pb4, B_bb], writes=[B_outs[j]])
                    b.emit("vector", lambda e, j=j, eb=eb: e.tensor_tensor(
                        out_t[:, j * 128:(j + 1) * 128], out_t[:, j * 128:(j + 1) * 128],
                        rsel_b[:, eb * 128:(eb + 1) * 128], OP.mult),
                        reads=[B_rb, B_outs[j]], writes=[B_outs[j]])
                    b.emit("vector", lambda e, j=j, h=half: e.scalar_tensor_tensor(
                        out_t[:, j * 128:(j + 1) * 128], out_t[:, j * 128:(j + 1) * 128],
                        qsel_sb[:, h * 4 + j:h * 4 + j + 1], x1r_t[:, j * 128:(j + 1) * 128],
                        op0=OP.mult, op1=OP.add),
                        reads=[B_qs, B_x1r, B_outs[j]], writes=[B_outs[j]])
                b.emit("gpsimd", lambda e, eb=eb, h=half: e.dma_start(
                    out=y_s[h * TH:(h + 1) * TH, eb * 128:(eb + 1) * 128]
                    .rearrange("(s p) j -> p s j", p=128),
                    in_=out_t[:].rearrange("p (s j) -> p s j", j=128)),
                    reads=B_outs, writes=[dB["y"]], sem="gd", inc_n=16,
                    accum=not (half == 0 and eb == 0))

        p4.close()
        b.fence()
        for (sap, map_) in dbg_mirrors:
            b.emit("gpsimd", lambda e, s=sap, m=map_: e.dma_start(out=m, in_=s),
                   sem="gd", inc_n=16)
        if dbg_mirrors:
            b.fence()
        b.emit("gpsimd", lambda e: e.engine_nop())
        b.finalize()
    return nc


_NC_CACHE = {}


def _get_program():
    if "nc" not in _NC_CACHE:
        _NC_CACHE["nc"] = build_program()
    return _NC_CACHE["nc"]


def _block_weights(w):
    wt = np.ascontiguousarray(w.T)
    v = wt.reshape(NQ, 8, 128, EB, 128)
    v = v.transpose(3, 0, 2, 1, 4)
    return np.ascontiguousarray(v.reshape(EB, NQ, 128, 1024))


def kernel(L0_attn_x, L0_attn_w_q, L0_attn_w_k, L0_attn_w_v, L0_attn_w_o,
           L0_ffn_weight, L0_ffn_bias, L0_ffn_centroids, L0_ffn_assignments):
    x = np.ascontiguousarray(np.asarray(L0_attn_x, dtype=np.float32))
    wq = _block_weights(np.asarray(L0_attn_w_q, dtype=np.float32))
    wk = _block_weights(np.asarray(L0_attn_w_k, dtype=np.float32))
    wv = _block_weights(np.asarray(L0_attn_w_v, dtype=np.float32))
    wo = _block_weights(np.asarray(L0_attn_w_o, dtype=np.float32))
    wf = _block_weights(np.asarray(L0_ffn_weight, dtype=np.float32))
    ct = np.ascontiguousarray(np.asarray(L0_ffn_centroids, dtype=np.float32).T)
    cb = np.ascontiguousarray(ct.reshape(DC, 128, NCL).transpose(1, 0, 2)
                              .reshape(128, DC * NCL))
    bias = np.ascontiguousarray(np.asarray(L0_ffn_bias, dtype=np.float32).reshape(1, D))
    asg = np.ascontiguousarray(np.asarray(L0_ffn_assignments, dtype=np.int32).reshape(1, D))

    xf = x.reshape(2 * S, D)
    nc = _get_program()
    in_maps = []
    for c in range(NCORES):
        in_maps.append({
            "x_s": np.ascontiguousarray(xf[c * T:(c + 1) * T, :]),
            "wq_b": wq, "wk_b": wk, "wv_b": wv, "wo_b": wo, "wf_b": wf,
            "cent_b": cb, "bias_in": bias, "asg_in": asg,
        })
    res = run_bass_kernel_spmd(nc, in_maps, core_ids=list(range(NCORES)))
    out = np.concatenate([res.results[c]["y_s"] for c in range(NCORES)], axis=0)
    return out.reshape(2, S, D).astype(np.float32)


# version 06
# speedup vs baseline: 1.3127x; 1.0368x over previous
import numpy as np
from contextlib import ExitStack

import concourse.bass as bass
import concourse.mybir as mybir
from concourse.bass_utils import run_bass_kernel_spmd
from concourse.alu_op_type import AluOpType as OP
from concourse import library_config

F32 = mybir.dt.float32
F32R = mybir.dt.float32r
I32 = mybir.dt.int32
AF = mybir.ActivationFunctionType
AX = mybir.AxisListType

NCORES = 8
D = 4096
T = 1024
TH = 512
S = 4096
H = 32
DC = 32
EB = 32
NQ = 4
NCL = 64
INV_SQRT_DK = float(1.0 / np.sqrt(np.float32(128.0)))
ENGS = ("sync", "scalar", "vector", "tensor", "gpsimd")
ENG_SEM = {"sync": "dh", "scalar": "sc", "vector": "dv", "tensor": "pe",
           "gpsimd": "gp"}


class Buf:

    __slots__ = ("w", "rs")

    def __init__(self):
        self.w = []
        self.rs = []


def _prune(toks):
    best = {}
    for (s, v) in toks:
        if best.get(s, -1) < v:
            best[s] = v
    return [(s, v) for s, v in best.items()]


class B:
    def __init__(self, nc, es):
        self.nc = nc
        self.es = es
        self.ops = {e: [] for e in ENGS}
        self.sems = {}
        self.fence_waits = []

    def _sem(self, name):
        if name not in self.sems:
            h = self.es.enter_context(self.nc.semaphore(name))
            self.sems[name] = [h, 0]
        return self.sems[name]

    def emit(self, eng, fn, reads=(), writes=(), extra=(), sem=None,
             inc_n=1, accum=False):
        sem = sem or ENG_SEM[eng]
        rec = self._sem(sem)
        waits = list(extra) + list(self.fence_waits)
        for bf in reads:
            waits += bf.w
        for bf in writes:
            if not accum:
                waits += bf.w + bf.rs
        rec[1] += inc_n
        tok = (sem, rec[1])
        for bf in reads:
            bf.rs = _prune(bf.rs + [tok])
        for bf in writes:
            if accum:
                bf.w = _prune(bf.w + [tok])
            else:
                bf.w = [tok]
                bf.rs = []
        self.ops[eng].append((fn, _prune(waits), (sem, inc_n)))
        return tok

    def fence(self):
        self.fence_waits = _prune(
            [(n, rec[1]) for n, rec in self.sems.items() if rec[1] > 0])

    def finalize(self):
        with self.nc.Block() as block:
            for eng in ENGS:
                oplist = self.ops[eng]
                if not oplist:
                    continue

                def mk(oplist):
                    def body(e):
                        seen = {}
                        for fn, waits, inc in oplist:
                            for (sn, v) in waits:
                                if seen.get(sn, -1) >= v:
                                    continue
                                e.wait_ge(self.sems[sn][0], v)
                                seen[sn] = v
                            instr = fn(e)
                            if instr is not None:
                                instr.then_inc(self.sems[inc[0]][0], inc[1])
                    return body

                getattr(block, eng)(mk(oplist))


def build_program(until_phase=99, dbg=None, dbg_out=()):
    nc = bass.Bass("TRN2", target_bir_lowering=False)

    dbg_mirrors = []

    def scratch(name, shape, dt):
        ap = nc.dram_tensor(name, shape, dt).ap()
        if name in dbg_out:
            m = nc.dram_tensor(name + "_dbg", shape, dt, kind="ExternalOutput").ap()
            dbg_mirrors.append((ap, m))
        return ap

    x_s = nc.dram_tensor("x_s", [T, D], F32, kind="ExternalInput").ap()
    w_hbm = {}
    for nm in ("wq_b", "wk_b", "wv_b", "wo_b", "wf_b"):
        w_hbm[nm] = nc.dram_tensor(nm, [EB, NQ, 128, 1024], F32,
                                   kind="ExternalInput").ap()
    cent_in = nc.dram_tensor("cent_b", [128, DC * NCL], F32, kind="ExternalInput").ap()
    bias_in = nc.dram_tensor("bias_in", [1, D], F32, kind="ExternalInput").ap()
    asg_in = nc.dram_tensor("asg_in", [1, D], I32, kind="ExternalInput").ap()
    y_s = nc.dram_tensor("y_s", [T, D], F32, kind="ExternalOutput").ap()

    kh_sh = scratch("kh_sh", [D, T], F32R)
    kl_sh = scratch("kl_sh", [D, T], F32R)
    v_sh = scratch("v_sh", [T, D], F32)
    kh_g = scratch("kh_g", [4 * D, T], F32R)
    kl_g = scratch("kl_g", [4 * D, T], F32R)
    v_g = scratch("v_g", [S, D], F32)
    qh_d = scratch("qh_d", [D, T], F32R)
    ql_d = scratch("ql_d", [D, T], F32R)
    oh_d = scratch("oh_d", [D, T], F32R)
    ol_d = scratch("ol_d", [D, T], F32R)
    x1_d = scratch("x1_d", [T, D], F32)
    lnh_d = scratch("lnh_d", [D, T], F32R)
    lnl_d = scratch("lnl_d", [D, T], F32R)
    kn_in = scratch("kn_in", [1, H], F32)
    kn_out = scratch("kn_out", [1, H], F32)
    cnt_in = scratch("cnt_in", [1, NCL], F32)
    cnt_out = scratch("cnt_out", [1, NCL], F32)

    dB = {nm: Buf() for nm in ("kh_sh", "kl_sh", "v_sh", "kh_g", "kl_g", "v_g",
                               "qh_d", "ql_d", "oh_d", "ol_d", "x1_d",
                               "lnh_d", "lnl_d", "kn_io", "cnt_io", "y")}

    es = ExitStack()
    b = B(nc, es)

    def sb(name, shape, dt=F32, st=None):
        stk = st or es
        t = stk.enter_context(nc.sbuf_tensor(name, shape, dt))
        return t, Buf()

    def pst(name, shape, st=None, dt=F32):
        stk = st or es
        t = stk.enter_context(nc.psum_tensor(name, shape, dt))
        return t, Buf()

    with es:
        ident_r, B_idr = sb("ident_r", [128, 128], F32R)
        ident_f, B_idf = sb("ident_f", [128, 128], F32)
        ones_r, B_onr = sb("ones_r", [128, 1], F32R)
        ones_f, B_onf = sb("ones_f", [128, 1], F32)
        onesc, B_onc = sb("onesc", [1, 128], F32)
        eps_t, B_eps = sb("eps_t", [128, 1], F32)
        qn_sb, B_qn = sb("qn_sb", [1, H], F32)
        kn_sb, B_kn = sb("kn_sb", [1, H], F32)
        qsel_sb, B_qs = sb("qsel_sb", [128, 8], F32)
        cnt_sb, B_cnt = sb("cnt_sb", [1, NCL], F32)

        p0 = es.enter_context(ExitStack())
        i128, B_i1 = sb("i128", [128, 1], I32, p0)
        irow, B_i2 = sb("irow", [1, 128], I32, p0)
        i128f, B_i1f = sb("i128f", [128, 1], F32, p0)
        irowf, B_i2f = sb("irowf", [1, 128], F32, p0)
        irowb, B_i2b = sb("irowb", [128, 128], F32, p0)

        b.emit("gpsimd", lambda e: e.iota(i128[:], pattern=[[0, 1]], base=0,
                                          channel_multiplier=1), writes=[B_i1])
        b.emit("gpsimd", lambda e: e.iota(irow[:], pattern=[[1, 128]], base=0,
                                          channel_multiplier=0), writes=[B_i2])
        b.emit("vector", lambda e: e.tensor_copy(i128f[:], i128[:]),
               reads=[B_i1], writes=[B_i1f])
        b.emit("vector", lambda e: e.tensor_copy(irowf[:], irow[:]),
               reads=[B_i2], writes=[B_i2f])
        ps_bc0, B_pbc0 = pst("ps_bc0", [128, 128], p0)
        b.emit("vector", lambda e: e.memset(onesc[:], 1.0), writes=[B_onc])
        b.emit("tensor", lambda e: e.matmul(ps_bc0[:], onesc[:], irowf[:],
                                            start=True, stop=True),
               reads=[B_i2f, B_onc], writes=[B_pbc0])
        b.emit("vector", lambda e: e.tensor_copy(irowb[:], ps_bc0[:]),
               reads=[B_pbc0], writes=[B_i2b])
        b.emit("vector", lambda e: e.tensor_scalar(ident_f[:], irowb[:], i128f[:], None,
                                                   op0=OP.is_equal),
               reads=[B_i2b, B_i1f], writes=[B_idf])
        b.emit("vector", lambda e: e.tensor_copy(ident_r[:], ident_f[:]),
               reads=[B_idf], writes=[B_idr])
        b.emit("vector", lambda e: e.memset(ones_f[:], 1.0), writes=[B_onf])
        b.emit("scalar", lambda e: e.copy(ones_r[:], ones_f[:]), reads=[B_onf], writes=[B_onr])
        b.emit("vector", lambda e: e.memset(eps_t[:], 1e-5), writes=[B_eps])
        b.emit("vector", lambda e: e.memset(qn_sb[:], 0.0), writes=[B_qn])
        b.emit("vector", lambda e: e.memset(kn_sb[:], 0.0), writes=[B_kn])
        b.emit("vector", lambda e: e.memset(cnt_sb[:], 0.0), writes=[B_cnt])

        p0.close()
        b.fence()

        p1 = es.enter_context(ExitStack())
        lnxh, B_lh = sb("lnxh", [128, DC * TH], F32R, p1)
        lnxl, B_ll = sb("lnxl", [128, DC * TH], F32R, p1)
        xt_t, B_xt = sb("xt_t", [128, D], F32, p1)
        lnh_tm, B_htm = sb("lnh_tm", [128, D // 2], F32R, p1)
        lnl_tm, B_ltm = sb("lnl_tm", [128, D // 2], F32R, p1)
        stats, B_st = sb("stats", [128, 48], F32, p1)
        mv_t, B_mv = sb("mv_t", [128, 2], F32, p1)
        std_t, B_sd = sb("std_t", [128, 1], F32, p1)
        rstd_t, B_rs = sb("rstd_t", [128, 1], F32, p1)
        nmr_t, B_nm = sb("nmr_t", [128, 1], F32, p1)
        whq, wlq, B_wh, B_wl, wfq, B_wf = [], [], [], [], [], []
        for i in range(2):
            t1, b1 = sb(f"wfq{i}", [128, 1024], F32, p1)
            t2, b2 = sb(f"whq{i}", [128, 1024], F32R, p1)
            t3, b3 = sb(f"wlq{i}", [128, 1024], F32R, p1)
            wfq.append(t1); whq.append(t2); wlq.append(t3)
            B_wf.append(b1); B_wh.append(b2); B_wl.append(b3)
        evh, B_evh = sb("evh", [128, TH], F32R, p1)
        evl, B_evl = sb("evl", [128, TH], F32R, p1)
        evn, B_evn = sb("evn", [128, TH], F32, p1)
        evf, B_evf = evh, B_evh
        sqn, B_sqn = evn, B_evn
        nrm2, B_nr2 = sb("nrm2", [1, 1], F32, p1)
        ps_pr, B_pp = zip(*[pst(f"ps_pr{i}", [128, TH], p1) for i in range(2)])
        ps_tr, B_pt = zip(*[pst(f"ps_tr{i}", [128, TH], p1) for i in range(2)])
        ps_nm, B_pn = pst("ps_nm", [1, TH], p1)

        def ln_into(src_rows, half, dsth, dstl, B_dh, B_dl, chunk_cols, extra_w=()):
            for tt in range(4):
                r0 = half * TH + tt * 128
                b.emit("sync", lambda e, r0=r0: e.dma_start(out=xt_t[:], in_=src_rows[r0:r0 + 128, :]),
                       writes=[B_xt], extra=extra_w, sem="dx", inc_n=16)

                def bnall(e):
                    ii = None
                    for i in range(8):
                        ii = e.bn_stats(stats[:, i * 6:(i + 1) * 6], xt_t[:, i * 512:(i + 1) * 512])
                    return ii
                b.emit("vector", bnall, reads=[B_xt], writes=[B_st])
                b.emit("vector", lambda e: e.bn_aggr(mv_t[:], stats[:]), reads=[B_st], writes=[B_mv])
                b.emit("scalar", lambda e: e.activation(std_t[:], mv_t[:, 1:2], AF.Sqrt,
                                                        bias=eps_t[:], scale=1.0),
                       reads=[B_mv, B_eps], writes=[B_sd])
                b.emit("vector", lambda e: e.reciprocal(rstd_t[:], std_t[:]),
                       reads=[B_sd], writes=[B_rs])
                b.emit("vector", lambda e: e.tensor_scalar(nmr_t[:], mv_t[:, 0:1], rstd_t[:], -1.0,
                                                           op0=OP.mult, op1=OP.mult),
                       reads=[B_mv, B_rs], writes=[B_nm])
                for seg in range(2):
                    s0 = seg * (D // 2)
                    b.emit("scalar", lambda e, s0=s0: e.activation(
                        lnh_tm[:], xt_t[:, s0:s0 + D // 2], AF.Identity,
                        bias=nmr_t[:], scale=rstd_t[:]),
                        reads=[B_xt, B_nm, B_rs], writes=[B_htm])
                    b.emit("vector", lambda e, s0=s0: e.scalar_tensor_tensor(
                        lnl_tm[:], xt_t[:, s0:s0 + D // 2], rstd_t[:], lnh_tm[:].bitcast(F32),
                        op0=OP.mult, op1=OP.subtract),
                        reads=[B_xt, B_rs, B_htm], writes=[B_ltm])
                    b.emit("vector", lambda e: e.tensor_scalar(lnl_tm[:], lnl_tm[:], nmr_t[:], None,
                                                               op0=OP.add),
                           reads=[B_nm], writes=[B_ltm])
                    for (src, B_src, dst, B_dst) in ((lnh_tm, B_htm, dsth, B_dh),
                                                     (lnl_tm, B_ltm, dstl, B_dl)):
                        for g in range(4):
                            pb, B_pb = ps_tr[g % 2], B_pt[g % 2]

                            def tr4(e, src=src, g=g, pb=pb):
                                ii = None
                                for k in range(4):
                                    c = g * 4 + k
                                    ii = e.transpose(pb[:, k * 128:(k + 1) * 128].bitcast(F32R),
                                                     src[:, c * 128:(c + 1) * 128], ident_r[:])
                                return ii
                            b.emit("tensor", tr4, reads=[B_src, B_idr], writes=[B_pb])

                            def ev4(e, dst=dst, g=g, tt=tt, pb=pb, cc=chunk_cols, seg=seg):
                                ap_out = dst[:].rearrange("p (c t) -> p c t", t=cc)[
                                    :, seg * 16 + g * 4:seg * 16 + (g + 1) * 4,
                                    tt * 128:tt * 128 + 128]
                                ap_in = pb[:].bitcast(F32R).rearrange("p (k t) -> p k t", t=128)
                                return e.copy(ap_out, ap_in)
                            b.emit("scalar", ev4, reads=[B_pb], writes=[B_dst],
                                   accum=(tt + seg + g > 0))

        def stream_quarter(wap, eb, q, i):
            b.emit("sync", lambda e, eb=eb, q=q, i=i: e.dma_start(out=wfq[i][:], in_=wap[eb, q]),
                   reads=[], writes=[B_wf[i]], sem=f"dw{i}", inc_n=16)
            b.emit("scalar", lambda e, i=i: e.copy(whq[i][:], wfq[i][:]),
                   reads=[B_wf[i]], writes=[B_wh[i]])
            b.emit("vector", lambda e, i=i: e.tensor_tensor(wlq[i][:], wfq[i][:],
                                                            whq[i][:].bitcast(F32), OP.subtract),
                   reads=[B_wf[i], B_wh[i]], writes=[B_wl[i]])

        def mms_unit(q, i, psum, B_psum):
            def mms(e, q=q, i=i, psum=psum):
                ii = None
                for cl in range(8):
                    c = q * 8 + cl
                    first = (q == 0 and cl == 0)
                    last = (q == NQ - 1 and cl == 7)
                    wh_ap = whq[i][:, cl * 128:(cl + 1) * 128]
                    wl_ap = wlq[i][:, cl * 128:(cl + 1) * 128]
                    xh_ap = lnxh[:, c * TH:(c + 1) * TH]
                    xl_ap = lnxl[:, c * TH:(c + 1) * TH]
                    e.matmul(psum[:], wh_ap, xh_ap, start=first, stop=False)
                    e.matmul(psum[:], wh_ap, xl_ap, start=False, stop=False)
                    ii = e.matmul(psum[:], wl_ap, xh_ap, start=False, stop=last)
                return ii
            b.emit("tensor", mms, reads=[B_wh[i], B_wl[i], B_lh, B_ll],
                   writes=[B_psum], accum=(q > 0))

        def norm_rmw(nb_t, B_nb, col):
            b.emit("vector", lambda e, col=col: e.tensor_tensor(
                nb_t[:, col:col + 1], nb_t[:, col:col + 1], nrm2[:], OP.max),
                reads=[B_nr2, B_nb], writes=[B_nb])

        if until_phase >= 1:
            for half in range(2):
                ln_into(x_s, half, lnxh, lnxl, B_lh, B_ll, TH)
                units = [(wn, eb, q) for wn in ("k", "v", "q")
                         for eb in range(EB) for q in range(NQ)]

                def sc1(idx):
                    wn, eb, q = units[idx]
                    stream_quarter(w_hbm[f"w{wn}_b"], eb, q, idx % 2)

                sc1(0)
                for idx, (wname, eb, q) in enumerate(units):
                    if idx + 1 < len(units):
                        sc1(idx + 1)
                    psum, B_psum = ps_pr[eb % 2], B_pp[eb % 2]
                    mms_unit(q, idx % 2, psum, B_psum)
                    if q < NQ - 1:
                        continue
                    if wname in ("k", "q"):
                        dh = kh_sh if wname == "k" else qh_d
                        dl = kl_sh if wname == "k" else ql_d
                        B_dh = dB["kh_sh"] if wname == "k" else dB["qh_d"]
                        B_dl = dB["kl_sh"] if wname == "k" else dB["ql_d"]
                        first_hl = (half == 0 and eb == 0)
                        b.emit("scalar", lambda e, p=psum: e.copy(evh[:], p[:]),
                               reads=[B_psum], writes=[B_evh])
                        b.emit("vector", lambda e, p=psum: e.tensor_tensor(
                            evl[:], p[:], evh[:].bitcast(F32), OP.subtract),
                            reads=[B_psum, B_evh], writes=[B_evl])
                        b.emit("gpsimd", lambda e, d=dh, eb=eb, h=half: e.dma_start(
                            out=d[eb * 128:(eb + 1) * 128, h * TH:(h + 1) * TH], in_=evh[:]),
                            reads=[B_evh], writes=[B_dh], sem="gd", inc_n=16,
                            accum=not first_hl)
                        b.emit("gpsimd", lambda e, d=dl, eb=eb, h=half: e.dma_start(
                            out=d[eb * 128:(eb + 1) * 128, h * TH:(h + 1) * TH], in_=evl[:]),
                            reads=[B_evl], writes=[B_dl], sem="gd", inc_n=16,
                            accum=not first_hl)
                        b.emit("scalar", lambda e: e.activation(sqn[:],
                                                                evh[:].bitcast(F32),
                                                                AF.Square, scale=1.0),
                               reads=[B_evh], writes=[B_sqn])
                        b.emit("tensor", lambda e: e.matmul(ps_nm[:], ones_f[:],
                                                            sqn[:],
                                                            start=True, stop=True),
                               reads=[B_sqn, B_onf], writes=[B_pn])
                        b.emit("vector", lambda e: e.tensor_reduce(nrm2[:], ps_nm[:],
                                                                   axis=AX.X, op=OP.max),
                               reads=[B_pn], writes=[B_nr2])
                        if wname == "k":
                            norm_rmw(kn_sb, B_kn, eb)
                        else:
                            norm_rmw(qn_sb, B_qn, eb)
                    else:
                        b.emit("scalar", lambda e, p=psum: e.copy(evf[:].bitcast(F32), p[:]),
                               reads=[B_psum], writes=[B_evf])
                        pb, B_pb = ps_tr[eb % 2], B_pt[eb % 2]

                        def tr4v(e, pb=pb):
                            ii = None
                            for j in range(4):
                                ii = e.transpose(pb[:, j * 128:(j + 1) * 128].bitcast(F32),
                                                 evf[:, j * 128:(j + 1) * 128].bitcast(F32),
                                                 ident_f[:])
                            return ii
                        b.emit("tensor", tr4v, reads=[B_evf, B_idf], writes=[B_pb])
                        b.emit("vector", lambda e, pb=pb: e.tensor_copy(evn[:], pb[:]),
                               reads=[B_pb], writes=[B_evn])
                        b.emit("gpsimd", lambda e, eb=eb, h=half: e.dma_start(
                            out=v_sh[h * TH:(h + 1) * TH, eb * 128:(eb + 1) * 128]
                            .rearrange("(j p) c -> p j c", p=128),
                            in_=evn[:].rearrange("p (j c) -> p j c", c=128)),
                            reads=[B_evn], writes=[dB["v_sh"]], sem="gd", inc_n=16,
                            accum=not (half == 0 and eb == 0))
                    if half == 1 and wname == "k" and eb == EB - 1:
                        b.emit("gpsimd", lambda e: e.dma_start(out=kn_in, in_=kn_sb[:]),
                               reads=[B_kn], writes=[dB["kn_io"]], sem="gd", inc_n=16)
                        b.emit("gpsimd", lambda e: e.collective_compute(
                            "AllReduce", OP.max, replica_groups=[[0, 1, 2, 3, 4, 5, 6, 7]],
                            ins=[kn_in.opt()], outs=[kn_out.opt()]),
                            reads=[dB["kn_io"]], writes=[dB["kn_io"]], sem="cc")
                        b.emit("gpsimd", lambda e: e.collective_compute(
                            "AllGather", OP.bypass, replica_groups=[[0, 1, 2, 3], [4, 5, 6, 7]],
                            ins=[kh_sh.opt()], outs=[kh_g.opt()]),
                            reads=[dB["kh_sh"]], writes=[dB["kh_g"]], sem="cc")
                        b.emit("gpsimd", lambda e: e.collective_compute(
                            "AllGather", OP.bypass, replica_groups=[[0, 1, 2, 3], [4, 5, 6, 7]],
                            ins=[kl_sh.opt()], outs=[kl_g.opt()]),
                            reads=[dB["kl_sh"]], writes=[dB["kl_g"]], sem="cc")
                    if half == 1 and wname == "v" and eb == EB - 1:
                        b.emit("gpsimd", lambda e: e.collective_compute(
                            "AllGather", OP.bypass, replica_groups=[[0, 1, 2, 3], [4, 5, 6, 7]],
                            ins=[v_sh.opt()], outs=[v_g.opt()]),
                            reads=[dB["v_sh"]], writes=[dB["v_g"]], sem="cc")

        if dbg == 1 or until_phase < 2:
            p1.close()
            b.fence()
            for (sap, map_) in dbg_mirrors:
                b.emit("gpsimd", lambda e, s=sap, m=map_: e.dma_start(out=m, in_=s),
                       sem="gd", inc_n=16)
            b.fence()
            b.emit("gpsimd", lambda e: e.engine_nop())
            b.finalize()
            return nc

        p1.close()
        b.fence()
        p2 = es.enter_context(ExitStack())
        khb, klb, vbb, qhb, qlb = [], [], [], [], []
        B_khb, B_klb, B_vbb, B_qhb, B_qlb = [], [], [], [], []
        for i in range(2):
            t1, b1 = sb(f"khb{i}", [128, S], F32R, p2); khb.append(t1); B_khb.append(b1)
            t2, b2 = sb(f"klb{i}", [128, S], F32R, p2); klb.append(t2); B_klb.append(b2)
            t3, b3 = sb(f"vbb{i}", [128, 32 * 129], F32, p2); vbb.append(t3); B_vbb.append(b3)
            t4, b4 = sb(f"qhb{i}", [128, T], F32R, p2); qhb.append(t4); B_qhb.append(b4)
            t5, b5 = sb(f"qlb{i}", [128, T], F32R, p2); qlb.append(t5); B_qlb.append(b5)
        ptile, B_ptile = zip(*[sb(f"ptile{i}", [128, TH], F32, p2) for i in range(3)])
        otm, B_otm = sb("otm", [128, TH], F32, p2)
        ohh, B_ohh = sb("ohh", [128, TH], F32R, p2)
        ohl, B_ohl = sb("ohl", [128, TH], F32R, p2)
        rz_t, B_rz = sb("rz_t", [128, 4], F32, p2)
        kng, B_kng = sb("kng", [1, H], F32, p2)
        mh_t, B_mh = sb("mh_t", [1, 1], F32, p2)
        mhb, B_mhb = sb("mhb", [128, 1], F32, p2)
        ps_s, B_ps = zip(*[pst(f"ps_s{i}", [128, TH], p2) for i in range(2)])
        ps_av, B_pav = zip(*[pst(f"ps_av{i}", [128, 129], p2) for i in range(4)])
        ps_ot, B_pot = pst("ps_ot", [128, TH], p2)
        ps_bc2, B_pbc2 = pst("ps_bc2", [128, 1], p2)

        for i in range(2):
            b.emit("vector", lambda e, i=i: e.memset(
                vbb[i][:].rearrange("p (c j) -> p c j", j=129)[:, :, 128:129], 1.0),
                writes=[B_vbb[i]])
        b.emit("sync", lambda e: e.dma_start(out=kng[:], in_=kn_out),
               reads=[dB["kn_io"]], writes=[B_kng], sem="dkg", inc_n=16)

        def load_head(h, i):
            for r in range(4):
                b.emit("sync", lambda e, h=h, i=i, r=r: e.dma_start(
                    out=khb[i][:, r * T:(r + 1) * T],
                    in_=kh_g[r * D + h * 128: r * D + (h + 1) * 128, :]),
                    reads=[dB["kh_g"]], writes=[B_khb[i]], sem=f"ka{i}", inc_n=16,
                    accum=(r > 0))
                b.emit("sync", lambda e, h=h, i=i, r=r: e.dma_start(
                    out=klb[i][:, r * T:(r + 1) * T],
                    in_=kl_g[r * D + h * 128: r * D + (h + 1) * 128, :]),
                    reads=[dB["kl_g"]], writes=[B_klb[i]], sem=f"kb{i}", inc_n=16,
                    accum=(r > 0))
            b.emit("sync", lambda e, h=h, i=i: e.dma_start(
                out=qhb[i][:], in_=qh_d[h * 128:(h + 1) * 128, :]),
                reads=[dB["qh_d"]], writes=[B_qhb[i]], sem=f"qa{i}", inc_n=16)
            b.emit("sync", lambda e, h=h, i=i: e.dma_start(
                out=qlb[i][:], in_=ql_d[h * 128:(h + 1) * 128, :]),
                reads=[dB["ql_d"]], writes=[B_qlb[i]], sem=f"qb{i}", inc_n=16)

            def ldv(e, h=h, i=i):
                return e.dma_start(
                    out=vbb[i][:].rearrange("p (c j) -> p c j", j=129)[:, :, 0:128],
                    in_=v_g[:, h * 128:(h + 1) * 128].rearrange("(c p) j -> p c j", p=128))
            b.emit("sync", ldv, reads=[dB["v_g"]], writes=[B_vbb[i]], sem=f"va{i}",
                   inc_n=16)

        def emit_mh(h):
            b.emit("vector", lambda e, h=h: e.tensor_tensor(
                mh_t[:], qn_sb[:, h:h + 1], kng[:, h:h + 1], OP.mult),
                reads=[B_qn, B_kng], writes=[B_mh])
            b.emit("scalar", lambda e: e.activation(mh_t[:], mh_t[:], AF.Sqrt,
                                                    scale=1.0002 / 128.0),
                   reads=[B_mh], writes=[B_mh])
            b.emit("vector", lambda e: e.tensor_scalar(mh_t[:], mh_t[:], -1.0, None, op0=OP.mult),
                   reads=[B_mh], writes=[B_mh])
            b.emit("tensor", lambda e: e.matmul(ps_bc2[:], onesc[:], mh_t[:],
                                                start=True, stop=True),
                   reads=[B_mh, B_onc], writes=[B_pbc2])
            b.emit("vector", lambda e: e.tensor_copy(mhb[:], ps_bc2[:]),
                   reads=[B_pbc2], writes=[B_mhb])

        def emit_smm(i, qi, c):
            sps, B_sps = ps_s[c % 2], B_ps[c % 2]

            def smm(e, i=i, c=c, qi=qi, sps=sps):
                e.matmul(sps[:], khb[i][:, c * 128:(c + 1) * 128],
                         qhb[i][:, qi * TH:(qi + 1) * TH], start=True, stop=False)
                e.matmul(sps[:], khb[i][:, c * 128:(c + 1) * 128],
                         qlb[i][:, qi * TH:(qi + 1) * TH], start=False, stop=False)
                return e.matmul(sps[:], klb[i][:, c * 128:(c + 1) * 128],
                                qhb[i][:, qi * TH:(qi + 1) * TH], start=False, stop=True)
            b.emit("tensor", smm, reads=[B_khb[i], B_klb[i], B_qhb[i], B_qlb[i]],
                   writes=[B_sps])

        def emit_exp(c):
            sps, B_sps = ps_s[c % 2], B_ps[c % 2]
            pt, B_pt2 = ptile[c % 3], B_ptile[c % 3]
            b.emit("scalar", lambda e, sps=sps, pt=pt: e.activation(
                pt[:], sps[:], AF.Exp, bias=mhb[:], scale=INV_SQRT_DK),
                reads=[B_sps, B_mhb], writes=[B_pt2])

        def emit_avm(i, c):
            pt, B_pt2 = ptile[c % 3], B_ptile[c % 3]

            def avm(e, i=i, c=c, pt=pt):
                ii = None
                for j in range(4):
                    ii = e.matmul(ps_av[j][:], pt[:, j * 128:(j + 1) * 128],
                                  vbb[i][:, c * 129:(c + 1) * 129],
                                  start=(c == 0), stop=(c == 31))
                return ii
            b.emit("tensor", avm, reads=[B_pt2, B_vbb[i]],
                   writes=list(B_pav), accum=(c > 0))

        def emit_evac(h, qi):
            for j in range(4):
                b.emit("vector", lambda e, j=j: e.reciprocal(rz_t[:, j:j + 1],
                                                             ps_av[j][:, 128:129]),
                       reads=[B_pav[j]], writes=[B_rz], accum=(j > 0))
                b.emit("vector", lambda e, j=j: e.tensor_scalar(
                    otm[:, j * 128:(j + 1) * 128], ps_av[j][:, 0:128],
                    rz_t[:, j:j + 1], None, op0=OP.mult),
                    reads=[B_pav[j], B_rz], writes=[B_otm], accum=(j > 0))

            def tr4o(e):
                ii = None
                for j in range(4):
                    ii = e.transpose(ps_ot[:, j * 128:(j + 1) * 128].bitcast(F32),
                                     otm[:, j * 128:(j + 1) * 128], ident_f[:])
                return ii
            b.emit("tensor", tr4o, reads=[B_otm, B_idf], writes=[B_pot])
            b.emit("scalar", lambda e: e.copy(ohh[:], ps_ot[:]), reads=[B_pot], writes=[B_ohh])
            b.emit("vector", lambda e: e.tensor_tensor(ohl[:], ps_ot[:],
                                                       ohh[:].bitcast(F32), OP.subtract),
                   reads=[B_pot, B_ohh], writes=[B_ohl])
            b.emit("gpsimd", lambda e, h=h, qi=qi: e.dma_start(
                out=oh_d[h * 128:(h + 1) * 128, qi * TH:(qi + 1) * TH], in_=ohh[:]),
                reads=[B_ohh], writes=[dB["oh_d"]], sem="gd", inc_n=16,
                accum=not (h == 0 and qi == 0))
            b.emit("gpsimd", lambda e, h=h, qi=qi: e.dma_start(
                out=ol_d[h * 128:(h + 1) * 128, qi * TH:(qi + 1) * TH], in_=ohl[:]),
                reads=[B_ohl], writes=[dB["ol_d"]], sem="gd", inc_n=16,
                accum=not (h == 0 and qi == 0))

        load_head(0, 0)
        pending_evac = None
        for h in range(H):
            i = h % 2
            if h + 1 < H:
                load_head(h + 1, (h + 1) % 2)
            emit_mh(h)
            for qi in range(2):
                for c in range(32):
                    emit_smm(i, qi, c)
                    if c == 0:
                        emit_exp(c)
                        if pending_evac is not None:
                            emit_evac(*pending_evac)
                            pending_evac = None
                    else:
                        emit_avm(i, c - 1)
                        emit_exp(c)
                emit_avm(i, 31)
                pending_evac = (h, qi)
        emit_evac(*pending_evac)

        if dbg == 2 or until_phase < 3:
            p2.close()
            b.fence()
            for (sap, map_) in dbg_mirrors:
                b.emit("gpsimd", lambda e, s=sap, m=map_: e.dma_start(out=m, in_=s),
                       sem="gd", inc_n=16)
            b.fence()
            b.emit("gpsimd", lambda e: e.engine_nop())
            b.finalize()
            return nc

        p2.close()
        b.fence()
        p3 = es.enter_context(ExitStack())
        o_hh, _Boh0 = sb("o_hh", [128, DC * TH], F32R, p3)
        o_ll, _Bol0 = sb("o_ll", [128, DC * TH], F32R, p3)
        B_oh = [Buf() for _ in range(NQ)]
        B_ol = [Buf() for _ in range(NQ)]
        wfq3, whq3, wlq3, B_wf3, B_wh3, B_wl3 = [], [], [], [], [], []
        for i in range(2):
            t1, b1 = sb(f"wfq3{i}", [128, 1024], F32, p3)
            t2, b2 = sb(f"whq3{i}", [128, 1024], F32R, p3)
            t3, b3 = sb(f"wlq3{i}", [128, 1024], F32R, p3)
            wfq3.append(t1); whq3.append(t2); wlq3.append(t3)
            B_wf3.append(b1); B_wh3.append(b2); B_wl3.append(b3)
        of_t, B_oft = sb("of_t", [128, TH], F32, p3)
        x_t4, B_xt4 = zip(*[sb(f"x_t4{i}", [128, TH], F32, p3) for i in range(2)])
        x1_t, B_x1t = sb("x1_t", [128, TH], F32, p3)
        ps_pr3, B_pp3 = zip(*[pst(f"ps_pr3{i}", [128, TH], p3) for i in range(2)])
        ps_tr3, B_pt3 = zip(*[pst(f"ps_tr3{i}", [128, TH], p3) for i in range(2)])

        def load_act_pieces(dsth, dstl, B_h, B_l, srch, srcl, B_srch, B_srcl,
                            half, sem_pfx):
            for q in range(NQ):
                for (dst, src, B_dst, B_src) in ((dsth, srch, B_h, B_srch),
                                                 (dstl, srcl, B_l, B_srcl)):
                    b.emit("sync", lambda e, q=q, h=half, dst=dst, src=src: e.dma_start(
                        out=dst[:, q * 8 * TH:(q + 1) * 8 * TH]
                        .rearrange("p (c t) -> p c t", t=TH),
                        in_=src[q * 8 * 128:(q + 1) * 8 * 128, h * TH:(h + 1) * TH]
                        .rearrange("(c p) t -> p c t", p=128)),
                        reads=[B_src], writes=[B_dst[q]], sem=f"{sem_pfx}{q}", inc_n=16)

        units3 = [(eb, q) for eb in range(EB) for q in range(NQ)]

        def stream_q3(wap, idx, B_wfL, wf, wh, wl, B_whL, B_wlL, pfx):
            eb, q = units3[idx]
            i = idx % 2
            b.emit("sync", lambda e, eb=eb, q=q, i=i: e.dma_start(out=wf[i][:], in_=wap[eb, q]),
                   writes=[B_wfL[i]], sem=f"{pfx}{i}", inc_n=16)
            b.emit("scalar", lambda e, i=i: e.copy(wh[i][:], wf[i][:]),
                   reads=[B_wfL[i]], writes=[B_whL[i]])
            b.emit("vector", lambda e, i=i: e.tensor_tensor(wl[i][:], wf[i][:],
                                                            wh[i][:].bitcast(F32), OP.subtract),
                   reads=[B_wfL[i], B_whL[i]], writes=[B_wlL[i]])

        def ldx4(eb, half):
            slot = eb % 2
            b.emit("sync", lambda e, eb=eb, h=half, slot=slot: e.dma_start(
                out=x_t4[slot][:].rearrange("p (s j) -> p s j", j=128),
                in_=x_s[h * TH:(h + 1) * TH, eb * 128:(eb + 1) * 128]
                .rearrange("(s p) j -> p s j", p=128)),
                writes=[B_xt4[slot]], sem=f"dx4{slot}", inc_n=16)

        for half in range(2):
            load_act_pieces(o_hh, o_ll, B_oh, B_ol, oh_d, ol_d,
                            dB["oh_d"], dB["ol_d"], half, "do")
            ldx4(0, half)
            ldx4(1, half)
            stream_q3(w_hbm["wo_b"], 0, B_wf3, wfq3, whq3, wlq3, B_wh3, B_wl3, "dw3")
            for idx, (eb, q) in enumerate(units3):
                if idx + 1 < len(units3):
                    stream_q3(w_hbm["wo_b"], idx + 1, B_wf3, wfq3, whq3, wlq3,
                              B_wh3, B_wl3, "dw3")
                i = idx % 2
                psum, B_psum = ps_pr3[eb % 2], B_pp3[eb % 2]

                def mms(e, q=q, i=i, psum=psum):
                    ii = None
                    for cl in range(8):
                        c = q * 8 + cl
                        first = (q == 0 and cl == 0)
                        last = (q == NQ - 1 and cl == 7)
                        wh_ap = whq3[i][:, cl * 128:(cl + 1) * 128]
                        wl_ap = wlq3[i][:, cl * 128:(cl + 1) * 128]
                        xh_ap = o_hh[:, c * TH:(c + 1) * TH]
                        xl_ap = o_ll[:, c * TH:(c + 1) * TH]
                        e.matmul(psum[:], wh_ap, xh_ap, start=first, stop=False)
                        e.matmul(psum[:], wh_ap, xl_ap, start=False, stop=False)
                        ii = e.matmul(psum[:], wl_ap, xh_ap, start=False, stop=last)
                    return ii
                b.emit("tensor", mms, reads=[B_wh3[i], B_wl3[i], B_oh[q], B_ol[q]],
                       writes=[B_psum], accum=(q > 0))
                if q < NQ - 1:
                    continue
                b.emit("scalar", lambda e, p=psum: e.copy(of_t[:], p[:]),
                       reads=[B_psum], writes=[B_oft])
                pb, B_pb = ps_tr3[eb % 2], B_pt3[eb % 2]

                def tr4p(e, pb=pb):
                    ii = None
                    for j in range(4):
                        ii = e.transpose(pb[:, j * 128:(j + 1) * 128].bitcast(F32),
                                         of_t[:, j * 128:(j + 1) * 128], ident_f[:])
                    return ii
                b.emit("tensor", tr4p, reads=[B_oft, B_idf], writes=[B_pb])
                b.emit("vector", lambda e, pb=pb, eb=eb: e.tensor_tensor(
                    x1_t[:], pb[:], x_t4[eb % 2][:], OP.add),
                    reads=[B_pb, B_xt4[eb % 2]], writes=[B_x1t])
                if eb + 2 < EB:
                    ldx4(eb + 2, half)
                b.emit("gpsimd", lambda e, eb=eb, h=half: e.dma_start(
                    out=x1_d[h * TH:(h + 1) * TH, eb * 128:(eb + 1) * 128]
                    .rearrange("(s p) j -> p s j", p=128),
                    in_=x1_t[:].rearrange("p (s j) -> p s j", j=128)),
                    reads=[B_x1t], writes=[dB["x1_d"]], sem="gd", inc_n=16,
                    accum=not (half == 0 and eb == 0))

        p3.close()
        b.fence()
        p3 = es.enter_context(ExitStack())
        ps_tr3, B_pt3 = zip(*[pst(f"ps_trb{i}", [128, TH], p3) for i in range(2)])
        cent_f, B_ctf = sb("cent_f", [128, DC * NCL], F32, p3)
        cent_h, B_cth = sb("cent_h", [128, DC * NCL], F32R, p3)
        cent_l, B_ctl = sb("cent_l", [128, DC * NCL], F32R, p3)
        b.emit("sync", lambda e: e.dma_start(out=cent_f[:], in_=cent_in),
               writes=[B_ctf], sem="dct", inc_n=16)
        b.emit("scalar", lambda e: e.copy(cent_h[:], cent_f[:]), reads=[B_ctf], writes=[B_cth])
        b.emit("vector", lambda e: e.tensor_tensor(cent_l[:], cent_f[:],
                                                   cent_h[:].bitcast(F32), OP.subtract),
               reads=[B_ctf, B_cth], writes=[B_ctl])
        ln2h_t, B_l2h = sb("ln2h_t", [128, DC * 128], F32R, p3)
        ln2l_t, B_l2l = sb("ln2l_t", [128, DC * 128], F32R, p3)
        xt2, B_xt2 = sb("xt2", [128, D], F32, p3)
        l2h_tm, B_h2 = sb("l2h_tm", [128, D], F32R, p3)
        l2l_tm, B_l2 = sb("l2l_tm", [128, D], F32R, p3)
        st2, B_st2 = sb("st2", [128, 48], F32, p3)
        mv2, B_mv2 = sb("mv2", [128, 2], F32, p3)
        sd2, B_sd2 = sb("sd2", [128, 1], F32, p3)
        rs2, B_rs2 = sb("rs2", [128, 1], F32, p3)
        nm2, B_nm2 = sb("nm2", [128, 1], F32, p3)
        dex, B_dex = sb("dex", [128, NCL], F32, p3)
        dmx, B_dmx = sb("dmx", [128, 1], F32, p3)
        dnb, B_dnb = sb("dnb", [128, 1], F32, p3)
        dzs, B_dzs = sb("dzs", [128, 1], F32, p3)
        dth, B_dth = sb("dth", [128, 1], F32, p3)
        hits, B_hit = sb("hits", [128, NCL], F32R, p3)
        ps_dt, B_pdt = pst("ps_dt", [128, NCL], p3)
        ps_ct, B_pct = pst("ps_ct", [1, NCL], p3)

        for ti in range(8):
            b.emit("sync", lambda e, ti=ti: e.dma_start(out=xt2[:],
                                                        in_=x1_d[ti * 128:(ti + 1) * 128, :]),
                   reads=[dB["x1_d"]], writes=[B_xt2], sem="dx", inc_n=16)

            def bn2(e):
                ii = None
                for i in range(8):
                    ii = e.bn_stats(st2[:, i * 6:(i + 1) * 6], xt2[:, i * 512:(i + 1) * 512])
                return ii
            b.emit("vector", bn2, reads=[B_xt2], writes=[B_st2])
            b.emit("vector", lambda e: e.bn_aggr(mv2[:], st2[:]), reads=[B_st2], writes=[B_mv2])
            b.emit("scalar", lambda e: e.activation(sd2[:], mv2[:, 1:2], AF.Sqrt,
                                                    bias=eps_t[:], scale=1.0),
                   reads=[B_mv2, B_eps], writes=[B_sd2])
            b.emit("vector", lambda e: e.reciprocal(rs2[:], sd2[:]), reads=[B_sd2], writes=[B_rs2])
            b.emit("vector", lambda e: e.tensor_scalar(nm2[:], mv2[:, 0:1], rs2[:], -1.0,
                                                       op0=OP.mult, op1=OP.mult),
                   reads=[B_mv2, B_rs2], writes=[B_nm2])
            b.emit("scalar", lambda e: e.activation(l2h_tm[:], xt2[:], AF.Identity,
                                                    bias=nm2[:], scale=rs2[:]),
                   reads=[B_xt2, B_nm2, B_rs2], writes=[B_h2])
            b.emit("vector", lambda e: e.scalar_tensor_tensor(
                l2l_tm[:], xt2[:], rs2[:], l2h_tm[:].bitcast(F32), op0=OP.mult, op1=OP.subtract),
                reads=[B_xt2, B_rs2, B_h2], writes=[B_l2])
            b.emit("vector", lambda e: e.tensor_scalar(l2l_tm[:], l2l_tm[:], nm2[:], None,
                                                       op0=OP.add),
                   reads=[B_nm2], writes=[B_l2])
            for (src, B_src, dst, B_dst) in ((l2h_tm, B_h2, ln2h_t, B_l2h),
                                             (l2l_tm, B_l2, ln2l_t, B_l2l)):
                for g in range(8):
                    pb, B_pb2 = ps_tr3[g % 2], B_pt3[g % 2]

                    def tr4l(e, src=src, g=g, pb=pb):
                        ii = None
                        for k in range(4):
                            c = g * 4 + k
                            ii = e.transpose(pb[:, k * 128:(k + 1) * 128].bitcast(F32R),
                                             src[:, c * 128:(c + 1) * 128], ident_r[:])
                        return ii
                    b.emit("tensor", tr4l, reads=[B_src, B_idr], writes=[B_pb2])
                    b.emit("scalar", lambda e, dst=dst, g=g, pb=pb: e.copy(
                        dst[:, g * 512:(g + 1) * 512], pb[:].bitcast(F32R)),
                        reads=[B_pb2], writes=[B_dst], accum=(g > 0))
            b.emit("gpsimd", lambda e, ti=ti: e.dma_start(
                out=lnh_d[:, ti * 128:(ti + 1) * 128].rearrange("(c p) t -> p c t", p=128),
                in_=ln2h_t[:].rearrange("p (c t) -> p c t", t=128)),
                reads=[B_l2h], writes=[dB["lnh_d"]], sem="gd", inc_n=16, accum=(ti > 0))
            b.emit("gpsimd", lambda e, ti=ti: e.dma_start(
                out=lnl_d[:, ti * 128:(ti + 1) * 128].rearrange("(c p) t -> p c t", p=128),
                in_=ln2l_t[:].rearrange("p (c t) -> p c t", t=128)),
                reads=[B_l2l], writes=[dB["lnl_d"]], sem="gd", inc_n=16, accum=(ti > 0))

            def dotsmm(e):
                ii = None
                for c in range(DC):
                    first = (c == 0)
                    last = (c == DC - 1)
                    e.matmul(ps_dt[:], ln2h_t[:, c * 128:(c + 1) * 128],
                             cent_h[:, c * NCL:(c + 1) * NCL], start=first, stop=False)
                    e.matmul(ps_dt[:], ln2h_t[:, c * 128:(c + 1) * 128],
                             cent_l[:, c * NCL:(c + 1) * NCL], start=False, stop=False)
                    ii = e.matmul(ps_dt[:], ln2l_t[:, c * 128:(c + 1) * 128],
                                  cent_h[:, c * NCL:(c + 1) * NCL], start=False, stop=last)
                return ii
            b.emit("tensor", dotsmm, reads=[B_l2h, B_l2l, B_cth, B_ctl], writes=[B_pdt])
            b.emit("vector", lambda e: e.tensor_reduce(dmx[:], ps_dt[:], axis=AX.X, op=OP.max),
                   reads=[B_pdt], writes=[B_dmx])
            b.emit("vector", lambda e: e.tensor_scalar(dnb[:], dmx[:], -10.0, None, op0=OP.mult),
                   reads=[B_dmx], writes=[B_dnb])
            b.emit("scalar", lambda e: e.activation(dex[:], ps_dt[:], AF.Exp,
                                                    bias=dnb[:], scale=10.0, accum_out=dzs[:]),
                   reads=[B_pdt, B_dnb], writes=[B_dex, B_dzs])
            b.emit("vector", lambda e: e.tensor_scalar(dth[:], dzs[:], 0.01, None, op0=OP.mult),
                   reads=[B_dzs], writes=[B_dth])
            b.emit("vector", lambda e: e.tensor_scalar(hits[:], dex[:], dth[:], None, op0=OP.is_gt),
                   reads=[B_dex, B_dth], writes=[B_hit])
            b.emit("vector", lambda e, ti=ti: e.tensor_reduce(
                qsel_sb[:, ti:ti + 1], hits[:].bitcast(F32), axis=AX.X, op=OP.max),
                reads=[B_hit], writes=[B_qs], accum=(ti > 0))
            b.emit("tensor", lambda e: e.matmul(ps_ct[:], ones_r[:], hits[:],
                                                start=True, stop=True),
                   reads=[B_hit, B_onr], writes=[B_pct])
            b.emit("vector", lambda e: e.tensor_tensor(cnt_sb[:], cnt_sb[:], ps_ct[:], OP.add),
                   reads=[B_pct, B_cnt], writes=[B_cnt])

        b.emit("gpsimd", lambda e: e.dma_start(out=cnt_in, in_=cnt_sb[:]),
               reads=[B_cnt], writes=[dB["cnt_io"]], sem="gd", inc_n=16)
        b.emit("gpsimd", lambda e: e.collective_compute(
            "AllReduce", OP.add, replica_groups=[[0, 1, 2, 3, 4, 5, 6, 7]],
            ins=[cnt_in.opt()], outs=[cnt_out.opt()]),
            reads=[dB["cnt_io"]], writes=[dB["cnt_io"]], sem="cc")

        if dbg == 3 or until_phase < 4:
            p3.close()
            b.fence()
            for (sap, map_) in dbg_mirrors:
                b.emit("gpsimd", lambda e, s=sap, m=map_: e.dma_start(out=m, in_=s),
                       sem="gd", inc_n=16)
            b.fence()
            b.emit("gpsimd", lambda e: e.engine_nop())
            b.finalize()
            return nc

        p3.close()
        b.fence()
        p4 = es.enter_context(ExitStack())
        bias_b, B_bb = sb("bias_b", [128, D], F32, p4)
        rsel_b, B_rb = sb("rsel_b", [128, D], F32, p4)
        p4a = es.enter_context(ExitStack())
        bias_r, B_br = sb("bias_r", [1, D], F32, p4a)
        asg_s, B_as = sb("asg_s", [1, D], I32, p4a)
        asg_f, B_af = sb("asg_f", [1, D], F32, p4a)
        asg_b, B_ab = sb("asg_b", [64, D], F32, p4a)
        i64, B_i64 = sb("i64", [64, 1], I32, p4a)
        i64f, B_i64f = sb("i64f", [64, 1], F32, p4a)
        onehot, B_oh1 = sb("onehot", [64, D], F32R, p4a)
        cnt_t, B_ct2 = sb("cnt_t", [64, 1], F32, p4a)
        csel, B_cs = sb("csel", [64, 1], F32R, p4a)
        rsel_r, B_rr = sb("rsel_r", [1, D], F32, p4a)
        ps_rs, B_prs = pst("ps_rs", [1, TH], p4a)
        ps_bc4, B_pbc4 = pst("ps_bc4", [128, TH], p4a)

        def bcast_mm(dst, B_dst, row_t, B_row, nparts, ncols):
            for q0 in range(0, ncols, TH):
                b.emit("tensor", lambda e, q0=q0: e.matmul(
                    ps_bc4[0:nparts, 0:TH], onesc[:, 0:nparts], row_t[:, q0:q0 + TH],
                    start=True, stop=True),
                    reads=[B_row, B_onc], writes=[B_pbc4])
                b.emit("vector", lambda e, q0=q0: e.tensor_copy(
                    dst[0:nparts, q0:q0 + TH], ps_bc4[0:nparts, 0:TH]),
                    reads=[B_pbc4], writes=[B_dst], accum=(q0 > 0))

        b.emit("sync", lambda e: e.dma_start(out=bias_r[:], in_=bias_in),
               writes=[B_br], sem="db1", inc_n=16)
        b.emit("sync", lambda e: e.dma_start(out=asg_s[:], in_=asg_in),
               writes=[B_as], sem="db2", inc_n=16)
        bcast_mm(bias_b, B_bb, bias_r, B_br, 128, D)
        b.emit("vector", lambda e: e.tensor_copy(asg_f[:], asg_s[:]), reads=[B_as], writes=[B_af])
        bcast_mm(asg_b, B_ab, asg_f, B_af, 64, D)
        b.emit("gpsimd", lambda e: e.iota(i64[:], pattern=[[0, 1]], base=0, channel_multiplier=1),
               writes=[B_i64])
        b.emit("vector", lambda e: e.tensor_copy(i64f[:], i64[:]), reads=[B_i64], writes=[B_i64f])
        b.emit("vector", lambda e: e.tensor_scalar(onehot[:], asg_b[:], i64f[:], None,
                                                   op0=OP.is_equal),
               reads=[B_ab, B_i64f], writes=[B_oh1])
        b.emit("sync", lambda e: e.dma_start(out=cnt_t[:], in_=cnt_out.rearrange("a b -> b a")),
               reads=[dB["cnt_io"]], writes=[B_ct2], sem="db3", inc_n=16)
        b.emit("vector", lambda e: e.tensor_scalar(csel[:], cnt_t[:], 0.5, None, op0=OP.is_gt),
               reads=[B_ct2], writes=[B_cs])
        for q in range(8):
            b.emit("tensor", lambda e, q=q: e.matmul(
                ps_rs[:], csel[:], onehot[:, q * TH:(q + 1) * TH], start=True, stop=True),
                reads=[B_cs, B_oh1], writes=[B_prs])
            b.emit("vector", lambda e, q=q: e.tensor_copy(rsel_r[:, q * TH:(q + 1) * TH], ps_rs[:]),
                   reads=[B_prs], writes=[B_rr], accum=(q > 0))
        bcast_mm(rsel_b, B_rb, rsel_r, B_rr, 128, D)

        p4a.close()
        b.fence()
        l1h, _B1h0 = sb("l1h", [128, DC * TH], F32R, p4)
        l1l, _B1l0 = sb("l1l", [128, DC * TH], F32R, p4)
        B_1h = [Buf() for _ in range(NQ)]
        B_1l = [Buf() for _ in range(NQ)]
        wfq4, whq4, wlq4, B_wf4, B_wh4, B_wl4 = [], [], [], [], [], []
        for i in range(2):
            t1, b1 = sb(f"wfq4{i}", [128, 1024], F32, p4)
            t2, b2 = sb(f"whq4{i}", [128, 1024], F32R, p4)
            t3, b3 = sb(f"wlq4{i}", [128, 1024], F32R, p4)
            wfq4.append(t1); whq4.append(t2); wlq4.append(t3)
            B_wf4.append(b1); B_wh4.append(b2); B_wl4.append(b3)
        ff_t, B_fft = sb("ff_t", [128, TH], F32, p4)
        x1r_t, B_x1r = sb("x1r_t", [128, TH], F32, p4)
        out_t, B_out4 = sb("out_t", [128, TH], F32, p4)
        B_outs = [Buf() for _ in range(4)]
        ps_pr4, B_pp4 = zip(*[pst(f"ps_pr4{i}", [128, TH], p4) for i in range(2)])
        ps_tr4, B_pt4 = zip(*[pst(f"ps_tr4{i}", [128, TH], p4) for i in range(2)])

        for half in range(2):
            load_act_pieces(l1h, l1l, B_1h, B_1l, lnh_d, lnl_d,
                            dB["lnh_d"], dB["lnl_d"], half, "dl")
            for eb in range(EB):
                psum, B_psum = ps_pr4[eb % 2], B_pp4[eb % 2]
                for q in range(NQ):
                    i = q % 2
                    b.emit("sync", lambda e, eb=eb, q=q, i=i: e.dma_start(
                        out=wfq4[i][:], in_=w_hbm["wf_b"][eb, q]),
                        writes=[B_wf4[i]], sem=f"dw{i}", inc_n=16)
                    b.emit("scalar", lambda e, i=i: e.copy(whq4[i][:], wfq4[i][:]),
                           reads=[B_wf4[i]], writes=[B_wh4[i]])
                    b.emit("vector", lambda e, i=i: e.tensor_tensor(
                        wlq4[i][:], wfq4[i][:], whq4[i][:].bitcast(F32), OP.subtract),
                        reads=[B_wf4[i], B_wh4[i]], writes=[B_wl4[i]])

                    def mms4(e, q=q, i=i, psum=psum):
                        ii = None
                        for cl in range(8):
                            c = q * 8 + cl
                            first = (q == 0 and cl == 0)
                            last = (q == NQ - 1 and cl == 7)
                            wh_ap = whq4[i][:, cl * 128:(cl + 1) * 128]
                            wl_ap = wlq4[i][:, cl * 128:(cl + 1) * 128]
                            xh_ap = l1h[:, c * TH:(c + 1) * TH]
                            xl_ap = l1l[:, c * TH:(c + 1) * TH]
                            e.matmul(psum[:], wh_ap, xh_ap, start=first, stop=False)
                            e.matmul(psum[:], wh_ap, xl_ap, start=False, stop=False)
                            ii = e.matmul(psum[:], wl_ap, xh_ap, start=False, stop=last)
                        return ii
                    b.emit("tensor", mms4, reads=[B_wh4[i], B_wl4[i], B_1h[q], B_1l[q]],
                           writes=[B_psum], accum=(q > 0))
                b.emit("scalar", lambda e, p=psum: e.copy(ff_t[:], p[:]),
                       reads=[B_psum], writes=[B_fft])
                b.emit("sync", lambda e, eb=eb, h=half: e.dma_start(
                    out=x1r_t[:].rearrange("p (s j) -> p s j", j=128),
                    in_=x1_d[h * TH:(h + 1) * TH, eb * 128:(eb + 1) * 128]
                    .rearrange("(s p) j -> p s j", p=128)),
                    reads=[dB["x1_d"]], writes=[B_x1r], sem="dx4", inc_n=16)
                pb, B_pb4 = ps_tr4[eb % 2], B_pt4[eb % 2]

                def tr4f(e, pb=pb):
                    ii = None
                    for j in range(4):
                        ii = e.transpose(pb[:, j * 128:(j + 1) * 128].bitcast(F32),
                                         ff_t[:, j * 128:(j + 1) * 128], ident_f[:])
                    return ii
                b.emit("tensor", tr4f, reads=[B_fft, B_idf], writes=[B_pb4])
                for j in range(4):
                    b.emit("vector", lambda e, j=j, pb=pb, eb=eb: e.tensor_tensor(
                        out_t[:, j * 128:(j + 1) * 128], pb[:, j * 128:(j + 1) * 128],
                        bias_b[:, eb * 128:(eb + 1) * 128], OP.add),
                        reads=[B_pb4, B_bb], writes=[B_outs[j]])
                    b.emit("vector", lambda e, j=j, eb=eb: e.tensor_tensor(
                        out_t[:, j * 128:(j + 1) * 128], out_t[:, j * 128:(j + 1) * 128],
                        rsel_b[:, eb * 128:(eb + 1) * 128], OP.mult),
                        reads=[B_rb, B_outs[j]], writes=[B_outs[j]])
                    b.emit("vector", lambda e, j=j, h=half: e.scalar_tensor_tensor(
                        out_t[:, j * 128:(j + 1) * 128], out_t[:, j * 128:(j + 1) * 128],
                        qsel_sb[:, h * 4 + j:h * 4 + j + 1], x1r_t[:, j * 128:(j + 1) * 128],
                        op0=OP.mult, op1=OP.add),
                        reads=[B_qs, B_x1r, B_outs[j]], writes=[B_outs[j]])
                b.emit("gpsimd", lambda e, eb=eb, h=half: e.dma_start(
                    out=y_s[h * TH:(h + 1) * TH, eb * 128:(eb + 1) * 128]
                    .rearrange("(s p) j -> p s j", p=128),
                    in_=out_t[:].rearrange("p (s j) -> p s j", j=128)),
                    reads=B_outs, writes=[dB["y"]], sem="gd", inc_n=16,
                    accum=not (half == 0 and eb == 0))

        p4.close()
        b.fence()
        for (sap, map_) in dbg_mirrors:
            b.emit("gpsimd", lambda e, s=sap, m=map_: e.dma_start(out=m, in_=s),
                   sem="gd", inc_n=16)
        if dbg_mirrors:
            b.fence()
        b.emit("gpsimd", lambda e: e.engine_nop())
        b.finalize()
    return nc


_NC_CACHE = {}


def _get_program():
    if "nc" not in _NC_CACHE:
        _NC_CACHE["nc"] = build_program()
    return _NC_CACHE["nc"]


def _block_weights(w):
    wt = np.ascontiguousarray(w.T)
    v = wt.reshape(NQ, 8, 128, EB, 128)
    v = v.transpose(3, 0, 2, 1, 4)
    return np.ascontiguousarray(v.reshape(EB, NQ, 128, 1024))


def kernel(L0_attn_x, L0_attn_w_q, L0_attn_w_k, L0_attn_w_v, L0_attn_w_o,
           L0_ffn_weight, L0_ffn_bias, L0_ffn_centroids, L0_ffn_assignments):
    x = np.ascontiguousarray(np.asarray(L0_attn_x, dtype=np.float32))
    wq = _block_weights(np.asarray(L0_attn_w_q, dtype=np.float32))
    wk = _block_weights(np.asarray(L0_attn_w_k, dtype=np.float32))
    wv = _block_weights(np.asarray(L0_attn_w_v, dtype=np.float32))
    wo = _block_weights(np.asarray(L0_attn_w_o, dtype=np.float32))
    wf = _block_weights(np.asarray(L0_ffn_weight, dtype=np.float32))
    ct = np.ascontiguousarray(np.asarray(L0_ffn_centroids, dtype=np.float32).T)
    cb = np.ascontiguousarray(ct.reshape(DC, 128, NCL).transpose(1, 0, 2)
                              .reshape(128, DC * NCL))
    bias = np.ascontiguousarray(np.asarray(L0_ffn_bias, dtype=np.float32).reshape(1, D))
    asg = np.ascontiguousarray(np.asarray(L0_ffn_assignments, dtype=np.int32).reshape(1, D))

    xf = x.reshape(2 * S, D)
    nc = _get_program()
    in_maps = []
    for c in range(NCORES):
        in_maps.append({
            "x_s": np.ascontiguousarray(xf[c * T:(c + 1) * T, :]),
            "wq_b": wq, "wk_b": wk, "wv_b": wv, "wo_b": wo, "wf_b": wf,
            "cent_b": cb, "bias_in": bias, "asg_in": asg,
        })
    res = run_bass_kernel_spmd(nc, in_maps, core_ids=list(range(NCORES)))
    out = np.concatenate([res.results[c]["y_s"] for c in range(NCORES)], axis=0)
    return out.reshape(2, S, D).astype(np.float32)
